# version 04
# baseline (speedup 1.0000x reference)
import numpy as np
import ml_dtypes
import concourse.bass as bass
import concourse.mybir as mybir
from concourse.bass_utils import run_bass_kernel_spmd

F32 = mybir.dt.float32
BF16 = mybir.dt.bfloat16
U8 = mybir.dt.uint8
AF = mybir.ActivationFunctionType
ALU = mybir.AluOpType
AX = mybir.AxisListType

NCORES = 8
D = 2048
S = 8192
TOK = S // NCORES
NDC = D // 128
NTT = TOK // 512
HPC = 2
NQC = S // 512
NKC = S // 128
I_FFN = 8192
NIC = I_FFN // 128
ISQ = float(1.0 / np.sqrt(128.0))
ENGINES = ("sync", "tensor", "scalar", "vector", "gpsimd")

_LAST_PROG = None
_LAST_SEMS = None
_DETECT_RACES = True
_PHASE_LIMIT = 5
_P2_MODE = "all"
_P1_MIN = False
_P6N = 7
_DEBUG_TAPS = []


class Prog:

    def __init__(self):
        self.ops = {e: [] for e in ENGINES}
        self.cnt = {}

    def on(self, eng, fn):
        self.ops[eng].append(fn)

    def inc(self, sem, n=1):
        self.cnt[sem] = self.cnt.get(sem, 0) + n
        return self.cnt[sem]

    def val(self, sem):
        return self.cnt.get(sem, 0)


def build_nc():
    PL = _PHASE_LIMIT
    P2M = _P2_MODE
    P1MIN = _P1_MIN
    P6N = _P6N
    nc = bass.Bass("TRN2", target_bir_lowering=False,
                   detect_race_conditions=_DETECT_RACES)

    xt_d = nc.dram_tensor("xt", [NTT * 128, NDC * 512], F32, kind="ExternalInput").ap()
    wq_d = nc.dram_tensor("wq", [128, NDC * 256], F32, kind="ExternalInput").ap()
    wk_d = nc.dram_tensor("wk", [128, NDC * 256], F32, kind="ExternalInput").ap()
    wv_d = nc.dram_tensor("wv", [128, NDC * 256], F32, kind="ExternalInput").ap()
    wo_d = nc.dram_tensor("wo", [NDC * 128, NDC * 128], F32, kind="ExternalInput").ap()
    bqk_d = nc.dram_tensor("bqk", [128, 4], F32, kind="ExternalInput").ap()
    bv_d = nc.dram_tensor("bv", [1, 256], F32, kind="ExternalInput").ap()
    bo_d = nc.dram_tensor("bo", [128, NDC], F32, kind="ExternalInput").ap()
    lng_d = nc.dram_tensor("lng", [128, NDC], F32, kind="ExternalInput").ap()
    lnb_d = nc.dram_tensor("lnb", [128, NDC], F32, kind="ExternalInput").ap()
    wgt_d = nc.dram_tensor("wgt", [NIC * 128, NDC * 128], U8, kind="ExternalInput").ap()
    wut_d = nc.dram_tensor("wut", [NIC * 128, NDC * 128], U8, kind="ExternalInput").ap()
    wdt_d = nc.dram_tensor("wdt", [NDC * 4 * 128, 16 * 128], U8, kind="ExternalInput").ap()
    gsc_d = nc.dram_tensor("gsc", [128, NIC], F32, kind="ExternalInput").ap()
    usc_d = nc.dram_tensor("usc", [128, NIC], F32, kind="ExternalInput").ap()
    dsc_d = nc.dram_tensor("dsc", [128, NDC], F32, kind="ExternalInput").ap()
    cmask_d = nc.dram_tensor("cmaskb", [128, 4 * 512], BF16, kind="ExternalInput").ap()
    onescol_d = nc.dram_tensor("onescol", [128, 1], BF16, kind="ExternalInput").ap()
    onesrow_d = nc.dram_tensor("onesrow", [1, 128], F32, kind="ExternalInput").ap()
    eps_d = nc.dram_tensor("epsv", [1, 2], F32, kind="ExternalInput").ap()

    out_d = nc.dram_tensor("out", [NTT * 128, NDC * 512], F32, kind="ExternalOutput").ap()

    xn_con = nc.dram_tensor("xn_con", [128, NDC * TOK], BF16, kind="Internal").ap()
    xn_gat = nc.dram_tensor("xn_gat", [NCORES * 128, NDC * TOK], BF16, kind="Internal",
                            addr_space="Shared").ap()
    av_con = nc.dram_tensor("av_con", [128 * HPC, S], BF16, kind="Internal").ap()
    av_gat = nc.dram_tensor("av_gat", [D, S], BF16, kind="Internal",
                            addr_space="Shared").ap()
    x1_sp = nc.dram_tensor("x1_sp", [NTT * 128, NDC * 512], F32, kind="Internal").ap()

    taps = {}
    for t in _DEBUG_TAPS:
        shapes = {"xn": ([128, NDC * TOK], BF16), "qt": ([128, HPC * S], BF16),
                  "kt": ([128, HPC * S], BF16), "vt": ([128, HPC * S], BF16),
                  "avo": ([128, NDC * TOK], BF16), "x1": ([NTT * 128, NDC * 512], F32),
                  "xn2": ([128, NDC * TOK], BF16), "ao": ([128, NDC * 512], F32),
                  "negm": ([128, HPC * NQC], F32), "hb": ([128, NIC * 512], BF16)}
        sh, dt = shapes[t]
        taps[t] = nc.dram_tensor("tap_" + t, sh, dt, kind="ExternalOutput").ap()

    from contextlib import ExitStack
    es = ExitStack()
    sb = lambda n, s_, d: es.enter_context(nc.sbuf_tensor(n, s_, d))
    psb = lambda n, s_, d: es.enter_context(nc.psum_tensor(n, s_, d))

    sems = {}
    for name in ("dS", "dG", "tT", "aA", "vV", "gG", "cC"):
        sems[name] = es.enter_context(nc.semaphore(name))

    onescol = sb("onescol_s", [128, 1], BF16)
    onesrow = sb("onesrow_s", [1, 128], F32)
    epsv = sb("eps_s", [1, 2], F32)
    bqk = sb("bqk_s", [128, 4], F32)
    bo_s = sb("bo_s", [128, NDC], F32)
    lng = sb("lng_s", [128, NDC], F32)
    lnb = sb("lnb_s", [128, NDC], F32)
    gsc = sb("gsc_s", [128, NIC], F32)
    usc = sb("usc_s", [128, NIC], F32)
    dsc = sb("dsc_s", [128, NDC], F32)
    negm = sb("negm_s", [128, HPC * NQC], F32)
    rowA = sb("rowA_s", [1, 512], F32)
    rowB = sb("rowB_s", [1, 512], F32)
    mhatln = sb("mhatln_s", [1, NQC], F32)
    mxkln = sb("mxkln_s", [1, NQC], F32)
    mexp = sb("mexp_s", [128, NQC], F32)

    psS = [psb(f"psS{i}", [128, 512], F32) for i in range(4)]
    psO = psb("psO", [128, 512], F32)
    psL = psb("psL", [1, 512], F32)
    psB = psb("psB", [128, 512], F32)

    p = Prog()
    freed = {}

    def wait_free(eng, key):
        st = freed.get(key)
        if st is not None:
            sem, v = st
            p.on(eng, lambda E, s=sems[sem], vv=v: (E.wait_ge(s, vv), None)[1])

    def release(key, sem, val):
        freed[key] = (sem, val)

    def emit(eng, f):
        p.on(eng, f)

    def dma(eng, sem_name, out_ap, in_ap, **kw):
        v = p.inc(sem_name, 16)
        p.on(eng, lambda E, o=out_ap, i=in_ap, s=sems[sem_name], kw=kw:
             E.dma_start(out=o, in_=i, **kw).then_inc(s, 16))
        return v

    def wait(eng, sem_name, val):
        if val > 0:
            p.on(eng, lambda E, s=sems[sem_name], v=val: (E.wait_ge(s, v), None)[1])

    for dst, src in ((onescol, onescol_d), (onesrow, onesrow_d), (epsv, eps_d),
                     (bqk, bqk_d), (bo_s, bo_d), (lng, lng_d),
                     (lnb, lnb_d), (gsc, gsc_d), (usc, usc_d), (dsc, dsc_d)):
        dma("sync", "dS", dst[:], src)
    v_consts = p.val("dS")

    es1 = ExitStack()
    sb1 = lambda n, s_, d: es1.enter_context(nc.sbuf_tensor(n, s_, d))
    xbuf = sb1("xbuf", [128, NDC * 512], F32)
    xbb = sb1("xbb", [128, NDC * 512], BF16)
    cb1 = sb1("cb1", [128, NDC * 512], F32)
    cb2 = sb1("cb2", [128, NDC * 512], F32)
    xnb = sb1("xnb", [128, NDC * TOK], BF16)
    sqb = sb1("sqb", [128, 2 * 512], BF16)

    def ln_plain(xf, xf_off, xbf, xbf_off, ncols, out_writer, aff=None):
        wait_free("tensor", "psL")
        for a in range(NDC):
            vs = p.inc("tT")
            p.on("tensor", lambda E, a=a: E.matmul(
                psL[:, 0:ncols], onescol[:],
                xbf[:, xbf_off + a * ncols: xbf_off + (a + 1) * ncols],
                start=(a == 0), stop=(a == NDC - 1)).then_inc(sems["tT"], 1))
        v_sum = p.val("tT")
        wait("scalar", "tT", v_sum)
        va = p.inc("aA")
        p.on("scalar", lambda E: E.activation(
            rowA[0:1, 0:ncols], psL[:, 0:ncols], AF.Identity,
            scale=float(-1.0 / D)).then_inc(sems["aA"], 1))
        release("psL", "aA", va)
        wait_free("tensor", "psB")
        wait("tensor", "aA", va)
        vb = p.inc("tT")
        p.on("tensor", lambda E: E.matmul(
            psB[:, 0:ncols], onesrow[:], rowA[0:1, 0:ncols],
            start=True, stop=True).then_inc(sems["tT"], 1))
        wait("vector", "tT", vb)
        wait_free("tensor", "psO")
        sq_rd = {}
        for a in range(NDC):
            vv = p.inc("vV")
            p.on("vector", lambda E, a=a: E.tensor_tensor(
                cb1[:, a * ncols:(a + 1) * ncols],
                xf[:, xf_off + a * ncols: xf_off + (a + 1) * ncols],
                psB[:, 0:ncols], ALU.add).then_inc(sems["vV"], 1))
            wait("scalar", "vV", vv)
            if a >= 2:
                wait("scalar", "tT", sq_rd[a % 2])
            vq = p.inc("aA")
            p.on("scalar", lambda E, a=a: E.activation(
                sqb[:, 512 * (a % 2):512 * (a % 2) + ncols],
                cb1[:, a * ncols:(a + 1) * ncols], AF.Square).then_inc(sems["aA"], 1))
            wait("tensor", "aA", vq)
            vt2 = p.inc("tT")
            p.on("tensor", lambda E, a=a: E.matmul(
                psO[0:1, 0:ncols], onescol[:],
                sqb[:, 512 * (a % 2):512 * (a % 2) + ncols],
                start=(a == 0), stop=(a == NDC - 1)).then_inc(sems["tT"], 1))
            sq_rd[a % 2] = vt2
        v_cent = p.val("vV")
        v_sq = p.val("tT")
        release("psB", "vV", v_cent)
        wait("scalar", "tT", v_sq)
        p.on("scalar", lambda E: E.activation(
            rowB[0:1, 0:ncols], psO[0:1, 0:ncols], AF.Ln,
            bias=epsv[0:1, 0:1], scale=float(1.0 / D)))
        vr = p.inc("aA")
        p.on("scalar", lambda E: E.activation(
            rowA[0:1, 0:ncols], rowB[0:1, 0:ncols], AF.Exp,
            scale=-0.5).then_inc(sems["aA"], 1))
        release("psO", "aA", vr)
        wait_free("tensor", "psB")
        wait("tensor", "aA", vr)
        vb2 = p.inc("tT")
        p.on("tensor", lambda E: E.matmul(
            psB[:, 0:ncols], onesrow[:], rowA[0:1, 0:ncols],
            start=True, stop=True).then_inc(sems["tT"], 1))
        wait("vector", "tT", vb2)
        for a in range(NDC):
            out_writer(a)
        vfin = p.val("vV")
        release("psB", "vV", vfin)
        return vfin

    v_xt_tiles = []
    for tt in range(NTT):
        if tt > 0:
            wait("sync", "vV", p.val("vV"))
            wait("sync", "tT", p.val("tT"))
            wait("gpsimd", "vV", p.val("vV"))
            wait("gpsimd", "tT", p.val("tT"))
        src = xt_d[128 * tt:128 * tt + 128, :]
        vx = dma("sync", "dS", xbuf[:], src)
        vxb = dma("gpsimd", "dG", xbb[:], src)
        wait("tensor", "dS", vx)
        wait("tensor", "dG", vxb)
        wait("vector", "dS", vx)
        wait("vector", "dG", vxb)
        wait("scalar", "dS", vx)

        if P1MIN:
            for a in range(NDC):
                vv = p.inc("vV")
                p.on("vector", lambda E, a=a, tt=tt: E.tensor_copy(
                    xnb[:, a * TOK + 512 * tt: a * TOK + 512 * tt + 512],
                    xbb[:, a * 512:(a + 1) * 512]).then_inc(sems["vV"], 1))
            v_xt_tiles.append(p.val("vV"))
            continue
        def w1(a):
            vv = p.inc("vV")
            p.on("vector", lambda E, a=a: E.tensor_tensor(
                cb2[:, a * 512:(a + 1) * 512], cb1[:, a * 512:(a + 1) * 512],
                psB[:, 0:512], ALU.mult).then_inc(sems["vV"], 1))
            wait("gpsimd", "vV", vv)
            vg = p.inc("gG")
            p.on("gpsimd", lambda E, a=a: E.tensor_copy(
                xbb[:, a * 512:(a + 1) * 512],
                cb2[:, a * 512:(a + 1) * 512]).then_inc(sems["gG"], 1))
        ln_plain(xbuf, 0, xbb, 0, 512, w1)
        v_y1bf = p.val("gG")
        wait("tensor", "gG", v_y1bf)

        def w2(a):
            vv0 = p.inc("vV")
            p.on("vector", lambda E, a=a: E.tensor_tensor(
                cb1[:, a * 512:(a + 1) * 512], cb1[:, a * 512:(a + 1) * 512],
                psB[:, 0:512], ALU.mult).then_inc(sems["vV"], 1))
            vv1 = p.inc("vV")
            p.on("vector", lambda E, a=a, tt=tt: E.tensor_scalar(
                xnb[:, a * TOK + 512 * tt: a * TOK + 512 * tt + 512],
                cb1[:, a * 512:(a + 1) * 512],
                lng[:, a:a + 1], lnb[:, a:a + 1],
                ALU.mult, ALU.add).then_inc(sems["vV"], 1))
        ln_plain(cb2, 0, xbb, 0, 512, w2)
        v_xn_tt = p.val("vV")
        v_xt_tiles.append(v_xn_tt)

    wait("sync", "vV", p.val("vV"))
    v_con = dma("sync", "dS", xn_con, xnb[:])
    if "xn" in taps:
        dma("sync", "dS", taps["xn"], xnb[:])
    wait("gpsimd", "dS", v_con)
    vcc1 = p.inc("cC")
    p.on("gpsimd", lambda E: E.collective_compute(
        "AllGather", ALU.bypass, replica_groups=[list(range(NCORES))],
        ins=[xn_con], outs=[xn_gat],
    ).then_inc(sems["cC"], 1))

    es1.close()

    if PL >= 2:
            es2 = ExitStack()
            sb2 = lambda n, s_, d: es2.enter_context(nc.sbuf_tensor(n, s_, d))
            wqb = sb2("wqb", [128, NDC * 256], BF16)
            wkb = sb2("wkb", [128, NDC * 256], BF16)
            wvb = sb2("wvb", [128, NDC * 256], BF16)
            xg = [sb2(f"xg{i}", [128, NDC * TOK], BF16) for i in range(2)]
            qt = sb2("qt", [128, HPC * S], BF16)
            kt = sb2("kt", [128, HPC * S], BF16)
            vt = sb2("vt", [128, HPC * S], BF16)
            pbuf = sb2("pbuf", [128, 4 * 512], BF16)
            avst = sb2("avst", [128, 512], BF16)
            sqn = sb2("sqn", [128, 2 * 512], BF16)
            nrow = sb2("nrow", [1, 2 * 512], F32)
            mkparts = sb2("mkparts", [1, NQC], F32)
            cmask = sb2("cmask_s", [128, 4 * 512], BF16)
            bvrow = sb2("bvrow_s", [1, 256], F32)
            bvb = sb2("bvb_s", [128, 256], F32)
            dma("sync", "dS", cmask[:], cmask_d)
            v_bv = dma("sync", "dS", bvrow[:], bv_d)
            wait_free("tensor", "psB")
            wait("tensor", "dS", v_bv)
            vbv = p.inc("tT")
            p.on("tensor", lambda E: E.matmul(psB[:, 0:256], onesrow[:], bvrow[:],
                                              start=True, stop=True).then_inc(sems["tT"], 1))
            wait("scalar", "tT", vbv)
            vbv2 = p.inc("aA")
            p.on("scalar", lambda E: E.activation(bvb[:], psB[:, 0:256],
                                                  AF.Identity).then_inc(sems["aA"], 1))
            release("psB", "aA", vbv2)

            if P2M == "cc":
                wait("gpsimd", "cC", vcc1)
            for dst, src in (((wqb, wq_d), (wkb, wk_d), (wvb, wv_d)) if P2M not in ("xg", "cc") else ()):
                wait("gpsimd", "cC", vcc1)
                dma("gpsimd", "dG", dst[:], src)
            v_wqkv = p.val("dG")

            bank_state = {}

            def use_psS(i, eng="tensor"):
                st = bank_state.get(i)
                if st:
                    wait(eng, st[0], st[1])

            xg_free = [None, None]
            for r in (range(NCORES) if P2M not in ("w", "cc") else ()):
                b = r % 2
                if xg_free[b]:
                    wait("sync", xg_free[b][0], xg_free[b][1])
                wait("sync", "cC", vcc1)
                v = dma("sync", "dS", xg[b][:], xn_gat[128 * r:128 * (r + 1), :])
                wait("tensor", "dS", v)
                if r == 0:
                    wait("tensor", "dG", v_wqkv)
                for half in range(2):
                    g = 2 * r + half
                    xoff = 512 * half
                    for (wb, dstbuf) in (((wqb, qt), (wkb, kt)) if P2M in ("qk", "all") else ()):
                        for h in range(2):
                            bank = (2 * g + h) % 4
                            use_psS(bank)
                            for a in range(NDC):
                                p.on("tensor", lambda E, a=a, h=h, wb=wb, bank=bank, b=b, xoff=xoff: E.matmul(
                                    psS[bank][:], wb[:, a * 256 + 128 * h: a * 256 + 128 * h + 128],
                                    xg[b][:, a * TOK + xoff: a * TOK + xoff + 512],
                                    start=(a == 0), stop=(a == NDC - 1)))
                            vt_mm = p.inc("tT")
                            p.on("tensor", ("INC_LAST", "tT"))
                            wait("scalar", "tT", vt_mm)
                            bcol = (0 if dstbuf is qt else 2) + h
                            va2 = p.inc("aA")
                            p.on("scalar", lambda E, bank=bank, h=h, g=g, dstbuf=dstbuf, bcol=bcol:
                                 E.activation(dstbuf[:, h * S + 512 * g: h * S + 512 * g + 512],
                                              psS[bank][:], AF.Identity,
                                              bias=bqk[:, bcol:bcol + 1]).then_inc(sems["aA"], 1))
                            bank_state[bank] = ("aA", va2)
                    for sub in (range(4) if P2M in ("v", "all") else ()):
                        bank = sub % 4
                        use_psS(bank)
                        for a in range(NDC):
                            p.on("tensor", lambda E, a=a, sub=sub, bank=bank, b=b, xoff=xoff: E.matmul(
                                psS[bank][:, 0:256],
                                xg[b][:, a * TOK + xoff + 128 * sub: a * TOK + xoff + 128 * sub + 128],
                                wvb[:, a * 256:(a + 1) * 256],
                                start=(a == 0), stop=(a == NDC - 1)))
                        vt_mm = p.inc("tT")
                        p.on("tensor", ("INC_LAST", "tT"))
                        wait("vector", "tT", vt_mm)
                        if g == 0 and sub == 0:
                            wait("vector", "aA", vbv2)
                        kc = 4 * g + sub
                        vv2 = p.inc("vV")
                        p.on("vector", lambda E, bank=bank, kc=kc: E.tensor_tensor(
                            vt[:, 256 * kc:256 * kc + 256], psS[bank][:, 0:256], bvb[:],
                            ALU.add).then_inc(sems["vV"], 1))
                        bank_state[bank] = ("vV", vv2)
                xg_free[b] = ("tT", p.val("tT"))

            if "qt" in taps:
                wait("sync", "aA", p.val("aA"))
                dma("sync", "dS", taps["qt"], qt[:])
                dma("sync", "dS", taps["kt"], kt[:])
            if "vt" in taps:
                wait("sync", "vV", p.val("vV"))
                dma("sync", "dS", taps["vt"], vt[:])

    if PL >= 3:
            v_qk_done = p.val("aA")
            v_prev_negm = 0
            sqn_rd = {}
            for h in range(HPC):
                nrow_free = [None, None]
                for t in range(2 * (S // 512)):
                    srcbuf, col = (qt, 512 * t) if t < S // 512 else (kt, 512 * (t - S // 512))
                    col += h * S
                    wait("scalar", "aA", v_qk_done)
                    if h > 0 and t == 0:
                        wait("scalar", "vV", v_prev_negm)
                    if sqn_rd.get(t % 2):
                        wait("scalar", "tT", sqn_rd[t % 2])
                    vq = p.inc("aA")
                    p.on("scalar", lambda E, srcbuf=srcbuf, col=col, t=t: E.activation(
                        sqn[:, 512 * (t % 2):512 * (t % 2) + 512],
                        srcbuf[:, col:col + 512], AF.Square).then_inc(sems["aA"], 1))
                    wait("tensor", "aA", vq)
                    use_psS(t % 2)
                    vm = p.inc("tT")
                    p.on("tensor", lambda E, t=t: E.matmul(
                        psS[t % 2][0:1, :], onescol[:],
                        sqn[:, 512 * (t % 2):512 * (t % 2) + 512],
                        start=True, stop=True).then_inc(sems["tT"], 1))
                    sqn_rd[t % 2] = vm
                    wait("scalar", "tT", vm)
                    if nrow_free[t % 2]:
                        wait("scalar", *nrow_free[t % 2])
                    sc = 1.0 if t < S // 512 else (1.0 / 128.0)
                    va3 = p.inc("aA")
                    p.on("scalar", lambda E, sc=sc, t=t: E.activation(
                        nrow[0:1, 512 * (t % 2):512 * (t % 2) + 512], psS[t % 2][0:1, :],
                        AF.Ln, bias=epsv[0:1, 1:2], scale=sc).then_inc(sems["aA"], 1))
                    bank_state[t % 2] = ("aA", va3)
                    wait("vector", "aA", va3)
                    vmx_t = p.inc("vV")
                    dst, colq = (mhatln, t) if t < S // 512 else (mkparts, t - S // 512)
                    p.on("vector", lambda E, dst=dst, colq=colq, t=t: E.reduce_max(
                        dst[0:1, colq:colq + 1],
                        nrow[0:1, 512 * (t % 2):512 * (t % 2) + 512],
                        axis=AX.X).then_inc(sems["vV"], 1))
                    nrow_free[t % 2] = ("vV", vmx_t)
                for qc in range(NQC):
                    p.on("vector", lambda E, qc=qc: E.reduce_max(
                        mxkln[0:1, qc:qc + 1], mkparts[0:1, :], axis=AX.X))
                vmx = p.inc("vV")
                p.on("vector", ("INC_LAST", "vV"))
                wait_free("tensor", "psB")
                wait("tensor", "vV", vmx)
                p.on("tensor", lambda E: E.matmul(psB[:, 0:NQC], onesrow[:], mhatln[:],
                                                  start=True, stop=False))
                vb3 = p.inc("tT")
                p.on("tensor", lambda E: E.matmul(psB[:, 0:NQC], onesrow[:], mxkln[:],
                                                  start=False, stop=True).then_inc(sems["tT"], 1))
                wait("scalar", "tT", vb3)
                vme = p.inc("aA")
                p.on("scalar", lambda E: E.activation(mexp[:], psB[:, 0:NQC], AF.Exp,
                                                      scale=0.5).then_inc(sems["aA"], 1))
                release("psB", "aA", vme)
                wait("vector", "aA", vme)
                vng = p.inc("vV")
                p.on("vector", lambda E, h=h: E.tensor_scalar_mul(
                    negm[:, NQC * h:NQC * (h + 1)], mexp[:], -1.0).then_inc(sems["vV"], 1))
                v_prev_negm = vng
            v_negm = p.val("vV")
            if "negm" in taps:
                wait("sync", "vV", v_negm)
                dma("sync", "dS", taps["negm"], negm[:])

            DEPTH = 4
            v_av_con = []
            for h in range(HPC):
                for qc in range(NQC):
                    J = 4 * (qc + 1)
                    qof = h * S + 512 * qc
                    sdone = {}
                    expdone = {}
                    maskdone = {}
                    avdone = {}

                    def emit_S(j):
                        bank = j % DEPTH
                        use_psS(bank)
                        if j < DEPTH and qc == 0 and h == 0:
                            wait("tensor", "vV", v_negm)
                        vs = p.inc("tT")
                        p.on("tensor", lambda E, j=j, bank=bank, qof=qof, h=h: E.matmul(
                            psS[bank][:], kt[:, h * S + 128 * j: h * S + 128 * j + 128],
                            qt[:, qof:qof + 512], start=True, stop=True).then_inc(sems["tT"], 1))
                        sdone[j] = vs

                    for j in range(min(DEPTH, J)):
                        emit_S(j)
                    for j in range(J):
                        bank = j % DEPTH
                        wait("scalar", "tT", sdone[j])
                        if j - DEPTH >= 0:
                            wait("scalar", "tT", avdone[j - DEPTH])
                        ve = p.inc("aA")
                        p.on("scalar", lambda E, j=j, bank=bank, h=h, qc=qc: E.activation(
                            pbuf[:, 512 * (j % DEPTH):512 * (j % DEPTH) + 512],
                            psS[bank][:], AF.Exp,
                            bias=negm[:, NQC * h + qc: NQC * h + qc + 1],
                            scale=ISQ).then_inc(sems["aA"], 1))
                        expdone[j] = ve
                        bank_state[bank] = ("aA", ve)
                        r = j - 4 * qc
                        if r >= 0:
                            wait("vector", "aA", ve)
                            vmk = p.inc("vV")
                            p.on("vector", lambda E, j=j, r=r: E.tensor_tensor(
                                pbuf[:, 512 * (j % DEPTH):512 * (j % DEPTH) + 512],
                                pbuf[:, 512 * (j % DEPTH):512 * (j % DEPTH) + 512],
                                cmask[:, 512 * r:512 * r + 512], ALU.mult).then_inc(sems["vV"], 1))
                            maskdone[j] = vmk
                            wait("tensor", "vV", vmk)
                        else:
                            wait("tensor", "aA", ve)
                        if j == 0:
                            wait_free("tensor", "psL")
                            wait_free("tensor", "psO")
                        p.on("tensor", lambda E, j=j, first=(j == 0), last=(j == J - 1): E.matmul(
                            psL[:], onescol[:], pbuf[:, 512 * (j % DEPTH):512 * (j % DEPTH) + 512],
                            start=first, stop=last))
                        vav = p.inc("tT")
                        p.on("tensor", lambda E, j=j, h=h, first=(j == 0), last=(j == J - 1): E.matmul(
                            psO[:], vt[:, 256 * j + 128 * h: 256 * j + 128 * h + 128],
                            pbuf[:, 512 * (j % DEPTH):512 * (j % DEPTH) + 512],
                            start=first, stop=last).then_inc(sems["tT"], 1))
                        avdone[j] = vav
                        if j + DEPTH < J:
                            emit_S(j + DEPTH)
                    wait("scalar", "tT", avdone[J - 1])
                    p.on("scalar", lambda E: E.activation(rowB[0:1, :], psL[:], AF.Ln))
                    vrl = p.inc("aA")
                    p.on("scalar", lambda E: E.activation(rowA[0:1, :], rowB[0:1, :], AF.Exp,
                                                          scale=-1.0).then_inc(sems["aA"], 1))
                    release("psL", "aA", vrl)
                    wait_free("tensor", "psB")
                    wait("tensor", "aA", vrl)
                    vbc = p.inc("tT")
                    p.on("tensor", lambda E: E.matmul(psB[:], onesrow[:], rowA[0:1, :],
                                                      start=True, stop=True).then_inc(sems["tT"], 1))
                    wait("vector", "tT", vbc)
                    p.on("vector", lambda E: E.tensor_copy(sqn.ap().bitcast(F32)[:, 0:512], psB[:]))
                    vfin = p.inc("vV")
                    p.on("vector", lambda E: E.tensor_tensor(
                        avst[:], psO[:], sqn.ap().bitcast(F32)[:, 0:512],
                        ALU.mult).then_inc(sems["vV"], 1))
                    release("psB", "vV", vfin)
                    release("psO", "vV", vfin)
                    wait("sync", "vV", vfin)
                    vdc = dma("sync", "dS", av_con[128 * h:128 * h + 128, 512 * qc:512 * qc + 512],
                              avst[:])
                    v_av_con.append(vdc)
                    wait("vector", "dS", vdc)

            wait("gpsimd", "dS", max(v_av_con))
            vcc2 = p.inc("cC")
            p.on("gpsimd", lambda E: E.collective_compute(
                "AllGather", ALU.bypass, replica_groups=[list(range(NCORES))],
                ins=[av_con], outs=[av_gat]).then_inc(sems["cC"], 1))

            es2.close()
    if PL >= 4:

            xn2 = es.enter_context(nc.sbuf_tensor("xn2", [128, NDC * TOK], BF16))
            es3 = ExitStack()
            sb3 = lambda n, s_, d: es3.enter_context(nc.sbuf_tensor(n, s_, d))
            avo = sb3("avo", [128, NDC * TOK], BF16)
            wob = [sb3(f"wob{i}", [128, NDC * 128], BF16) for i in range(2)]
            ao = sb3("ao", [128, NDC * 512], F32)
            aobf = sb3("aobf", [128, NDC * 512], BF16)
            cb = sb3("cb", [128, NDC * 512], F32)
            xre = [sb3(f"xre{i}", [128, 512], F32) for i in range(2)]
            sqb3 = sb3("sqb3", [128, 2 * 512], BF16)

            cell = {}
            wait("gpsimd", "cC", vcc2)
            def _getcid(E):
                cell["cid"] = E.partition_id()
            p.on("gpsimd", _getcid)
            vavo = p.inc("dG", 16 * NDC)
            def _avoload(E):
                for j in range(NDC):
                    E.dma_start(out=avo[:, j * TOK:(j + 1) * TOK],
                                in_=av_gat[128 * j:128 * (j + 1), bass.ts(cell["cid"], TOK)]
                                ).then_inc(sems["dG"], 16)
            p.on("gpsimd", _avoload)

            def ln_plain3(xf, xbf, out_writer, sqbuf):
                for a in range(NDC):
                    p.on("tensor", lambda E, a=a: E.matmul(
                        psL[:], onescol[:], xbf[:, a * 512:(a + 1) * 512],
                        start=(a == 0), stop=(a == NDC - 1)))
                v_sum = p.inc("tT")
                p.on("tensor", ("INC_LAST", "tT"))
                wait("scalar", "tT", v_sum)
                va = p.inc("aA")
                p.on("scalar", lambda E: E.activation(
                    rowA[0:1, :], psL[:], AF.Identity,
                    scale=float(-1.0 / D)).then_inc(sems["aA"], 1))
                release("psL", "aA", va)
                wait_free("tensor", "psB")
                wait("tensor", "aA", va)
                vb = p.inc("tT")
                p.on("tensor", lambda E: E.matmul(psB[:], onesrow[:], rowA[0:1, :],
                                                  start=True, stop=True).then_inc(sems["tT"], 1))
                wait("vector", "tT", vb)
                wait_free("tensor", "psO")
                sq_rd = {}
                for a in range(NDC):
                    vv = p.inc("vV")
                    p.on("vector", lambda E, a=a: E.tensor_tensor(
                        cb[:, a * 512:(a + 1) * 512], xf[:, a * 512:(a + 1) * 512],
                        psB[:], ALU.add).then_inc(sems["vV"], 1))
                    wait("scalar", "vV", vv)
                    if a >= 2:
                        wait("scalar", "tT", sq_rd[a % 2])
                    vq = p.inc("aA")
                    p.on("scalar", lambda E, a=a: E.activation(
                        sqbuf[:, 512 * (a % 2):512 * (a % 2) + 512],
                        cb[:, a * 512:(a + 1) * 512], AF.Square).then_inc(sems["aA"], 1))
                    wait("tensor", "aA", vq)
                    vt2 = p.inc("tT")
                    p.on("tensor", lambda E, a=a: E.matmul(
                        psO[0:1, :], onescol[:], sqbuf[:, 512 * (a % 2):512 * (a % 2) + 512],
                        start=(a == 0), stop=(a == NDC - 1)).then_inc(sems["tT"], 1))
                    sq_rd[a % 2] = vt2
                v_cent = p.val("vV")
                release("psB", "vV", v_cent)
                vsq = p.val("tT")
                wait("scalar", "tT", vsq)
                p.on("scalar", lambda E: E.activation(rowB[0:1, :], psO[0:1, :], AF.Ln,
                                                      bias=epsv[0:1, 0:1], scale=float(1.0 / D)))
                vr = p.inc("aA")
                p.on("scalar", lambda E: E.activation(rowA[0:1, :], rowB[0:1, :], AF.Exp,
                                                      scale=-0.5).then_inc(sems["aA"], 1))
                release("psO", "aA", vr)
                wait_free("tensor", "psB")
                wait("tensor", "aA", vr)
                vb2 = p.inc("tT")
                p.on("tensor", lambda E: E.matmul(psB[:], onesrow[:], rowA[0:1, :],
                                                  start=True, stop=True).then_inc(sems["tT"], 1))
                wait("vector", "tT", vb2)
                for a in range(NDC):
                    out_writer(a)
                release("psB", "vV", p.val("vV"))

            for tt in range(NTT):
                for ch in range(NDC):
                    bw = ch % 2
                    st = freed.get(("wob", bw))
                    if st:
                        wait("gpsimd", st[0], st[1])
                    if tt == 0 and ch < 2:
                        wait("gpsimd", "cC", vcc2)
                    vw = dma("gpsimd", "dG", wob[bw][:],
                             wo_d[128 * ch:128 * ch + 128, :])
                    wait("tensor", "dG", max(vw, vavo))
                    bank = ch % 4
                    use_psS(bank)
                    for a in range(NDC):
                        p.on("tensor", lambda E, a=a, bw=bw, tt=tt, bank=bank: E.matmul(
                            psS[bank][:],
                            wob[bw][:, 128 * a:128 * a + 128],
                            avo[:, a * TOK + 512 * tt: a * TOK + 512 * tt + 512],
                            start=(a == 0), stop=(a == NDC - 1)))
                    vmm = p.inc("tT")
                    p.on("tensor", ("INC_LAST", "tT"))
                    freed[("wob", bw)] = ("tT", vmm)
                    wait("scalar", "tT", vmm)
                    vev = p.inc("aA")
                    p.on("scalar", lambda E, ch=ch, bank=bank: E.activation(
                        ao[:, ch * 512:(ch + 1) * 512], psS[bank][:], AF.Identity,
                        bias=bo_s[:, ch:ch + 1]).then_inc(sems["aA"], 1))
                    bank_state[bank] = ("aA", vev)
                    wait("gpsimd", "aA", vev)
                    vg = p.inc("gG")
                    p.on("gpsimd", lambda E, ch=ch: E.tensor_copy(
                        aobf[:, ch * 512:(ch + 1) * 512],
                        ao[:, ch * 512:(ch + 1) * 512]).then_inc(sems["gG"], 1))
                if "ao" in taps and tt == 0:
                    wait("sync", "aA", p.val("aA"))
                    dma("sync", "dS", taps["ao"], ao[:])
                wait("tensor", "gG", p.val("gG"))

                def w3(a):
                    vv = p.inc("vV")
                    p.on("vector", lambda E, a=a: E.tensor_tensor(
                        ao[:, a * 512:(a + 1) * 512], cb[:, a * 512:(a + 1) * 512],
                        psB[:], ALU.mult).then_inc(sems["vV"], 1))
                ln_plain3(ao, aobf, w3, sqb3)
                for a in range(NDC):
                    bx = a % 2
                    st = freed.get(("xre", bx))
                    if st:
                        wait("sync", st[0], st[1])
                    vx = dma("sync", "dS", xre[bx][:],
                             xt_d[128 * tt:128 * tt + 128, a * 512:(a + 1) * 512])
                    wait("vector", "dS", vx)
                    vv = p.inc("vV")
                    p.on("vector", lambda E, a=a, bx=bx: E.tensor_tensor(
                        ao[:, a * 512:(a + 1) * 512], ao[:, a * 512:(a + 1) * 512],
                        xre[bx][:], ALU.add).then_inc(sems["vV"], 1))
                    freed[("xre", bx)] = ("vV", vv)
                    wait("gpsimd", "vV", vv)
                    vg = p.inc("gG")
                    p.on("gpsimd", lambda E, a=a: E.tensor_copy(
                        aobf[:, a * 512:(a + 1) * 512],
                        ao[:, a * 512:(a + 1) * 512]).then_inc(sems["gG"], 1))
                v_x1 = p.val("vV")
                wait("sync", "vV", v_x1)
                vsp = dma("sync", "dS", x1_sp[128 * tt:128 * tt + 128, :], ao[:])
                if "x1" in taps:
                    dma("sync", "dS", taps["x1"][128 * tt:128 * tt + 128, :], ao[:])
                wait("tensor", "gG", p.val("gG"))
                def w4(a):
                    vv = p.inc("vV")
                    p.on("vector", lambda E, a=a, tt=tt: E.tensor_tensor(
                        xn2[:, a * TOK + 512 * tt: a * TOK + 512 * tt + 512],
                        cb[:, a * 512:(a + 1) * 512], psB[:], ALU.mult).then_inc(sems["vV"], 1))
                ln_plain3(ao, aobf, w4, sqb3)
                wait("tensor", "vV", p.val("vV"))
            v_xn2 = p.val("vV")
            if "xn2" in taps:
                wait("sync", "vV", v_xn2)
                dma("sync", "dS", taps["xn2"], xn2[:])
            if "avo" in taps:
                dma("sync", "dS", taps["avo"], avo[:])
            es3.close()
    if PL >= 5:

            es4 = ExitStack()
            sb4 = lambda n, s_, d: es4.enter_context(nc.sbuf_tensor(n, s_, d))
            hbuf = sb4("hbuf", [128, NIC * 512], BF16)
            wg8 = [sb4(f"wg8{i}", [128, NDC * 128], U8) for i in range(2)]
            wu8b = [sb4(f"wu8{i}", [128, NDC * 128], U8) for i in range(2)]
            wgb = [sb4(f"wgb{i}", [128, NDC * 128], BF16) for i in range(2)]
            wub = [sb4(f"wub{i}", [128, NDC * 128], BF16) for i in range(2)]
            wd8 = [sb4(f"wd8{i}", [128, 2048], U8) for i in range(2)]
            wdb = sb4("wdb", [128, NIC * 128], BF16)
            sgu = sb4("sgu", [128, 4 * 512], F32)
            fo = sb4("fo", [128, NDC * 512], F32)
            sqb4 = sb4("sqb4", [128, 2 * 512], BF16)
            fobf = sb4("fobf", [128, NDC * 512], BF16)
            xr4 = [sb4(f"xr4{i}", [128, 512], F32) for i in range(2)]

            for tt in range(NTT):
                for i in range(NIC):
                    bw = i % 2
                    for (w8s, wbs, wdrm) in ((wg8, wgb, wgt_d), (wu8b, wub, wut_d)):
                        st = freed.get(("w8", id(w8s), bw))
                        if st:
                            wait("sync", st[0], st[1])
                        if tt == 0 and i < 2:
                            wait("sync", "vV", v_xn2)
                        vw = dma("sync", "dS", w8s[bw][:],
                                 wdrm[128 * i:128 * i + 128, :])
                        st2 = freed.get(("wb", id(wbs), bw))
                        if st2:
                            wait("gpsimd", st2[0], st2[1])
                        wait("gpsimd", "dS", vw)
                        vg = p.inc("gG")
                        p.on("gpsimd", lambda E, w8s=w8s, wbs=wbs, bw=bw: E.tensor_scalar(
                            wbs[bw][:], w8s[bw][:], -128.0, None, ALU.add).then_inc(sems["gG"], 1))
                        freed[("w8", id(w8s), bw)] = ("gG", vg)
                        freed[("wconv", id(wbs), bw)] = vg
                    if P6N < 2:
                        continue
                    bank_g = (2 * i) % 4
                    bank_u = (2 * i + 1) % 4
                    wait("tensor", "gG", freed[("wconv", id(wgb), bw)])
                    wait("tensor", "gG", freed[("wconv", id(wub), bw)])
                    for (bk, wbs) in ((bank_g, wgb), (bank_u, wub)):
                        use_psS(bk)
                        for a in range(NDC):
                            p.on("tensor", lambda E, a=a, bk=bk, wbs=wbs, bw=bw, tt=tt: E.matmul(
                                psS[bk][:], wbs[bw][:, 128 * a:128 * a + 128],
                                xn2[:, a * TOK + 512 * tt: a * TOK + 512 * tt + 512],
                                start=(a == 0), stop=(a == NDC - 1)))
                    vmm = p.inc("tT")
                    p.on("tensor", ("INC_LAST", "tT"))
                    freed[("wb", id(wgb), bw)] = ("tT", vmm)
                    freed[("wb", id(wub), bw)] = ("tT", vmm)
                    sguoff = 1024 * (i % 2)
                    st3 = freed.get(("sgu", i % 2))
                    if st3:
                        wait("scalar", st3[0], st3[1])
                    wait("scalar", "tT", vmm)
                    p.on("scalar", lambda E, bank_g=bank_g, i=i, sguoff=sguoff: E.activation(
                        sgu[:, sguoff:sguoff + 512], psS[bank_g][:], AF.Silu,
                        scale=gsc[:, i:i + 1]))
                    vev = p.inc("aA")
                    p.on("scalar", lambda E, bank_u=bank_u, i=i, sguoff=sguoff: E.activation(
                        sgu[:, sguoff + 512:sguoff + 1024], psS[bank_u][:], AF.Identity,
                        scale=usc[:, i:i + 1]).then_inc(sems["aA"], 1))
                    bank_state[bank_g] = ("aA", vev)
                    bank_state[bank_u] = ("aA", vev)
                    wait("vector", "aA", vev)
                    vh = p.inc("vV")
                    p.on("vector", lambda E, i=i, sguoff=sguoff: E.tensor_tensor(
                        hbuf[:, 512 * i:512 * i + 512], sgu[:, sguoff:sguoff + 512],
                        sgu[:, sguoff + 512:sguoff + 1024], ALU.mult).then_inc(sems["vV"], 1))
                    freed[("sgu", i % 2)] = ("vV", vh)
                    freed[("h", i)] = ("vV", vh)
                if "hb" in taps and tt == 0:
                    wait("sync", "vV", p.val("vV"))
                    dma("sync", "dS", taps["hb"], hbuf[:])

                for hd in (range(NDC) if P6N >= 3 else ()):
                    qdone = []
                    for q in range(4):
                        bq8 = q % 2
                        st = freed.get(("wd8", bq8))
                        if st:
                            wait("sync", st[0], st[1])
                        vw = dma("sync", "dS", wd8[bq8][:],
                                 wdt_d[128 * (4 * hd + q):128 * (4 * hd + q) + 128, :])
                        stq = freed.get(("wdbq", q))
                        if stq:
                            wait("gpsimd", stq[0], stq[1])
                        wait("gpsimd", "dS", vw)
                        vg = p.inc("gG")
                        p.on("gpsimd", lambda E, q=q, bq8=bq8: E.tensor_scalar(
                            wdb[:, 2048 * q:2048 * (q + 1)], wd8[bq8][:], -128.0, None,
                            ALU.add).then_inc(sems["gG"], 1))
                        freed[("wd8", bq8)] = ("gG", vg)
                        qdone.append(vg)
                    bank = hd % 4
                    use_psS(bank)
                    for a in range(NIC):
                        if a % 16 == 0:
                            wait("tensor", "gG", qdone[a // 16])
                        if a == 0:
                            wait("tensor", "vV", freed[("h", NIC - 1)][1])
                        p.on("tensor", lambda E, a=a, bank=bank: E.matmul(
                            psS[bank][:], wdb[:, 128 * a:128 * a + 128],
                            hbuf[:, 512 * a:512 * a + 512],
                            start=(a == 0), stop=(a == NIC - 1)))
                    vmm = p.inc("tT")
                    p.on("tensor", ("INC_LAST", "tT"))
                    for q in range(4):
                        freed[("wdbq", q)] = ("tT", vmm)
                    wait("scalar", "tT", vmm)
                    vev = p.inc("aA")
                    p.on("scalar", lambda E, hd=hd, bank=bank: E.activation(
                        fo[:, hd * 512:(hd + 1) * 512], psS[bank][:], AF.Identity,
                        scale=dsc[:, hd:hd + 1]).then_inc(sems["aA"], 1))
                    bank_state[bank] = ("aA", vev)
                wait("tensor", "gG", p.val("gG"))
                if P6N < 4:
                    continue
                v_dn_all = p.val("tT")
                for hd in range(NDC):
                    wait("gpsimd", "aA", p.val("aA"))
                    wait("gpsimd", "tT", v_dn_all)
                    vg = p.inc("gG")
                    p.on("gpsimd", lambda E, hd=hd: E.tensor_copy(
                        fobf[:, hd * 512:(hd + 1) * 512],
                        fo[:, hd * 512:(hd + 1) * 512]).then_inc(sems["gG"], 1))
                wait("tensor", "gG", p.val("gG"))
                if P6N < 5:
                    continue

                def w5(a):
                    vv = p.inc("vV")
                    p.on("vector", lambda E, a=a: E.tensor_tensor(
                        fo[:, a * 512:(a + 1) * 512], fo[:, a * 512:(a + 1) * 512],
                        psB[:], ALU.mult).then_inc(sems["vV"], 1))

                def ln_plain5(xf, xbf, out_writer, sqbuf, cbl):
                    for a in range(NDC):
                        p.on("tensor", lambda E, a=a: E.matmul(
                            psL[:], onescol[:], xbf[:, a * 512:(a + 1) * 512],
                            start=(a == 0), stop=(a == NDC - 1)))
                    v_sum = p.inc("tT")
                    p.on("tensor", ("INC_LAST", "tT"))
                    wait("scalar", "tT", v_sum)
                    va = p.inc("aA")
                    p.on("scalar", lambda E: E.activation(
                        rowA[0:1, :], psL[:], AF.Identity,
                        scale=float(-1.0 / D)).then_inc(sems["aA"], 1))
                    release("psL", "aA", va)
                    wait_free("tensor", "psB")
                    wait("tensor", "aA", va)
                    vb = p.inc("tT")
                    p.on("tensor", lambda E: E.matmul(psB[:], onesrow[:], rowA[0:1, :],
                                                      start=True, stop=True).then_inc(sems["tT"], 1))
                    wait("vector", "tT", vb)
                    wait_free("tensor", "psO")
                    sq_rd = {}
                    for a in range(NDC):
                        vv = p.inc("vV")
                        p.on("vector", lambda E, a=a: E.tensor_tensor(
                            cbl[:, a * 512:(a + 1) * 512], xf[:, a * 512:(a + 1) * 512],
                            psB[:], ALU.add).then_inc(sems["vV"], 1))
                        wait("scalar", "vV", vv)
                        if a >= 2:
                            wait("scalar", "tT", sq_rd[a % 2])
                        vq = p.inc("aA")
                        p.on("scalar", lambda E, a=a: E.activation(
                            sqbuf[:, 512 * (a % 2):512 * (a % 2) + 512],
                            cbl[:, a * 512:(a + 1) * 512], AF.Square).then_inc(sems["aA"], 1))
                        wait("tensor", "aA", vq)
                        vt2 = p.inc("tT")
                        p.on("tensor", lambda E, a=a: E.matmul(
                            psO[0:1, :], onescol[:], sqbuf[:, 512 * (a % 2):512 * (a % 2) + 512],
                            start=(a == 0), stop=(a == NDC - 1)).then_inc(sems["tT"], 1))
                        sq_rd[a % 2] = vt2
                    v_cent = p.val("vV")
                    release("psB", "vV", v_cent)
                    vsq = p.val("tT")
                    wait("scalar", "tT", vsq)
                    p.on("scalar", lambda E: E.activation(rowB[0:1, :], psO[0:1, :], AF.Ln,
                                                          bias=epsv[0:1, 0:1], scale=float(1.0 / D)))
                    vr = p.inc("aA")
                    p.on("scalar", lambda E: E.activation(rowA[0:1, :], rowB[0:1, :], AF.Exp,
                                                          scale=-0.5).then_inc(sems["aA"], 1))
                    release("psO", "aA", vr)
                    wait_free("tensor", "psB")
                    wait("tensor", "aA", vr)
                    vb2 = p.inc("tT")
                    p.on("tensor", lambda E: E.matmul(psB[:], onesrow[:], rowA[0:1, :],
                                                      start=True, stop=True).then_inc(sems["tT"], 1))
                    wait("vector", "tT", vb2)
                    for a in range(NDC):
                        out_writer(a)
                    release("psB", "vV", p.val("vV"))

                ln_plain5(fo, fobf, w5, sqb4, fo)
                v_y5 = p.val("vV")
                if P6N < 6:
                    continue
                if P6N >= 7:
                    for a in range(NDC):
                        bx = a % 2
                        st = freed.get(("xr4", bx))
                        if st:
                            wait("sync", st[0], st[1])
                        if a < 2:
                            wait("sync", "dS", vsp)
                        vx1 = dma("sync", "dS", xr4[bx][:],
                                  x1_sp[128 * tt:128 * tt + 128, a * 512:(a + 1) * 512])
                        wait("vector", "dS", vx1)
                        vv1 = p.inc("vV")
                        p.on("vector", lambda E, a=a, bx=bx: E.tensor_tensor(
                            fo[:, a * 512:(a + 1) * 512], fo[:, a * 512:(a + 1) * 512],
                            xr4[bx][:], ALU.add).then_inc(sems["vV"], 1))
                        freed[("xr4", bx)] = ("vV", vv1)
                wait("sync", "vV", p.val("vV"))
                vst = dma("sync", "dS", out_d[128 * tt:128 * tt + 128, :], fo[:])
                freed[("fo",)] = ("dS", vst)
                if tt + 1 < NTT:
                    wait("scalar", "dS", vst)

            es4.close()

    wait("sync", "tT", p.val("tT"))
    wait("sync", "aA", p.val("aA"))
    wait("sync", "vV", p.val("vV"))
    wait("sync", "gG", p.val("gG"))
    wait("sync", "dS", p.val("dS"))
    wait("sync", "dG", p.val("dG"))

    global _LAST_PROG, _LAST_SEMS
    _LAST_PROG, _LAST_SEMS = p, sems
    with nc.Block() as block:
        for e in ENGINES:
            def runner(E, e=e):
                last = None
                for f in p.ops[e]:
                    if isinstance(f, tuple) and f[0] == "INC_LAST":
                        assert last is not None
                        last.then_inc(sems[f[1]], 1)
                        continue
                    r = f(E)
                    if r is not None:
                        last = r
            getattr(block, e)(runner)
    if PL >= 5:
        es.close()
    return nc


def _prep_inputs(inp):
    X = np.asarray(inp["L0_attn_X"])
    Wq = np.asarray(inp["L0_attn_Wq"]); bq = np.asarray(inp["L0_attn_bq"])
    Wk = np.asarray(inp["L0_attn_Wk"]); bk = np.asarray(inp["L0_attn_bk"])
    Wv = np.asarray(inp["L0_attn_Wv"]); bv = np.asarray(inp["L0_attn_bv"])
    Wo = np.asarray(inp["L0_attn_Wo"]); bo = np.asarray(inp["L0_attn_bo"])
    lng = np.asarray(inp["L0_attn_ln_g"]); lnb = np.asarray(inp["L0_attn_ln_b"])
    gw = np.asarray(inp["L0_ffn_gate_w"]); gs = np.asarray(inp["L0_ffn_gate_scale"])
    uw = np.asarray(inp["L0_ffn_up_w"]); us = np.asarray(inp["L0_ffn_up_scale"])
    dw = np.asarray(inp["L0_ffn_down_w"]); ds = np.asarray(inp["L0_ffn_down_scale"])

    def u8_of(a):
        return np.ascontiguousarray(a.view(np.uint8).reshape(a.shape + (4,))[..., 0])

    gw8 = u8_of(gw).reshape(NIC, 128, NDC, 128)
    wgt = np.ascontiguousarray(gw8.transpose(0, 3, 2, 1)).reshape(NIC * 128, NDC * 128)
    uw8 = u8_of(uw).reshape(NIC, 128, NDC, 128)
    wut = np.ascontiguousarray(uw8.transpose(0, 3, 2, 1)).reshape(NIC * 128, NDC * 128)
    dw8 = u8_of(dw).reshape(NDC, 128, 4, 16, 128)
    wdt = np.ascontiguousarray(dw8.transpose(0, 2, 4, 3, 1)).reshape(NDC * 4 * 128, 16 * 128)

    bf = ml_dtypes.bfloat16
    cm = np.zeros((128, 4 * 512), np.float32)
    kk = np.arange(128)[:, None]; qq = np.arange(512)[None, :]
    for r in range(4):
        cm[:, 512 * r:512 * r + 512] = (128 * r + kk <= qq)
    cmask = cm.astype(bf)
    onescol = np.ones((128, 1), np.float32).astype(bf)
    onesrow = np.ones((1, 128), np.float32)
    epsv = np.array([[1e-5, 1e-30]], np.float32)
    bo_l = np.ascontiguousarray(bo.reshape(NDC, 128).T)
    lng_l = np.ascontiguousarray(lng.reshape(NDC, 128).T)
    lnb_l = np.ascontiguousarray(lnb.reshape(NDC, 128).T)
    gs_l = np.ascontiguousarray(gs.reshape(NIC, 128).T)
    us_l = np.ascontiguousarray(us.reshape(NIC, 128).T)
    ds_l = np.ascontiguousarray(ds.reshape(NDC, 128).T)

    wo_t = np.ascontiguousarray(
        Wo.reshape(NDC, 128, NDC, 128).transpose(2, 1, 0, 3)).reshape(NDC * 128, NDC * 128)

    def qkv_tile(w, hsl):
        return np.ascontiguousarray(
            w[:, hsl].reshape(NDC, 128, 256).transpose(1, 0, 2)).reshape(128, NDC * 256)

    in_maps = []
    for c in range(NCORES):
        hsl = slice(256 * c, 256 * c + 256)
        bqk = np.zeros((128, 4), np.float32)
        bqk[:, 0:2] = bq[hsl].reshape(2, 128).T
        bqk[:, 2:4] = bk[hsl].reshape(2, 128).T
        Xc = X[0, TOK * c:TOK * (c + 1), :]
        xt_t = np.ascontiguousarray(
            Xc.reshape(NTT, 512, NDC, 128).transpose(0, 3, 2, 1)).reshape(NTT * 128, NDC * 512)
        in_maps.append(dict(
            xt=xt_t,
            wq=qkv_tile(Wq, hsl),
            wk=qkv_tile(Wk, hsl),
            wv=qkv_tile(Wv, hsl),
            wo=wo_t, bqk=bqk, bv=np.ascontiguousarray(bv[hsl])[None, :],
            bo=bo_l, lng=lng_l, lnb=lnb_l,
            wgt=wgt, wut=wut, wdt=wdt, gsc=gs_l, usc=us_l, dsc=ds_l,
            cmaskb=cmask, onescol=onescol, onesrow=onesrow, epsv=epsv,
        ))
    return in_maps


def kernel(**inp):
    in_maps = _prep_inputs(inp)
    nc = build_nc()
    res = run_bass_kernel_spmd(nc, in_maps, core_ids=list(range(NCORES)))
    cores = []
    for c in range(NCORES):
        o = np.asarray(res.results[c]["out"]).reshape(NTT, 128, NDC, 512)
        cores.append(o.transpose(0, 3, 2, 1).reshape(TOK, D))
    ret = np.ascontiguousarray(np.concatenate(cores, axis=0))[None]
    if _DEBUG_TAPS:
        kernel.debug_results = res.results
    return ret


# version 07
# speedup vs baseline: 2.2988x; 2.2988x over previous
import numpy as np
import ml_dtypes
import concourse.bass as bass
import concourse.mybir as mybir
from concourse.bass_utils import run_bass_kernel_spmd

F32 = mybir.dt.float32
BF16 = mybir.dt.bfloat16
U8 = mybir.dt.uint8
AF = mybir.ActivationFunctionType
ALU = mybir.AluOpType
AX = mybir.AxisListType

NCORES = 8
D = 2048
S = 8192
TOK = S // NCORES
NDC = D // 128
NTT = TOK // 512
HPC = 2
NQC = S // 512
NKC = S // 128
I_FFN = 8192
NIC = I_FFN // 128
ISQ = float(1.0 / np.sqrt(128.0))
ENGINES = ("sync", "tensor", "scalar", "vector", "gpsimd")

_LAST_PROG = None
_LAST_SEMS = None
_DETECT_RACES = True
_PHASE_LIMIT = 5
_P2_MODE = "all"
_P1_MIN = False
_P6N = 7
_DEBUG_TAPS = []


class Prog:

    def __init__(self):
        self.ops = {e: [] for e in ENGINES}
        self.cnt = {}

    def on(self, eng, fn):
        self.ops[eng].append(fn)

    def inc(self, sem, n=1):
        self.cnt[sem] = self.cnt.get(sem, 0) + n
        return self.cnt[sem]

    def val(self, sem):
        return self.cnt.get(sem, 0)


def build_nc():
    PL = _PHASE_LIMIT
    P2M = _P2_MODE
    P1MIN = _P1_MIN
    P6N = _P6N
    nc = bass.Bass("TRN2", target_bir_lowering=False,
                   detect_race_conditions=_DETECT_RACES)

    xt_d = nc.dram_tensor("xt", [NTT * 128, NDC * 512], F32, kind="ExternalInput").ap()
    wq_d = nc.dram_tensor("wq", [128, NDC * 256], F32, kind="ExternalInput").ap()
    wk_d = nc.dram_tensor("wk", [128, NDC * 256], F32, kind="ExternalInput").ap()
    wv_d = nc.dram_tensor("wv", [128, NDC * 256], F32, kind="ExternalInput").ap()
    wo_d = nc.dram_tensor("wo", [NDC * 128, NDC * 128], F32, kind="ExternalInput").ap()
    bqk_d = nc.dram_tensor("bqk", [128, 4], F32, kind="ExternalInput").ap()
    bv_d = nc.dram_tensor("bv", [1, 256], F32, kind="ExternalInput").ap()
    bo_d = nc.dram_tensor("bo", [128, NDC], F32, kind="ExternalInput").ap()
    lng_d = nc.dram_tensor("lng", [128, NDC], F32, kind="ExternalInput").ap()
    lnb_d = nc.dram_tensor("lnb", [128, NDC], F32, kind="ExternalInput").ap()
    wgt_d = nc.dram_tensor("wgt", [NIC * 128, NDC * 128], U8, kind="ExternalInput").ap()
    wut_d = nc.dram_tensor("wut", [NIC * 128, NDC * 128], U8, kind="ExternalInput").ap()
    wdt_d = nc.dram_tensor("wdt", [NDC * 4 * 128, 16 * 128], U8, kind="ExternalInput").ap()
    gsc_d = nc.dram_tensor("gsc", [128, NIC], F32, kind="ExternalInput").ap()
    usc_d = nc.dram_tensor("usc", [128, NIC], F32, kind="ExternalInput").ap()
    dsc_d = nc.dram_tensor("dsc", [128, NDC], F32, kind="ExternalInput").ap()
    cmask_d = nc.dram_tensor("cmaskb", [128, 4 * 512], BF16, kind="ExternalInput").ap()
    onescol_d = nc.dram_tensor("onescol", [128, 1], BF16, kind="ExternalInput").ap()
    onesrow_d = nc.dram_tensor("onesrow", [1, 128], F32, kind="ExternalInput").ap()
    eps_d = nc.dram_tensor("epsv", [1, 2], F32, kind="ExternalInput").ap()

    out_d = nc.dram_tensor("out", [NTT * 128, NDC * 512], F32, kind="ExternalOutput").ap()

    xn_con = nc.dram_tensor("xn_con", [128, NDC * TOK], BF16, kind="Internal").ap()
    xn_gat = nc.dram_tensor("xn_gat", [NCORES * 128, NDC * TOK], BF16, kind="Internal",
                            addr_space="Shared").ap()
    av_con = nc.dram_tensor("av_con", [128 * HPC, S], BF16, kind="Internal").ap()
    av_gat = nc.dram_tensor("av_gat", [D, S], BF16, kind="Internal",
                            addr_space="Shared").ap()
    x1_sp = nc.dram_tensor("x1_sp", [NTT * 128, NDC * 512], F32, kind="Internal").ap()

    taps = {}
    for t in _DEBUG_TAPS:
        shapes = {"xn": ([128, NDC * TOK], BF16), "qt": ([128, HPC * S], BF16),
                  "kt": ([128, HPC * S], BF16), "vt": ([128, HPC * S], BF16),
                  "avo": ([128, NDC * TOK], BF16), "x1": ([NTT * 128, NDC * 512], F32),
                  "xn2": ([128, NDC * TOK], BF16), "ao": ([128, NDC * 512], F32),
                  "negm": ([128, HPC * NQC], F32), "hb": ([128, NIC * 512], BF16)}
        sh, dt = shapes[t]
        taps[t] = nc.dram_tensor("tap_" + t, sh, dt, kind="ExternalOutput").ap()

    from contextlib import ExitStack
    es = ExitStack()
    sb = lambda n, s_, d: es.enter_context(nc.sbuf_tensor(n, s_, d))
    psb = lambda n, s_, d: es.enter_context(nc.psum_tensor(n, s_, d))

    sems = {}
    for name in ("dS", "dG", "tT", "aA", "vV", "gG", "cC"):
        sems[name] = es.enter_context(nc.semaphore(name))

    onescol = sb("onescol_s", [128, 1], BF16)
    onesrow = sb("onesrow_s", [1, 128], F32)
    epsv = sb("eps_s", [1, 2], F32)
    bqk = sb("bqk_s", [128, 4], F32)
    bo_s = sb("bo_s", [128, NDC], F32)
    lng = sb("lng_s", [128, NDC], F32)
    lnb = sb("lnb_s", [128, NDC], F32)
    gsc = sb("gsc_s", [128, NIC], F32)
    usc = sb("usc_s", [128, NIC], F32)
    dsc = sb("dsc_s", [128, NDC], F32)
    negm = sb("negm_s", [128, HPC * NQC], F32)
    vflush = sb("vflush_s", [1, 8], F32)
    gflush = sb("gflush_s", [1, 8], F32)
    rowA = sb("rowA_s", [1, 512], F32)
    rowB = sb("rowB_s", [1, 512], F32)
    mhatln = sb("mhatln_s", [1, NQC], F32)
    mxkln = sb("mxkln_s", [1, NQC], F32)
    mexp = sb("mexp_s", [128, NQC], F32)

    psS = [psb(f"psS{i}", [128, 512], F32) for i in range(4)]
    psO = psb("psO", [128, 512], F32)
    psL = psb("psL", [1, 512], F32)
    psB = psb("psB", [128, 512], F32)

    p = Prog()
    freed = {}

    def wait_free(eng, key):
        st = freed.get(key)
        if st is not None:
            sem, v = st
            p.on(eng, lambda E, s=sems[sem], vv=v: (E.wait_ge(s, vv), None)[1])

    def release(key, sem, val):
        freed[key] = (sem, val)

    def emit(eng, f):
        p.on(eng, f)

    def dma(eng, sem_name, out_ap, in_ap, **kw):
        v = p.inc(sem_name, 16)
        p.on(eng, lambda E, o=out_ap, i=in_ap, s=sems[sem_name], kw=kw:
             E.dma_start(out=o, in_=i, **kw).then_inc(s, 16))
        return v

    def flushV():
        v = p.inc("vV")
        p.on("vector", lambda E: E.memset(vflush[0:1, 0:1], 0.0).then_inc(sems["vV"], 1))
        return v

    def flushG():
        v = p.inc("gG")
        p.on("gpsimd", lambda E: E.memset(gflush[0:1, 0:1], 0.0).then_inc(sems["gG"], 1))
        return v

    def wait(eng, sem_name, val):
        if val > 0:
            p.on(eng, lambda E, s=sems[sem_name], v=val: (E.wait_ge(s, v), None)[1])

    for dst, src in ((onescol, onescol_d), (onesrow, onesrow_d), (epsv, eps_d),
                     (bqk, bqk_d), (bo_s, bo_d), (lng, lng_d),
                     (lnb, lnb_d), (gsc, gsc_d), (usc, usc_d), (dsc, dsc_d)):
        dma("sync", "dS", dst[:], src)
    v_consts = p.val("dS")

    es1 = ExitStack()
    sb1 = lambda n, s_, d: es1.enter_context(nc.sbuf_tensor(n, s_, d))
    xbuf = sb1("xbuf", [128, NDC * 512], F32)
    xbb = sb1("xbb", [128, NDC * 512], BF16)
    cb1 = sb1("cb1", [128, NDC * 512], F32)
    cb2 = sb1("cb2", [128, NDC * 512], F32)
    xnb = sb1("xnb", [128, NDC * TOK], BF16)
    sqb = sb1("sqb", [128, 2 * 512], BF16)

    def ln_plain(xf, xf_off, xbf, xbf_off, ncols, out_writer, aff=None):
        wait_free("tensor", "psL")
        for a in range(NDC):
            vs = p.inc("tT")
            p.on("tensor", lambda E, a=a: E.matmul(
                psL[:, 0:ncols], onescol[:],
                xbf[:, xbf_off + a * ncols: xbf_off + (a + 1) * ncols],
                start=(a == 0), stop=(a == NDC - 1)).then_inc(sems["tT"], 1))
        v_sum = p.val("tT")
        wait("scalar", "tT", v_sum)
        va = p.inc("aA")
        p.on("scalar", lambda E: E.activation(
            rowA[0:1, 0:ncols], psL[:, 0:ncols], AF.Identity,
            scale=float(-1.0 / D)).then_inc(sems["aA"], 1))
        release("psL", "aA", va)
        wait_free("tensor", "psB")
        wait("tensor", "aA", va)
        vb = p.inc("tT")
        p.on("tensor", lambda E: E.matmul(
            psB[:, 0:ncols], onesrow[:], rowA[0:1, 0:ncols],
            start=True, stop=True).then_inc(sems["tT"], 1))
        wait("vector", "tT", vb)
        wait_free("tensor", "psO")
        sq_rd = {}
        for a in range(NDC):
            vv = p.inc("vV")
            p.on("vector", lambda E, a=a: E.tensor_tensor(
                cb1[:, a * ncols:(a + 1) * ncols],
                xf[:, xf_off + a * ncols: xf_off + (a + 1) * ncols],
                psB[:, 0:ncols], ALU.add).then_inc(sems["vV"], 1))
            wait("scalar", "vV", vv)
            if a >= 2:
                wait("scalar", "tT", sq_rd[a % 2])
            vq = p.inc("aA")
            p.on("scalar", lambda E, a=a: E.activation(
                sqb[:, 512 * (a % 2):512 * (a % 2) + ncols],
                cb1[:, a * ncols:(a + 1) * ncols], AF.Square).then_inc(sems["aA"], 1))
            wait("tensor", "aA", vq)
            vt2 = p.inc("tT")
            p.on("tensor", lambda E, a=a: E.matmul(
                psO[0:1, 0:ncols], onescol[:],
                sqb[:, 512 * (a % 2):512 * (a % 2) + ncols],
                start=(a == 0), stop=(a == NDC - 1)).then_inc(sems["tT"], 1))
            sq_rd[a % 2] = vt2
        v_cent = p.val("vV")
        v_sq = p.val("tT")
        release("psB", "vV", v_cent)
        wait("scalar", "tT", v_sq)
        p.on("scalar", lambda E: E.activation(
            rowB[0:1, 0:ncols], psO[0:1, 0:ncols], AF.Ln,
            bias=epsv[0:1, 0:1], scale=float(1.0 / D)))
        vr = p.inc("aA")
        p.on("scalar", lambda E: E.activation(
            rowA[0:1, 0:ncols], rowB[0:1, 0:ncols], AF.Exp,
            scale=-0.5).then_inc(sems["aA"], 1))
        release("psO", "aA", vr)
        wait_free("tensor", "psB")
        wait("tensor", "aA", vr)
        vb2 = p.inc("tT")
        p.on("tensor", lambda E: E.matmul(
            psB[:, 0:ncols], onesrow[:], rowA[0:1, 0:ncols],
            start=True, stop=True).then_inc(sems["tT"], 1))
        wait("vector", "tT", vb2)
        for a in range(NDC):
            out_writer(a)
        vfin = p.val("vV")
        release("psB", "vV", vfin)
        return vfin

    v_xt_tiles = []
    for tt in range(NTT):
        if tt > 0:
            wait("sync", "vV", p.val("vV"))
            wait("sync", "tT", p.val("tT"))
            wait("gpsimd", "vV", p.val("vV"))
            wait("gpsimd", "tT", p.val("tT"))
        src = xt_d[128 * tt:128 * tt + 128, :]
        vx = dma("sync", "dS", xbuf[:], src)
        vxb = dma("gpsimd", "dG", xbb[:], src)
        wait("tensor", "dS", vx)
        wait("tensor", "dG", vxb)
        wait("vector", "dS", vx)
        wait("vector", "dG", vxb)
        wait("scalar", "dS", vx)

        if P1MIN:
            for a in range(NDC):
                vv = p.inc("vV")
                p.on("vector", lambda E, a=a, tt=tt: E.tensor_copy(
                    xnb[:, a * TOK + 512 * tt: a * TOK + 512 * tt + 512],
                    xbb[:, a * 512:(a + 1) * 512]).then_inc(sems["vV"], 1))
            v_xt_tiles.append(p.val("vV"))
            continue
        def w1(a):
            vv = p.inc("vV")
            p.on("vector", lambda E, a=a: E.tensor_tensor(
                cb2[:, a * 512:(a + 1) * 512], cb1[:, a * 512:(a + 1) * 512],
                psB[:, 0:512], ALU.mult).then_inc(sems["vV"], 1))
            wait("gpsimd", "vV", vv)
            vg = p.inc("gG")
            p.on("gpsimd", lambda E, a=a: E.tensor_copy(
                xbb[:, a * 512:(a + 1) * 512],
                cb2[:, a * 512:(a + 1) * 512]).then_inc(sems["gG"], 1))
        ln_plain(xbuf, 0, xbb, 0, 512, w1)
        v_y1bf = flushG()
        wait("tensor", "gG", v_y1bf)

        def w2(a):
            vv0 = p.inc("vV")
            p.on("vector", lambda E, a=a: E.tensor_tensor(
                cb1[:, a * 512:(a + 1) * 512], cb1[:, a * 512:(a + 1) * 512],
                psB[:, 0:512], ALU.mult).then_inc(sems["vV"], 1))
            vv1 = p.inc("vV")
            p.on("vector", lambda E, a=a, tt=tt: E.tensor_scalar(
                xnb[:, a * TOK + 512 * tt: a * TOK + 512 * tt + 512],
                cb1[:, a * 512:(a + 1) * 512],
                lng[:, a:a + 1], lnb[:, a:a + 1],
                ALU.mult, ALU.add).then_inc(sems["vV"], 1))
        ln_plain(cb2, 0, xbb, 0, 512, w2)
        v_xn_tt = p.val("vV")
        v_xt_tiles.append(v_xn_tt)

    wait("sync", "vV", flushV())
    v_con = dma("sync", "dS", xn_con, xnb[:])
    if "xn" in taps:
        dma("sync", "dS", taps["xn"], xnb[:])
    wait("gpsimd", "dS", v_con)
    vcc1 = p.inc("cC")
    p.on("gpsimd", lambda E: E.collective_compute(
        "AllGather", ALU.bypass, replica_groups=[list(range(NCORES))],
        ins=[xn_con], outs=[xn_gat],
    ).then_inc(sems["cC"], 1))

    es1.close()

    if PL >= 2:
            es2 = ExitStack()
            sb2 = lambda n, s_, d: es2.enter_context(nc.sbuf_tensor(n, s_, d))
            wqb = sb2("wqb", [128, NDC * 256], BF16)
            wkb = sb2("wkb", [128, NDC * 256], BF16)
            wvb = sb2("wvb", [128, NDC * 256], BF16)
            xg = [sb2(f"xg{i}", [128, NDC * TOK], BF16) for i in range(2)]
            qt = sb2("qt", [128, HPC * S], BF16)
            kt = sb2("kt", [128, HPC * S], BF16)
            vt = sb2("vt", [128, HPC * S], BF16)
            pbuf = sb2("pbuf", [128, 4 * 512], BF16)
            avst = sb2("avst", [128, 512], BF16)
            sqn = sb2("sqn", [128, 2 * 512], BF16)
            nrow = sb2("nrow", [1, 2 * 512], F32)
            mkparts = sb2("mkparts", [1, NQC], F32)
            cmask = sb2("cmask_s", [128, 4 * 512], BF16)
            bvrow = sb2("bvrow_s", [1, 256], F32)
            bvb = sb2("bvb_s", [128, 256], F32)
            dma("sync", "dS", cmask[:], cmask_d)
            v_bv = dma("sync", "dS", bvrow[:], bv_d)
            wait_free("tensor", "psB")
            wait("tensor", "dS", v_bv)
            vbv = p.inc("tT")
            p.on("tensor", lambda E: E.matmul(psB[:, 0:256], onesrow[:], bvrow[:],
                                              start=True, stop=True).then_inc(sems["tT"], 1))
            wait("scalar", "tT", vbv)
            vbv2 = p.inc("aA")
            p.on("scalar", lambda E: E.activation(bvb[:], psB[:, 0:256],
                                                  AF.Identity).then_inc(sems["aA"], 1))
            release("psB", "aA", vbv2)

            if P2M == "cc":
                wait("gpsimd", "cC", vcc1)
            for dst, src in (((wqb, wq_d), (wkb, wk_d), (wvb, wv_d)) if P2M not in ("xg", "cc") else ()):
                wait("gpsimd", "dS", v_con)
                dma("gpsimd", "dG", dst[:], src)
            v_wqkv = p.val("dG")

            bank_state = {}

            def use_psS(i, eng="tensor"):
                st = bank_state.get(i)
                if st:
                    wait(eng, st[0], st[1])

            xg_free = [None, None]
            for r in (range(NCORES) if P2M not in ("w", "cc") else ()):
                b = r % 2
                if xg_free[b]:
                    wait("sync", xg_free[b][0], xg_free[b][1])
                wait("sync", "cC", vcc1)
                v = dma("sync", "dS", xg[b][:], xn_gat[128 * r:128 * (r + 1), :])
                wait("tensor", "dS", v)
                if r == 0:
                    wait("tensor", "dG", v_wqkv)
                for half in range(2):
                    g = 2 * r + half
                    xoff = 512 * half
                    for (wb, dstbuf) in (((wqb, qt), (wkb, kt)) if P2M in ("qk", "all") else ()):
                        for h in range(2):
                            bank = (2 * g + h) % 4
                            use_psS(bank)
                            for a in range(NDC):
                                p.on("tensor", lambda E, a=a, h=h, wb=wb, bank=bank, b=b, xoff=xoff: E.matmul(
                                    psS[bank][:], wb[:, a * 256 + 128 * h: a * 256 + 128 * h + 128],
                                    xg[b][:, a * TOK + xoff: a * TOK + xoff + 512],
                                    start=(a == 0), stop=(a == NDC - 1)))
                            vt_mm = p.inc("tT")
                            p.on("tensor", ("INC_LAST", "tT"))
                            wait("scalar", "tT", vt_mm)
                            bcol = (0 if dstbuf is qt else 2) + h
                            va2 = p.inc("aA")
                            p.on("scalar", lambda E, bank=bank, h=h, g=g, dstbuf=dstbuf, bcol=bcol:
                                 E.activation(dstbuf[:, h * S + 512 * g: h * S + 512 * g + 512],
                                              psS[bank][:], AF.Identity,
                                              bias=bqk[:, bcol:bcol + 1]).then_inc(sems["aA"], 1))
                            bank_state[bank] = ("aA", va2)
                    for sub in (range(4) if P2M in ("v", "all") else ()):
                        bank = sub % 4
                        use_psS(bank)
                        for a in range(NDC):
                            p.on("tensor", lambda E, a=a, sub=sub, bank=bank, b=b, xoff=xoff: E.matmul(
                                psS[bank][:, 0:256],
                                xg[b][:, a * TOK + xoff + 128 * sub: a * TOK + xoff + 128 * sub + 128],
                                wvb[:, a * 256:(a + 1) * 256],
                                start=(a == 0), stop=(a == NDC - 1)))
                        vt_mm = p.inc("tT")
                        p.on("tensor", ("INC_LAST", "tT"))
                        wait("vector", "tT", vt_mm)
                        if g == 0 and sub == 0:
                            wait("vector", "aA", vbv2)
                        kc = 4 * g + sub
                        vv2 = p.inc("vV")
                        p.on("vector", lambda E, bank=bank, kc=kc: E.tensor_tensor(
                            vt[:, 256 * kc:256 * kc + 256], psS[bank][:, 0:256], bvb[:],
                            ALU.add).then_inc(sems["vV"], 1))
                        bank_state[bank] = ("vV", vv2)
                xg_free[b] = ("tT", p.val("tT"))

            if "qt" in taps:
                wait("sync", "aA", p.val("aA"))
                dma("sync", "dS", taps["qt"], qt[:])
                dma("sync", "dS", taps["kt"], kt[:])
            if "vt" in taps:
                wait("sync", "vV", p.val("vV"))
                dma("sync", "dS", taps["vt"], vt[:])

    if PL >= 3:
            v_qk_done = p.val("aA")
            v_prev_negm = 0
            sqn_rd = {}
            for h in range(HPC):
                nrow_free = [None, None]
                for t in range(2 * (S // 512)):
                    srcbuf, col = (qt, 512 * t) if t < S // 512 else (kt, 512 * (t - S // 512))
                    col += h * S
                    wait("scalar", "aA", v_qk_done)
                    if h > 0 and t == 0:
                        wait("scalar", "vV", v_prev_negm)
                    if sqn_rd.get(t % 2):
                        wait("scalar", "tT", sqn_rd[t % 2])
                    vq = p.inc("aA")
                    p.on("scalar", lambda E, srcbuf=srcbuf, col=col, t=t: E.activation(
                        sqn[:, 512 * (t % 2):512 * (t % 2) + 512],
                        srcbuf[:, col:col + 512], AF.Square).then_inc(sems["aA"], 1))
                    wait("tensor", "aA", vq)
                    use_psS(t % 2)
                    vm = p.inc("tT")
                    p.on("tensor", lambda E, t=t: E.matmul(
                        psS[t % 2][0:1, :], onescol[:],
                        sqn[:, 512 * (t % 2):512 * (t % 2) + 512],
                        start=True, stop=True).then_inc(sems["tT"], 1))
                    sqn_rd[t % 2] = vm
                    wait("scalar", "tT", vm)
                    if nrow_free[t % 2]:
                        wait("scalar", *nrow_free[t % 2])
                    sc = 1.0 if t < S // 512 else (1.0 / 128.0)
                    va3 = p.inc("aA")
                    p.on("scalar", lambda E, sc=sc, t=t: E.activation(
                        nrow[0:1, 512 * (t % 2):512 * (t % 2) + 512], psS[t % 2][0:1, :],
                        AF.Ln, bias=epsv[0:1, 1:2], scale=sc).then_inc(sems["aA"], 1))
                    bank_state[t % 2] = ("aA", va3)
                    wait("vector", "aA", va3)
                    vmx_t = p.inc("vV")
                    dst, colq = (mhatln, t) if t < S // 512 else (mkparts, t - S // 512)
                    p.on("vector", lambda E, dst=dst, colq=colq, t=t: E.reduce_max(
                        dst[0:1, colq:colq + 1],
                        nrow[0:1, 512 * (t % 2):512 * (t % 2) + 512],
                        axis=AX.X).then_inc(sems["vV"], 1))
                    nrow_free[t % 2] = ("vV", vmx_t)
                for qc in range(NQC):
                    p.on("vector", lambda E, qc=qc: E.reduce_max(
                        mxkln[0:1, qc:qc + 1], mkparts[0:1, :], axis=AX.X))
                vmx = p.inc("vV")
                p.on("vector", ("INC_LAST", "vV"))
                wait_free("tensor", "psB")
                wait("tensor", "vV", vmx)
                p.on("tensor", lambda E: E.matmul(psB[:, 0:NQC], onesrow[:], mhatln[:],
                                                  start=True, stop=False))
                vb3 = p.inc("tT")
                p.on("tensor", lambda E: E.matmul(psB[:, 0:NQC], onesrow[:], mxkln[:],
                                                  start=False, stop=True).then_inc(sems["tT"], 1))
                wait("scalar", "tT", vb3)
                vme = p.inc("aA")
                p.on("scalar", lambda E: E.activation(mexp[:], psB[:, 0:NQC], AF.Exp,
                                                      scale=0.5).then_inc(sems["aA"], 1))
                release("psB", "aA", vme)
                wait("vector", "aA", vme)
                vng = p.inc("vV")
                p.on("vector", lambda E, h=h: E.tensor_scalar_mul(
                    negm[:, NQC * h:NQC * (h + 1)], mexp[:], -1.0).then_inc(sems["vV"], 1))
                v_prev_negm = vng
            v_negm = p.val("vV")
            if "negm" in taps:
                wait("sync", "vV", v_negm)
                dma("sync", "dS", taps["negm"], negm[:])

            DEPTH = 4
            v_av_con = []
            for h in range(HPC):
                for qc in range(NQC):
                    J = 4 * (qc + 1)
                    qof = h * S + 512 * qc
                    sdone = {}
                    expdone = {}
                    maskdone = {}
                    avdone = {}

                    def emit_S(j):
                        bank = j % DEPTH
                        use_psS(bank)
                        if j < DEPTH and qc == 0 and h == 0:
                            wait("tensor", "vV", v_negm)
                        vs = p.inc("tT")
                        p.on("tensor", lambda E, j=j, bank=bank, qof=qof, h=h: E.matmul(
                            psS[bank][:], kt[:, h * S + 128 * j: h * S + 128 * j + 128],
                            qt[:, qof:qof + 512], start=True, stop=True).then_inc(sems["tT"], 1))
                        sdone[j] = vs

                    for j in range(min(DEPTH, J)):
                        emit_S(j)
                    for j in range(J):
                        bank = j % DEPTH
                        wait("scalar", "tT", sdone[j])
                        if j - DEPTH >= 0:
                            wait("scalar", "tT", avdone[j - DEPTH])
                        ve = p.inc("aA")
                        p.on("scalar", lambda E, j=j, bank=bank, h=h, qc=qc: E.activation(
                            pbuf[:, 512 * (j % DEPTH):512 * (j % DEPTH) + 512],
                            psS[bank][:], AF.Exp,
                            bias=negm[:, NQC * h + qc: NQC * h + qc + 1],
                            scale=ISQ).then_inc(sems["aA"], 1))
                        expdone[j] = ve
                        bank_state[bank] = ("aA", ve)
                        r = j - 4 * qc
                        if r >= 0:
                            wait("vector", "aA", ve)
                            vmk = p.inc("vV")
                            p.on("vector", lambda E, j=j, r=r: E.tensor_tensor(
                                pbuf[:, 512 * (j % DEPTH):512 * (j % DEPTH) + 512],
                                pbuf[:, 512 * (j % DEPTH):512 * (j % DEPTH) + 512],
                                cmask[:, 512 * r:512 * r + 512], ALU.mult).then_inc(sems["vV"], 1))
                            maskdone[j] = vmk
                            wait("tensor", "vV", vmk)
                        else:
                            wait("tensor", "aA", ve)
                        if j == 0:
                            wait_free("tensor", "psL")
                            wait_free("tensor", "psO")
                        p.on("tensor", lambda E, j=j, first=(j == 0), last=(j == J - 1): E.matmul(
                            psL[:], onescol[:], pbuf[:, 512 * (j % DEPTH):512 * (j % DEPTH) + 512],
                            start=first, stop=last))
                        vav = p.inc("tT")
                        p.on("tensor", lambda E, j=j, h=h, first=(j == 0), last=(j == J - 1): E.matmul(
                            psO[:], vt[:, 256 * j + 128 * h: 256 * j + 128 * h + 128],
                            pbuf[:, 512 * (j % DEPTH):512 * (j % DEPTH) + 512],
                            start=first, stop=last).then_inc(sems["tT"], 1))
                        avdone[j] = vav
                        if j + DEPTH < J:
                            emit_S(j + DEPTH)
                    wait("scalar", "tT", avdone[J - 1])
                    p.on("scalar", lambda E: E.activation(rowB[0:1, :], psL[:], AF.Ln))
                    vrl = p.inc("aA")
                    p.on("scalar", lambda E: E.activation(rowA[0:1, :], rowB[0:1, :], AF.Exp,
                                                          scale=-1.0).then_inc(sems["aA"], 1))
                    release("psL", "aA", vrl)
                    wait_free("tensor", "psB")
                    wait("tensor", "aA", vrl)
                    vbc = p.inc("tT")
                    p.on("tensor", lambda E: E.matmul(psB[:], onesrow[:], rowA[0:1, :],
                                                      start=True, stop=True).then_inc(sems["tT"], 1))
                    wait("vector", "tT", vbc)
                    p.on("vector", lambda E: E.tensor_copy(sqn.ap().bitcast(F32)[:, 0:512], psB[:]))
                    vfin = p.inc("vV")
                    p.on("vector", lambda E: E.tensor_tensor(
                        avst[:], psO[:], sqn.ap().bitcast(F32)[:, 0:512],
                        ALU.mult).then_inc(sems["vV"], 1))
                    release("psB", "vV", vfin)
                    release("psO", "vV", vfin)
                    wait("sync", "vV", flushV())
                    vdc = dma("sync", "dS", av_con[128 * h:128 * h + 128, 512 * qc:512 * qc + 512],
                              avst[:])
                    v_av_con.append(vdc)
                    wait("vector", "dS", vdc)

            wait("gpsimd", "dS", max(v_av_con))
            vcc2 = p.inc("cC")
            p.on("gpsimd", lambda E: E.collective_compute(
                "AllGather", ALU.bypass, replica_groups=[list(range(NCORES))],
                ins=[av_con], outs=[av_gat]).then_inc(sems["cC"], 1))

            es2.close()
    if PL >= 4:

            xn2 = es.enter_context(nc.sbuf_tensor("xn2", [128, NDC * TOK], BF16))
            es3 = ExitStack()
            sb3 = lambda n, s_, d: es3.enter_context(nc.sbuf_tensor(n, s_, d))
            avo = sb3("avo", [128, NDC * TOK], BF16)
            wob = [sb3(f"wob{i}", [128, NDC * 128], BF16) for i in range(2)]
            ao = sb3("ao", [128, NDC * 512], F32)
            aobf = sb3("aobf", [128, NDC * 512], BF16)
            cb = sb3("cb", [128, NDC * 512], F32)
            xre = [sb3(f"xre{i}", [128, 512], F32) for i in range(2)]
            sqb3 = sb3("sqb3", [128, 2 * 512], BF16)

            cell = {}
            wait("gpsimd", "cC", vcc2)
            def _getcid(E):
                cell["cid"] = E.partition_id()
            p.on("gpsimd", _getcid)
            vavo = p.inc("dG", 16 * NDC)
            def _avoload(E):
                for j in range(NDC):
                    E.dma_start(out=avo[:, j * TOK:(j + 1) * TOK],
                                in_=av_gat[128 * j:128 * (j + 1), bass.ts(cell["cid"], TOK)]
                                ).then_inc(sems["dG"], 16)
            p.on("gpsimd", _avoload)

            def ln_plain3(xf, xbf, out_writer, sqbuf):
                for a in range(NDC):
                    p.on("tensor", lambda E, a=a: E.matmul(
                        psL[:], onescol[:], xbf[:, a * 512:(a + 1) * 512],
                        start=(a == 0), stop=(a == NDC - 1)))
                v_sum = p.inc("tT")
                p.on("tensor", ("INC_LAST", "tT"))
                wait("scalar", "tT", v_sum)
                va = p.inc("aA")
                p.on("scalar", lambda E: E.activation(
                    rowA[0:1, :], psL[:], AF.Identity,
                    scale=float(-1.0 / D)).then_inc(sems["aA"], 1))
                release("psL", "aA", va)
                wait_free("tensor", "psB")
                wait("tensor", "aA", va)
                vb = p.inc("tT")
                p.on("tensor", lambda E: E.matmul(psB[:], onesrow[:], rowA[0:1, :],
                                                  start=True, stop=True).then_inc(sems["tT"], 1))
                wait("vector", "tT", vb)
                wait_free("tensor", "psO")
                sq_rd = {}
                for a in range(NDC):
                    vv = p.inc("vV")
                    p.on("vector", lambda E, a=a: E.tensor_tensor(
                        cb[:, a * 512:(a + 1) * 512], xf[:, a * 512:(a + 1) * 512],
                        psB[:], ALU.add).then_inc(sems["vV"], 1))
                    wait("scalar", "vV", vv)
                    if a >= 2:
                        wait("scalar", "tT", sq_rd[a % 2])
                    vq = p.inc("aA")
                    p.on("scalar", lambda E, a=a: E.activation(
                        sqbuf[:, 512 * (a % 2):512 * (a % 2) + 512],
                        cb[:, a * 512:(a + 1) * 512], AF.Square).then_inc(sems["aA"], 1))
                    wait("tensor", "aA", vq)
                    vt2 = p.inc("tT")
                    p.on("tensor", lambda E, a=a: E.matmul(
                        psO[0:1, :], onescol[:], sqbuf[:, 512 * (a % 2):512 * (a % 2) + 512],
                        start=(a == 0), stop=(a == NDC - 1)).then_inc(sems["tT"], 1))
                    sq_rd[a % 2] = vt2
                v_cent = p.val("vV")
                release("psB", "vV", v_cent)
                vsq = p.val("tT")
                wait("scalar", "tT", vsq)
                p.on("scalar", lambda E: E.activation(rowB[0:1, :], psO[0:1, :], AF.Ln,
                                                      bias=epsv[0:1, 0:1], scale=float(1.0 / D)))
                vr = p.inc("aA")
                p.on("scalar", lambda E: E.activation(rowA[0:1, :], rowB[0:1, :], AF.Exp,
                                                      scale=-0.5).then_inc(sems["aA"], 1))
                release("psO", "aA", vr)
                wait_free("tensor", "psB")
                wait("tensor", "aA", vr)
                vb2 = p.inc("tT")
                p.on("tensor", lambda E: E.matmul(psB[:], onesrow[:], rowA[0:1, :],
                                                  start=True, stop=True).then_inc(sems["tT"], 1))
                wait("vector", "tT", vb2)
                for a in range(NDC):
                    out_writer(a)
                release("psB", "vV", p.val("vV"))

            for tt in range(NTT):
                for ch in range(NDC):
                    bw = ch % 2
                    st = freed.get(("wob", bw))
                    if st:
                        wait("gpsimd", st[0], st[1])
                    if tt == 0 and ch < 2:
                        wait("gpsimd", "dS", max(v_av_con))
                    vw = dma("gpsimd", "dG", wob[bw][:],
                             wo_d[128 * ch:128 * ch + 128, :])
                    wait("tensor", "dG", max(vw, vavo))
                    bank = ch % 4
                    use_psS(bank)
                    for a in range(NDC):
                        p.on("tensor", lambda E, a=a, bw=bw, tt=tt, bank=bank: E.matmul(
                            psS[bank][:],
                            wob[bw][:, 128 * a:128 * a + 128],
                            avo[:, a * TOK + 512 * tt: a * TOK + 512 * tt + 512],
                            start=(a == 0), stop=(a == NDC - 1)))
                    vmm = p.inc("tT")
                    p.on("tensor", ("INC_LAST", "tT"))
                    freed[("wob", bw)] = ("tT", vmm)
                    wait("scalar", "tT", vmm)
                    vev = p.inc("aA")
                    p.on("scalar", lambda E, ch=ch, bank=bank: E.activation(
                        ao[:, ch * 512:(ch + 1) * 512], psS[bank][:], AF.Identity,
                        bias=bo_s[:, ch:ch + 1]).then_inc(sems["aA"], 1))
                    bank_state[bank] = ("aA", vev)
                    wait("gpsimd", "aA", vev)
                    vg = p.inc("gG")
                    p.on("gpsimd", lambda E, ch=ch: E.tensor_copy(
                        aobf[:, ch * 512:(ch + 1) * 512],
                        ao[:, ch * 512:(ch + 1) * 512]).then_inc(sems["gG"], 1))
                if "ao" in taps and tt == 0:
                    wait("sync", "aA", p.val("aA"))
                    dma("sync", "dS", taps["ao"], ao[:])
                wait("tensor", "gG", flushG())

                def w3(a):
                    vv = p.inc("vV")
                    p.on("vector", lambda E, a=a: E.tensor_tensor(
                        ao[:, a * 512:(a + 1) * 512], cb[:, a * 512:(a + 1) * 512],
                        psB[:], ALU.mult).then_inc(sems["vV"], 1))
                ln_plain3(ao, aobf, w3, sqb3)
                for a in range(NDC):
                    bx = a % 2
                    st = freed.get(("xre", bx))
                    if st:
                        wait("sync", st[0], st[1])
                    vx = dma("sync", "dS", xre[bx][:],
                             xt_d[128 * tt:128 * tt + 128, a * 512:(a + 1) * 512])
                    wait("vector", "dS", vx)
                    vv = p.inc("vV")
                    p.on("vector", lambda E, a=a, bx=bx: E.tensor_tensor(
                        ao[:, a * 512:(a + 1) * 512], ao[:, a * 512:(a + 1) * 512],
                        xre[bx][:], ALU.add).then_inc(sems["vV"], 1))
                    freed[("xre", bx)] = ("vV", vv + 1)
                    wait("gpsimd", "vV", vv)
                    vg = p.inc("gG")
                    p.on("gpsimd", lambda E, a=a: E.tensor_copy(
                        aobf[:, a * 512:(a + 1) * 512],
                        ao[:, a * 512:(a + 1) * 512]).then_inc(sems["gG"], 1))
                v_x1 = flushV()
                wait("sync", "vV", v_x1)
                vsp = dma("sync", "dS", x1_sp[128 * tt:128 * tt + 128, :], ao[:])
                if "x1" in taps:
                    dma("sync", "dS", taps["x1"][128 * tt:128 * tt + 128, :], ao[:])
                wait("tensor", "gG", flushG())
                def w4(a):
                    vv = p.inc("vV")
                    p.on("vector", lambda E, a=a, tt=tt: E.tensor_tensor(
                        xn2[:, a * TOK + 512 * tt: a * TOK + 512 * tt + 512],
                        cb[:, a * 512:(a + 1) * 512], psB[:], ALU.mult).then_inc(sems["vV"], 1))
                ln_plain3(ao, aobf, w4, sqb3)
                wait("tensor", "vV", p.val("vV"))
            v_xn2 = p.val("vV")
            if "xn2" in taps:
                wait("sync", "vV", v_xn2)
                dma("sync", "dS", taps["xn2"], xn2[:])
            if "avo" in taps:
                dma("sync", "dS", taps["avo"], avo[:])
            es3.close()
    if PL >= 5:

            es4 = ExitStack()
            sb4 = lambda n, s_, d: es4.enter_context(nc.sbuf_tensor(n, s_, d))
            hbuf = sb4("hbuf", [128, NIC * 512], BF16)
            wgb = [sb4(f"wgb{i}", [128, NDC * 128], BF16) for i in range(2)]
            wub = [sb4(f"wub{i}", [128, NDC * 128], BF16) for i in range(2)]
            wdb = sb4("wdb", [128, 5 * 2048], BF16)
            sgu = sb4("sgu", [128, 4 * 512], F32)
            fo = sb4("fo", [128, NDC * 512], F32)
            sqb4 = sb4("sqb4", [128, 2 * 512], BF16)
            fobf = sb4("fobf", [128, NDC * 512], BF16)
            xr4 = [sb4(f"xr4{i}", [128, 512], F32) for i in range(2)]

            for tt in range(NTT):
                for i in range(NIC):
                    bw = i % 2
                    for (wbs, wdrm) in ((wgb, wgt_d), (wub, wut_d)):
                        st = freed.get(("wb", id(wbs), bw))
                        if st:
                            wait("gpsimd", st[0], st[1])
                        if tt == 0 and i < 2:
                            wait("gpsimd", "vV", v_xn2)
                        vw = dma("gpsimd", "dG", wbs[bw][:],
                                 wdrm[128 * i:128 * i + 128, :])
                        wait("vector", "dG", vw)
                        vg = p.inc("vV")
                        p.on("vector", lambda E, wbs=wbs, bw=bw: E.tensor_scalar(
                            wbs[bw][:], wbs[bw][:], -128.0, None, ALU.add).then_inc(sems["vV"], 1))
                        freed[("wconv", id(wbs), bw)] = vg
                    if P6N < 2:
                        continue
                    bank_g = (2 * i) % 4
                    bank_u = (2 * i + 1) % 4
                    wait("tensor", "vV", freed[("wconv", id(wgb), bw)])
                    wait("tensor", "vV", freed[("wconv", id(wub), bw)])
                    for (bk, wbs) in ((bank_g, wgb), (bank_u, wub)):
                        use_psS(bk)
                        for a in range(NDC):
                            p.on("tensor", lambda E, a=a, bk=bk, wbs=wbs, bw=bw, tt=tt: E.matmul(
                                psS[bk][:], wbs[bw][:, 128 * a:128 * a + 128],
                                xn2[:, a * TOK + 512 * tt: a * TOK + 512 * tt + 512],
                                start=(a == 0), stop=(a == NDC - 1)))
                    vmm = p.inc("tT")
                    p.on("tensor", ("INC_LAST", "tT"))
                    freed[("wb", id(wgb), bw)] = ("tT", vmm)
                    freed[("wb", id(wub), bw)] = ("tT", vmm)
                    sguoff = 1024 * (i % 2)
                    st3 = freed.get(("sgu", i % 2))
                    if st3:
                        wait("scalar", st3[0], st3[1])
                    wait("scalar", "tT", vmm)
                    p.on("scalar", lambda E, bank_g=bank_g, i=i, sguoff=sguoff: E.activation(
                        sgu[:, sguoff:sguoff + 512], psS[bank_g][:], AF.Silu,
                        scale=gsc[:, i:i + 1]))
                    vev = p.inc("aA")
                    p.on("scalar", lambda E, bank_u=bank_u, i=i, sguoff=sguoff: E.activation(
                        sgu[:, sguoff + 512:sguoff + 1024], psS[bank_u][:], AF.Identity,
                        scale=usc[:, i:i + 1]).then_inc(sems["aA"], 1))
                    bank_state[bank_g] = ("aA", vev)
                    bank_state[bank_u] = ("aA", vev)
                    wait("vector", "aA", vev)
                    vh = p.inc("vV")
                    p.on("vector", lambda E, i=i, sguoff=sguoff: E.tensor_tensor(
                        hbuf[:, 512 * i:512 * i + 512], sgu[:, sguoff:sguoff + 512],
                        sgu[:, sguoff + 512:sguoff + 1024], ALU.mult).then_inc(sems["vV"], 1))
                    freed[("sgu", i % 2)] = ("vV", vh + 1)
                    freed[("h", i)] = ("vV", vh)
                if "hb" in taps and tt == 0:
                    wait("sync", "vV", p.val("vV"))
                    dma("sync", "dS", taps["hb"], hbuf[:])

                for hd in (range(NDC) if P6N >= 3 else ()):
                    qdone = []
                    for q in range(4):
                        s5 = (4 * hd + q + 8 * tt * NDC) % 5
                        stq = freed.get(("wdbs", s5))
                        if stq:
                            wait("gpsimd", stq[0], stq[1])
                        vw = dma("gpsimd", "dG", wdb[:, 2048 * s5:2048 * (s5 + 1)],
                                 wdt_d[128 * (4 * hd + q):128 * (4 * hd + q) + 128, :])
                        wait("vector", "dG", vw)
                        vg = p.inc("vV")
                        p.on("vector", lambda E, s5=s5: E.tensor_scalar(
                            wdb[:, 2048 * s5:2048 * (s5 + 1)], wdb[:, 2048 * s5:2048 * (s5 + 1)],
                            -128.0, None, ALU.add).then_inc(sems["vV"], 1))
                        qdone.append(vg)
                    bank = hd % 4
                    use_psS(bank)
                    for a in range(NIC):
                        if a % 16 == 0:
                            wait("tensor", "vV", qdone[a // 16])
                        if a == 0:
                            wait("tensor", "vV", freed[("h", NIC - 1)][1])
                        s5 = (4 * hd + a // 16 + 8 * tt * NDC) % 5
                        p.on("tensor", lambda E, a=a, bank=bank, s5=s5: E.matmul(
                            psS[bank][:], wdb[:, 2048 * s5 + 128 * (a % 16):2048 * s5 + 128 * (a % 16) + 128],
                            hbuf[:, 512 * a:512 * a + 512],
                            start=(a == 0), stop=(a == NIC - 1)))
                    vmm = p.inc("tT")
                    p.on("tensor", ("INC_LAST", "tT"))
                    for q in range(4):
                        s5 = (4 * hd + q + 8 * tt * NDC) % 5
                        freed[("wdbs", s5)] = ("tT", vmm)
                    wait("scalar", "tT", vmm)
                    vev = p.inc("aA")
                    p.on("scalar", lambda E, hd=hd, bank=bank: E.activation(
                        fo[:, hd * 512:(hd + 1) * 512], psS[bank][:], AF.Identity,
                        scale=dsc[:, hd:hd + 1]).then_inc(sems["aA"], 1))
                    bank_state[bank] = ("aA", vev)
                wait("tensor", "gG", p.val("gG"))
                if P6N < 4:
                    continue
                v_dn_all = p.val("tT")
                for hd in range(NDC):
                    wait("gpsimd", "aA", p.val("aA"))
                    wait("gpsimd", "tT", v_dn_all)
                    vg = p.inc("gG")
                    p.on("gpsimd", lambda E, hd=hd: E.tensor_copy(
                        fobf[:, hd * 512:(hd + 1) * 512],
                        fo[:, hd * 512:(hd + 1) * 512]).then_inc(sems["gG"], 1))
                wait("tensor", "gG", flushG())
                if P6N < 5:
                    continue

                def w5(a):
                    vv = p.inc("vV")
                    p.on("vector", lambda E, a=a: E.tensor_tensor(
                        fo[:, a * 512:(a + 1) * 512], fo[:, a * 512:(a + 1) * 512],
                        psB[:], ALU.mult).then_inc(sems["vV"], 1))

                def ln_plain5(xf, xbf, out_writer, sqbuf, cbl):
                    for a in range(NDC):
                        p.on("tensor", lambda E, a=a: E.matmul(
                            psL[:], onescol[:], xbf[:, a * 512:(a + 1) * 512],
                            start=(a == 0), stop=(a == NDC - 1)))
                    v_sum = p.inc("tT")
                    p.on("tensor", ("INC_LAST", "tT"))
                    wait("scalar", "tT", v_sum)
                    va = p.inc("aA")
                    p.on("scalar", lambda E: E.activation(
                        rowA[0:1, :], psL[:], AF.Identity,
                        scale=float(-1.0 / D)).then_inc(sems["aA"], 1))
                    release("psL", "aA", va)
                    wait_free("tensor", "psB")
                    wait("tensor", "aA", va)
                    vb = p.inc("tT")
                    p.on("tensor", lambda E: E.matmul(psB[:], onesrow[:], rowA[0:1, :],
                                                      start=True, stop=True).then_inc(sems["tT"], 1))
                    wait("vector", "tT", vb)
                    wait_free("tensor", "psO")
                    sq_rd = {}
                    for a in range(NDC):
                        vv = p.inc("vV")
                        p.on("vector", lambda E, a=a: E.tensor_tensor(
                            cbl[:, a * 512:(a + 1) * 512], xf[:, a * 512:(a + 1) * 512],
                            psB[:], ALU.add).then_inc(sems["vV"], 1))
                        wait("scalar", "vV", vv)
                        if a >= 2:
                            wait("scalar", "tT", sq_rd[a % 2])
                        vq = p.inc("aA")
                        p.on("scalar", lambda E, a=a: E.activation(
                            sqbuf[:, 512 * (a % 2):512 * (a % 2) + 512],
                            cbl[:, a * 512:(a + 1) * 512], AF.Square).then_inc(sems["aA"], 1))
                        wait("tensor", "aA", vq)
                        vt2 = p.inc("tT")
                        p.on("tensor", lambda E, a=a: E.matmul(
                            psO[0:1, :], onescol[:], sqbuf[:, 512 * (a % 2):512 * (a % 2) + 512],
                            start=(a == 0), stop=(a == NDC - 1)).then_inc(sems["tT"], 1))
                        sq_rd[a % 2] = vt2
                    v_cent = p.val("vV")
                    release("psB", "vV", v_cent)
                    vsq = p.val("tT")
                    wait("scalar", "tT", vsq)
                    p.on("scalar", lambda E: E.activation(rowB[0:1, :], psO[0:1, :], AF.Ln,
                                                          bias=epsv[0:1, 0:1], scale=float(1.0 / D)))
                    vr = p.inc("aA")
                    p.on("scalar", lambda E: E.activation(rowA[0:1, :], rowB[0:1, :], AF.Exp,
                                                          scale=-0.5).then_inc(sems["aA"], 1))
                    release("psO", "aA", vr)
                    wait_free("tensor", "psB")
                    wait("tensor", "aA", vr)
                    vb2 = p.inc("tT")
                    p.on("tensor", lambda E: E.matmul(psB[:], onesrow[:], rowA[0:1, :],
                                                      start=True, stop=True).then_inc(sems["tT"], 1))
                    wait("vector", "tT", vb2)
                    for a in range(NDC):
                        out_writer(a)
                    release("psB", "vV", p.val("vV"))

                ln_plain5(fo, fobf, w5, sqb4, fo)
                v_y5 = p.val("vV")
                if P6N < 6:
                    continue
                if P6N >= 7:
                    for a in range(NDC):
                        bx = a % 2
                        st = freed.get(("xr4", bx))
                        if st:
                            wait("sync", st[0], st[1])
                        if a < 2:
                            wait("sync", "dS", vsp)
                        vx1 = dma("sync", "dS", xr4[bx][:],
                                  x1_sp[128 * tt:128 * tt + 128, a * 512:(a + 1) * 512])
                        wait("vector", "dS", vx1)
                        vv1 = p.inc("vV")
                        p.on("vector", lambda E, a=a, bx=bx: E.tensor_tensor(
                            fo[:, a * 512:(a + 1) * 512], fo[:, a * 512:(a + 1) * 512],
                            xr4[bx][:], ALU.add).then_inc(sems["vV"], 1))
                        freed[("xr4", bx)] = ("vV", vv1 + 1)
                wait("sync", "vV", flushV())
                vst = dma("sync", "dS", out_d[128 * tt:128 * tt + 128, :], fo[:])
                freed[("fo",)] = ("dS", vst)
                if tt + 1 < NTT:
                    wait("scalar", "dS", vst)

            es4.close()

    wait("sync", "tT", p.val("tT"))
    wait("sync", "aA", p.val("aA"))
    wait("sync", "vV", p.val("vV"))
    wait("sync", "gG", p.val("gG"))
    wait("sync", "dS", p.val("dS"))
    wait("sync", "dG", p.val("dG"))

    global _LAST_PROG, _LAST_SEMS
    _LAST_PROG, _LAST_SEMS = p, sems
    with nc.Block() as block:
        for e in ENGINES:
            def runner(E, e=e):
                last = None
                for f in p.ops[e]:
                    if isinstance(f, tuple) and f[0] == "INC_LAST":
                        assert last is not None
                        last.then_inc(sems[f[1]], 1)
                        continue
                    r = f(E)
                    if r is not None:
                        last = r
            getattr(block, e)(runner)
    if PL >= 5:
        es.close()
    return nc


def _prep_inputs(inp):
    X = np.asarray(inp["L0_attn_X"])
    Wq = np.asarray(inp["L0_attn_Wq"]); bq = np.asarray(inp["L0_attn_bq"])
    Wk = np.asarray(inp["L0_attn_Wk"]); bk = np.asarray(inp["L0_attn_bk"])
    Wv = np.asarray(inp["L0_attn_Wv"]); bv = np.asarray(inp["L0_attn_bv"])
    Wo = np.asarray(inp["L0_attn_Wo"]); bo = np.asarray(inp["L0_attn_bo"])
    lng = np.asarray(inp["L0_attn_ln_g"]); lnb = np.asarray(inp["L0_attn_ln_b"])
    gw = np.asarray(inp["L0_ffn_gate_w"]); gs = np.asarray(inp["L0_ffn_gate_scale"])
    uw = np.asarray(inp["L0_ffn_up_w"]); us = np.asarray(inp["L0_ffn_up_scale"])
    dw = np.asarray(inp["L0_ffn_down_w"]); ds = np.asarray(inp["L0_ffn_down_scale"])

    def u8_of(a):
        return np.ascontiguousarray(a.view(np.uint8).reshape(a.shape + (4,))[..., 0])

    gw8 = u8_of(gw).reshape(NIC, 128, NDC, 128)
    wgt = np.ascontiguousarray(gw8.transpose(0, 3, 2, 1)).reshape(NIC * 128, NDC * 128)
    uw8 = u8_of(uw).reshape(NIC, 128, NDC, 128)
    wut = np.ascontiguousarray(uw8.transpose(0, 3, 2, 1)).reshape(NIC * 128, NDC * 128)
    dw8 = u8_of(dw).reshape(NDC, 128, 4, 16, 128)
    wdt = np.ascontiguousarray(dw8.transpose(0, 2, 4, 3, 1)).reshape(NDC * 4 * 128, 16 * 128)

    bf = ml_dtypes.bfloat16
    cm = np.zeros((128, 4 * 512), np.float32)
    kk = np.arange(128)[:, None]; qq = np.arange(512)[None, :]
    for r in range(4):
        cm[:, 512 * r:512 * r + 512] = (128 * r + kk <= qq)
    cmask = cm.astype(bf)
    onescol = np.ones((128, 1), np.float32).astype(bf)
    onesrow = np.ones((1, 128), np.float32)
    epsv = np.array([[1e-5, 1e-30]], np.float32)
    bo_l = np.ascontiguousarray(bo.reshape(NDC, 128).T)
    lng_l = np.ascontiguousarray(lng.reshape(NDC, 128).T)
    lnb_l = np.ascontiguousarray(lnb.reshape(NDC, 128).T)
    gs_l = np.ascontiguousarray(gs.reshape(NIC, 128).T)
    us_l = np.ascontiguousarray(us.reshape(NIC, 128).T)
    ds_l = np.ascontiguousarray(ds.reshape(NDC, 128).T)

    wo_t = np.ascontiguousarray(
        Wo.reshape(NDC, 128, NDC, 128).transpose(2, 1, 0, 3)).reshape(NDC * 128, NDC * 128)

    def qkv_tile(w, hsl):
        return np.ascontiguousarray(
            w[:, hsl].reshape(NDC, 128, 256).transpose(1, 0, 2)).reshape(128, NDC * 256)

    in_maps = []
    for c in range(NCORES):
        hsl = slice(256 * c, 256 * c + 256)
        bqk = np.zeros((128, 4), np.float32)
        bqk[:, 0:2] = bq[hsl].reshape(2, 128).T
        bqk[:, 2:4] = bk[hsl].reshape(2, 128).T
        Xc = X[0, TOK * c:TOK * (c + 1), :]
        xt_t = np.ascontiguousarray(
            Xc.reshape(NTT, 512, NDC, 128).transpose(0, 3, 2, 1)).reshape(NTT * 128, NDC * 512)
        in_maps.append(dict(
            xt=xt_t,
            wq=qkv_tile(Wq, hsl),
            wk=qkv_tile(Wk, hsl),
            wv=qkv_tile(Wv, hsl),
            wo=wo_t, bqk=bqk, bv=np.ascontiguousarray(bv[hsl])[None, :],
            bo=bo_l, lng=lng_l, lnb=lnb_l,
            wgt=wgt, wut=wut, wdt=wdt, gsc=gs_l, usc=us_l, dsc=ds_l,
            cmaskb=cmask, onescol=onescol, onesrow=onesrow, epsv=epsv,
        ))
    return in_maps


def kernel(**inp):
    in_maps = _prep_inputs(inp)
    nc = build_nc()
    res = run_bass_kernel_spmd(nc, in_maps, core_ids=list(range(NCORES)))
    cores = []
    for c in range(NCORES):
        o = np.asarray(res.results[c]["out"]).reshape(NTT, 128, NDC, 512)
        cores.append(o.transpose(0, 3, 2, 1).reshape(TOK, D))
    ret = np.ascontiguousarray(np.concatenate(cores, axis=0))[None]
    if _DEBUG_TAPS:
        kernel.debug_results = res.results
    return ret


# version 18
# speedup vs baseline: 2.7100x; 1.1789x over previous
import numpy as np
import ml_dtypes
import concourse.bass as bass
import concourse.mybir as mybir
from concourse.bass_utils import run_bass_kernel_spmd

F32 = mybir.dt.float32
BF16 = mybir.dt.bfloat16
U8 = mybir.dt.uint8
AF = mybir.ActivationFunctionType
ALU = mybir.AluOpType
AX = mybir.AxisListType

NCORES = 8
D = 2048
S = 8192
TOK = S // NCORES
NDC = D // 128
NTT = TOK // 512
HPC = 2
NQC = S // 512
NKC = S // 128
I_FFN = 8192
NIC = I_FFN // 128
ISQ = float(1.0 / np.sqrt(128.0))
ENGINES = ("sync", "tensor", "scalar", "vector", "gpsimd")

_LAST_PROG = None
_LAST_SEMS = None
_DETECT_RACES = True
_PHASE_LIMIT = 5
_P2_MODE = "all"
_P1_MIN = False
_P6N = 7
_DEBUG_TAPS = []


class Prog:

    def __init__(self):
        self.ops = {e: [] for e in ENGINES}
        self.cnt = {}

    def on(self, eng, fn):
        self.ops[eng].append(fn)

    def inc(self, sem, n=1):
        self.cnt[sem] = self.cnt.get(sem, 0) + n
        return self.cnt[sem]

    def val(self, sem):
        return self.cnt.get(sem, 0)


def build_nc():
    PL = _PHASE_LIMIT
    P2M = _P2_MODE
    P1MIN = _P1_MIN
    P6N = _P6N
    nc = bass.Bass("TRN2", target_bir_lowering=False,
                   detect_race_conditions=_DETECT_RACES)

    xt_d = nc.dram_tensor("xt", [NTT * 128, NDC * 512], F32, kind="ExternalInput").ap()
    wq_d = nc.dram_tensor("wq", [128, NDC * 256], F32, kind="ExternalInput").ap()
    wk_d = nc.dram_tensor("wk", [128, NDC * 256], F32, kind="ExternalInput").ap()
    wv_d = nc.dram_tensor("wv", [128, NDC * 256], F32, kind="ExternalInput").ap()
    wo_d = nc.dram_tensor("wo", [NDC * 128, NDC * 128], F32, kind="ExternalInput").ap()
    bqk_d = nc.dram_tensor("bqk", [128, 4], F32, kind="ExternalInput").ap()
    bv_d = nc.dram_tensor("bv", [1, 256], F32, kind="ExternalInput").ap()
    bo_d = nc.dram_tensor("bo", [128, NDC], F32, kind="ExternalInput").ap()
    lng_d = nc.dram_tensor("lng", [128, NDC], F32, kind="ExternalInput").ap()
    lnb_d = nc.dram_tensor("lnb", [128, NDC], F32, kind="ExternalInput").ap()
    wgt_d = nc.dram_tensor("wgt", [NIC * 128, NDC * 128], U8, kind="ExternalInput").ap()
    wut_d = nc.dram_tensor("wut", [NIC * 128, NDC * 128], U8, kind="ExternalInput").ap()
    wdt_d = nc.dram_tensor("wdt", [NDC * 4 * 128, 16 * 128], U8, kind="ExternalInput").ap()
    gsc_d = nc.dram_tensor("gsc", [128, NIC], F32, kind="ExternalInput").ap()
    usc_d = nc.dram_tensor("usc", [128, NIC], F32, kind="ExternalInput").ap()
    dsc_d = nc.dram_tensor("dsc", [128, NDC], F32, kind="ExternalInput").ap()
    cmask_d = nc.dram_tensor("cmaskb", [128, 4 * 512], BF16, kind="ExternalInput").ap()
    onescol_d = nc.dram_tensor("onescol", [128, 1], BF16, kind="ExternalInput").ap()
    onesrow_d = nc.dram_tensor("onesrow", [1, 128], F32, kind="ExternalInput").ap()
    eps_d = nc.dram_tensor("epsv", [1, 2], F32, kind="ExternalInput").ap()

    out_d = nc.dram_tensor("out", [NTT * 128, NDC * 512], F32, kind="ExternalOutput").ap()

    xn_con2 = [nc.dram_tensor(f"xn_con{i}", [128, NDC * 512], BF16, kind="Internal").ap()
               for i in range(NTT)]
    xn_gat2 = [nc.dram_tensor(f"xn_gat{i}", [NCORES * 128, NDC * 512], BF16,
                              kind="Internal", addr_space="Shared").ap()
               for i in range(NTT)]
    av_con2 = [nc.dram_tensor(f"av_con{i}", [128, S], BF16, kind="Internal").ap()
               for i in range(HPC)]
    av_gat2 = [nc.dram_tensor(f"av_gat{i}", [NCORES * 128, S], BF16,
                              kind="Internal", addr_space="Shared").ap()
               for i in range(HPC)]
    x1_sp = nc.dram_tensor("x1_sp", [NTT * 128, NDC * 512], F32, kind="Internal").ap()

    taps = {}
    for t in _DEBUG_TAPS:
        shapes = {"xn": ([128, NDC * TOK], BF16), "qt": ([128, HPC * S], BF16),
                  "kt": ([128, HPC * S], BF16), "vt": ([128, HPC * S], BF16),
                  "avo": ([128, NDC * TOK], BF16), "x1": ([NTT * 128, NDC * 512], F32),
                  "xn2": ([128, NDC * TOK], BF16), "ao": ([128, NDC * 512], F32),
                  "negm": ([128, HPC * NQC], F32), "hb": ([128, NIC * 512], BF16)}
        sh, dt = shapes[t]
        taps[t] = nc.dram_tensor("tap_" + t, sh, dt, kind="ExternalOutput").ap()

    from contextlib import ExitStack
    es = ExitStack()
    sb = lambda n, s_, d: es.enter_context(nc.sbuf_tensor(n, s_, d))
    psb = lambda n, s_, d: es.enter_context(nc.psum_tensor(n, s_, d))

    sems = {}
    for name in ("dS", "dG", "tT", "aA", "vV", "gG", "cC"):
        sems[name] = es.enter_context(nc.semaphore(name))

    onescol = sb("onescol_s", [128, 1], BF16)
    onesrow = sb("onesrow_s", [1, 128], F32)
    epsv = sb("eps_s", [1, 2], F32)
    bqk = sb("bqk_s", [128, 4], F32)
    bo_s = sb("bo_s", [128, NDC], F32)
    lng = sb("lng_s", [128, NDC], F32)
    lnb = sb("lnb_s", [128, NDC], F32)
    gsc = sb("gsc_s", [128, NIC], F32)
    usc = sb("usc_s", [128, NIC], F32)
    dsc = sb("dsc_s", [128, NDC], F32)
    negm = sb("negm_s", [128, HPC * NQC], F32)
    vflush = sb("vflush_s", [1, 8], F32)
    gflush = sb("gflush_s", [1, 8], F32)
    rowA = sb("rowA_s", [1, 512], F32)
    rowB = sb("rowB_s", [1, 512], F32)
    mhatln = sb("mhatln_s", [1, NQC], F32)
    mxkln = sb("mxkln_s", [1, NQC], F32)
    mexp = sb("mexp_s", [128, NQC], F32)

    psS = [psb(f"psS{i}", [128, 512], F32) for i in range(4)]
    psO2 = [psb(f"psOx{i}", [128, 512], F32) for i in range(2)]
    psO = psO2[0]
    psL = psb("psL", [1, 512], F32)
    psB = psb("psB", [128, 512], F32)

    p = Prog()
    freed = {}

    def wait_free(eng, key):
        st = freed.get(key)
        if st is not None:
            sem, v = st
            p.on(eng, lambda E, s=sems[sem], vv=v: (E.wait_ge(s, vv), None)[1])

    def release(key, sem, val):
        freed[key] = (sem, val)

    def emit(eng, f):
        p.on(eng, f)

    def dma(eng, sem_name, out_ap, in_ap, **kw):
        v = p.inc(sem_name, 16)
        p.on(eng, lambda E, o=out_ap, i=in_ap, s=sems[sem_name], kw=kw:
             E.dma_start(out=o, in_=i, **kw).then_inc(s, 16))
        return v

    def flushV():
        v = p.inc("vV")
        p.on("vector", lambda E: E.memset(vflush[0:1, 0:1], 0.0).then_inc(sems["vV"], 1))
        return v

    def flushG():
        v = p.inc("gG")
        p.on("gpsimd", lambda E: E.memset(gflush[0:1, 0:1], 0.0).then_inc(sems["gG"], 1))
        return v

    def wait(eng, sem_name, val):
        if val > 0:
            p.on(eng, lambda E, s=sems[sem_name], v=val: (E.wait_ge(s, v), None)[1])

    for dst, src in ((onescol, onescol_d), (onesrow, onesrow_d), (epsv, eps_d),
                     (bqk, bqk_d), (bo_s, bo_d), (lng, lng_d),
                     (lnb, lnb_d), (gsc, gsc_d), (usc, usc_d), (dsc, dsc_d)):
        dma("sync", "dS", dst[:], src)
    v_consts = p.val("dS")

    es1 = ExitStack()
    sb1 = lambda n, s_, d: es1.enter_context(nc.sbuf_tensor(n, s_, d))
    xbuf = sb1("xbuf", [128, NDC * 512], F32)
    xbb = sb1("xbb", [128, NDC * 512], BF16)
    cb1 = sb1("cb1", [128, NDC * 512], F32)
    cb2 = sb1("cb2", [128, NDC * 512], F32)
    xnb = sb1("xnb", [128, NDC * TOK], BF16)
    sqb = sb1("sqb", [128, 2 * 512], BF16)

    def ln_plain(xf, xf_off, xbf, xbf_off, ncols, out_writer, aff=None):
        wait_free("tensor", "psL")
        for a in range(NDC):
            vs = p.inc("tT")
            p.on("tensor", lambda E, a=a: E.matmul(
                psL[:, 0:ncols], onescol[:],
                xbf[:, xbf_off + a * ncols: xbf_off + (a + 1) * ncols],
                start=(a == 0), stop=(a == NDC - 1)).then_inc(sems["tT"], 1))
        v_sum = p.val("tT")
        wait("scalar", "tT", v_sum)
        va = p.inc("aA")
        p.on("scalar", lambda E: E.activation(
            rowA[0:1, 0:ncols], psL[:, 0:ncols], AF.Identity,
            scale=float(-1.0 / D)).then_inc(sems["aA"], 1))
        release("psL", "aA", va)
        wait_free("tensor", "psB")
        wait("tensor", "aA", va)
        vb = p.inc("tT")
        p.on("tensor", lambda E: E.matmul(
            psB[:, 0:ncols], onesrow[:], rowA[0:1, 0:ncols],
            start=True, stop=True).then_inc(sems["tT"], 1))
        wait("vector", "tT", vb)
        wait_free("tensor", "psO")
        sq_rd = {}
        for a in range(NDC):
            vv = p.inc("vV")
            p.on("vector", lambda E, a=a: E.tensor_tensor(
                cb1[:, a * ncols:(a + 1) * ncols],
                xf[:, xf_off + a * ncols: xf_off + (a + 1) * ncols],
                psB[:, 0:ncols], ALU.add).then_inc(sems["vV"], 1))
            wait("scalar", "vV", vv)
            if a >= 2:
                wait("scalar", "tT", sq_rd[a % 2])
            vq = p.inc("aA")
            p.on("scalar", lambda E, a=a: E.activation(
                sqb[:, 512 * (a % 2):512 * (a % 2) + ncols],
                cb1[:, a * ncols:(a + 1) * ncols], AF.Square).then_inc(sems["aA"], 1))
            wait("tensor", "aA", vq)
            vt2 = p.inc("tT")
            p.on("tensor", lambda E, a=a: E.matmul(
                psO[0:1, 0:ncols], onescol[:],
                sqb[:, 512 * (a % 2):512 * (a % 2) + ncols],
                start=(a == 0), stop=(a == NDC - 1)).then_inc(sems["tT"], 1))
            sq_rd[a % 2] = vt2
        v_cent = p.val("vV")
        v_sq = p.val("tT")
        release("psB", "vV", v_cent)
        wait("scalar", "tT", v_sq)
        p.on("scalar", lambda E: E.activation(
            rowB[0:1, 0:ncols], psO[0:1, 0:ncols], AF.Ln,
            bias=epsv[0:1, 0:1], scale=float(1.0 / D)))
        vr = p.inc("aA")
        p.on("scalar", lambda E: E.activation(
            rowA[0:1, 0:ncols], rowB[0:1, 0:ncols], AF.Exp,
            scale=-0.5).then_inc(sems["aA"], 1))
        release("psO", "aA", vr)
        wait_free("tensor", "psB")
        wait("tensor", "aA", vr)
        vb2 = p.inc("tT")
        p.on("tensor", lambda E: E.matmul(
            psB[:, 0:ncols], onesrow[:], rowA[0:1, 0:ncols],
            start=True, stop=True).then_inc(sems["tT"], 1))
        wait("vector", "tT", vb2)
        for a in range(NDC):
            out_writer(a)
        vfin = p.val("vV")
        release("psB", "vV", vfin)
        return vfin

    v_xt_tiles = []
    for tt in range(NTT):
        if tt > 0:
            wait("sync", "vV", p.val("vV"))
            wait("sync", "tT", p.val("tT"))
            wait("gpsimd", "vV", p.val("vV"))
            wait("gpsimd", "tT", p.val("tT"))
        src = xt_d[128 * tt:128 * tt + 128, :]
        vx = dma("sync", "dS", xbuf[:], src)
        vxb = dma("gpsimd", "dG", xbb[:], src)
        wait("tensor", "dS", vx)
        wait("tensor", "dG", vxb)
        wait("vector", "dS", vx)
        wait("vector", "dG", vxb)
        wait("scalar", "dS", vx)

        if P1MIN:
            for a in range(NDC):
                vv = p.inc("vV")
                p.on("vector", lambda E, a=a, tt=tt: E.tensor_copy(
                    xnb[:, a * TOK + 512 * tt: a * TOK + 512 * tt + 512],
                    xbb[:, a * 512:(a + 1) * 512]).then_inc(sems["vV"], 1))
            v_xt_tiles.append(p.val("vV"))
            continue
        def w1(a):
            vv = p.inc("vV")
            p.on("vector", lambda E, a=a: E.tensor_tensor(
                cb2[:, a * 512:(a + 1) * 512], cb1[:, a * 512:(a + 1) * 512],
                psB[:, 0:512], ALU.mult).then_inc(sems["vV"], 1))
            wait("gpsimd", "vV", vv)
            vg = p.inc("gG")
            p.on("gpsimd", lambda E, a=a: E.tensor_copy(
                xbb[:, a * 512:(a + 1) * 512],
                cb2[:, a * 512:(a + 1) * 512]).then_inc(sems["gG"], 1))
        ln_plain(xbuf, 0, xbb, 0, 512, w1)
        v_y1bf = flushG()
        wait("tensor", "gG", v_y1bf)

        def w2(a):
            vv0 = p.inc("vV")
            p.on("vector", lambda E, a=a: E.tensor_tensor(
                cb1[:, a * 512:(a + 1) * 512], cb1[:, a * 512:(a + 1) * 512],
                psB[:, 0:512], ALU.mult).then_inc(sems["vV"], 1))
            vv1 = p.inc("vV")
            p.on("vector", lambda E, a=a, tt=tt: E.tensor_scalar(
                xnb[:, tt * NDC * 512 + a * 512: tt * NDC * 512 + a * 512 + 512],
                cb1[:, a * 512:(a + 1) * 512],
                lng[:, a:a + 1], lnb[:, a:a + 1],
                ALU.mult, ALU.add).then_inc(sems["vV"], 1))
        ln_plain(cb2, 0, xbb, 0, 512, w2)
        v_xn_tt = p.val("vV")
        v_xt_tiles.append(v_xn_tt)
        wait("sync", "vV", flushV())
        v_con = dma("sync", "dS", xn_con2[tt],
                    xnb[:, tt * NDC * 512:(tt + 1) * NDC * 512])
        wait("gpsimd", "dS", v_con)
        if tt > 0:
            wait("gpsimd", "cC", p.val("cC"))
        p.on("gpsimd", lambda E, tt=tt: E.collective_compute(
            "AllGather", ALU.bypass, replica_groups=[list(range(NCORES))],
            ins=[xn_con2[tt]], outs=[xn_gat2[tt]],
        ).then_inc(sems["cC"], 1))
        p.inc("cC")

    vcc1 = p.val("cC")
    if "xn" in taps:
        dma("sync", "dS", taps["xn"], xnb[:])

    es1.close()

    if PL >= 2:
            es2 = ExitStack()
            sb2 = lambda n, s_, d: es2.enter_context(nc.sbuf_tensor(n, s_, d))
            wqb = sb2("wqb", [128, NDC * 256], BF16)
            wkb = sb2("wkb", [128, NDC * 256], BF16)
            wvb = sb2("wvb", [128, NDC * 256], BF16)
            xg = [sb2(f"xg{i}", [128, NDC * 512], BF16) for i in range(2)]
            qt = sb2("qt", [128, HPC * S], BF16)
            kt = sb2("kt", [128, HPC * S], BF16)
            vt = sb2("vt", [128, HPC * S], BF16)
            pbuf = sb2("pbuf", [128, 4 * 512], BF16)
            avst = sb2("avst", [128, 512], BF16)
            sqn = sb2("sqn", [128, 2 * 512], BF16)
            nrow = sb2("nrow", [1, 2 * 512], F32)
            mkparts = sb2("mkparts", [1, NQC], F32)
            cmask = sb2("cmask_s", [128, 4 * 512], BF16)
            bvrow = sb2("bvrow_s", [1, 256], F32)
            bvb = sb2("bvb_s", [128, 256], F32)
            dma("sync", "dS", cmask[:], cmask_d)
            v_bv = dma("sync", "dS", bvrow[:], bv_d)
            wait_free("tensor", "psB")
            wait("tensor", "dS", v_bv)
            vbv = p.inc("tT")
            p.on("tensor", lambda E: E.matmul(psB[:, 0:256], onesrow[:], bvrow[:],
                                              start=True, stop=True).then_inc(sems["tT"], 1))
            wait("scalar", "tT", vbv)
            vbv2 = p.inc("aA")
            p.on("scalar", lambda E: E.activation(bvb[:], psB[:, 0:256],
                                                  AF.Identity).then_inc(sems["aA"], 1))
            release("psB", "aA", vbv2)

            if P2M == "cc":
                wait("gpsimd", "cC", vcc1)
            for dst, src in (((wqb, wq_d), (wkb, wk_d), (wvb, wv_d)) if P2M not in ("xg", "cc") else ()):
                wait("gpsimd", "dS", v_con)
                dma("gpsimd", "dG", dst[:], src)
            v_wqkv = p.val("dG")

            bank_state = {}

            def use_psS(i, eng="tensor"):
                st = bank_state.get(i)
                if st:
                    wait(eng, st[0], st[1])

            xg_free = [None, None]
            for half in range(2):
                for r in (range(NCORES) if P2M not in ("w", "cc") else ()):
                    b = (half * NCORES + r) % 2
                    if xg_free[b]:
                        wait("sync", xg_free[b][0], xg_free[b][1])
                    wait("sync", "cC", half + 1)
                    v = dma("sync", "dS", xg[b][:],
                            xn_gat2[half][128 * r:128 * (r + 1), :])
                    wait("tensor", "dS", v)
                    if half == 0 and r == 0:
                        wait("tensor", "dG", v_wqkv)
                    g = 2 * r + half
                    for (wb, dstbuf) in (((wqb, qt), (wkb, kt)) if P2M in ("qk", "all") else ()):
                        for h in range(2):
                            bank = (2 * g + h) % 4
                            use_psS(bank)
                            vt_mm = p.inc("tT")
                            for a in range(NDC):
                                p.on("tensor", lambda E, a=a, h=h, wb=wb, bank=bank, b=b: (
                                    E.matmul(
                                        psS[bank][:], wb[:, a * 256 + 128 * h: a * 256 + 128 * h + 128],
                                        xg[b][:, a * 512:(a + 1) * 512],
                                        start=(a == 0), stop=(a == NDC - 1)
                                    ).then_inc(sems["tT"], 1) if a == NDC - 1 else
                                    E.matmul(
                                        psS[bank][:], wb[:, a * 256 + 128 * h: a * 256 + 128 * h + 128],
                                        xg[b][:, a * 512:(a + 1) * 512],
                                        start=(a == 0), stop=(a == NDC - 1))))
                            wait("scalar", "tT", vt_mm)
                            bcol = (0 if dstbuf is qt else 2) + h
                            va2 = p.inc("aA")
                            p.on("scalar", lambda E, bank=bank, h=h, g=g, dstbuf=dstbuf, bcol=bcol:
                                 E.activation(dstbuf[:, h * S + 512 * g: h * S + 512 * g + 512],
                                              psS[bank][:], AF.Identity,
                                              bias=bqk[:, bcol:bcol + 1]).then_inc(sems["aA"], 1))
                            bank_state[bank] = ("aA", va2)
                    for sub in (range(4) if P2M in ("v", "all") else ()):
                        bank = sub % 4
                        use_psS(bank)
                        vt_mm = p.inc("tT")
                        for a in range(NDC):
                            p.on("tensor", lambda E, a=a, sub=sub, bank=bank, b=b: (
                                E.matmul(
                                    psS[bank][:, 0:256],
                                    xg[b][:, a * 512 + 128 * sub: a * 512 + 128 * sub + 128],
                                    wvb[:, a * 256:(a + 1) * 256],
                                    start=(a == 0), stop=(a == NDC - 1)
                                ).then_inc(sems["tT"], 1) if a == NDC - 1 else
                                E.matmul(
                                    psS[bank][:, 0:256],
                                    xg[b][:, a * 512 + 128 * sub: a * 512 + 128 * sub + 128],
                                    wvb[:, a * 256:(a + 1) * 256],
                                    start=(a == 0), stop=(a == NDC - 1))))
                        wait("vector", "tT", vt_mm)
                        if half == 0 and r == 0 and sub == 0:
                            wait("vector", "aA", vbv2)
                        kc = 4 * g + sub
                        vv2 = p.inc("vV")
                        p.on("vector", lambda E, bank=bank, kc=kc: E.tensor_tensor(
                            vt[:, 256 * kc:256 * kc + 256], psS[bank][:, 0:256], bvb[:],
                            ALU.add).then_inc(sems["vV"], 1))
                        bank_state[bank] = ("vV", vv2)
                    xg_free[b] = ("tT", p.val("tT"))

            if "qt" in taps:
                wait("sync", "aA", p.val("aA"))
                dma("sync", "dS", taps["qt"], qt[:])
                dma("sync", "dS", taps["kt"], kt[:])
            if "vt" in taps:
                wait("sync", "vV", p.val("vV"))
                dma("sync", "dS", taps["vt"], vt[:])

    if PL >= 3:
            v_qk_done = p.val("aA")
            v_prev_negm = 0
            sqn_rd = {}
            for h in range(HPC):
                nrow_free = [None, None]
                for t in range(2 * (S // 512)):
                    srcbuf, col = (qt, 512 * t) if t < S // 512 else (kt, 512 * (t - S // 512))
                    col += h * S
                    wait("scalar", "aA", v_qk_done)
                    if h > 0 and t == 0:
                        wait("scalar", "vV", v_prev_negm)
                    if sqn_rd.get(t % 2):
                        wait("scalar", "tT", sqn_rd[t % 2])
                    vq = p.inc("aA")
                    p.on("scalar", lambda E, srcbuf=srcbuf, col=col, t=t: E.activation(
                        sqn[:, 512 * (t % 2):512 * (t % 2) + 512],
                        srcbuf[:, col:col + 512], AF.Square).then_inc(sems["aA"], 1))
                    wait("tensor", "aA", vq)
                    use_psS(t % 4)
                    vm = p.inc("tT")
                    p.on("tensor", lambda E, t=t: E.matmul(
                        psS[t % 4][0:1, :], onescol[:],
                        sqn[:, 512 * (t % 2):512 * (t % 2) + 512],
                        start=True, stop=True).then_inc(sems["tT"], 1))
                    sqn_rd[t % 2] = vm
                    wait("scalar", "tT", vm)
                    if nrow_free[t % 2]:
                        wait("scalar", *nrow_free[t % 2])
                    sc = 1.0 if t < S // 512 else (1.0 / 128.0)
                    va3 = p.inc("aA")
                    p.on("scalar", lambda E, sc=sc, t=t: E.activation(
                        nrow[0:1, 512 * (t % 2):512 * (t % 2) + 512], psS[t % 4][0:1, :],
                        AF.Ln, bias=epsv[0:1, 1:2], scale=sc).then_inc(sems["aA"], 1))
                    bank_state[t % 4] = ("aA", va3)
                    wait("vector", "aA", va3)
                    vmx_t = p.inc("vV")
                    dst, colq = (mhatln, t) if t < S // 512 else (mkparts, t - S // 512)
                    p.on("vector", lambda E, dst=dst, colq=colq, t=t: E.reduce_max(
                        dst[0:1, colq:colq + 1],
                        nrow[0:1, 512 * (t % 2):512 * (t % 2) + 512],
                        axis=AX.X).then_inc(sems["vV"], 1))
                    nrow_free[t % 2] = ("vV", vmx_t)
                for qc in range(NQC):
                    p.on("vector", lambda E, qc=qc: E.reduce_max(
                        mxkln[0:1, qc:qc + 1], mkparts[0:1, :], axis=AX.X))
                vmx = p.inc("vV")
                p.on("vector", ("INC_LAST", "vV"))
                wait_free("tensor", "psB")
                wait("tensor", "vV", vmx)
                p.on("tensor", lambda E: E.matmul(psB[:, 0:NQC], onesrow[:], mhatln[:],
                                                  start=True, stop=False))
                vb3 = p.inc("tT")
                p.on("tensor", lambda E: E.matmul(psB[:, 0:NQC], onesrow[:], mxkln[:],
                                                  start=False, stop=True).then_inc(sems["tT"], 1))
                wait("scalar", "tT", vb3)
                vme = p.inc("aA")
                p.on("scalar", lambda E: E.activation(mexp[:], psB[:, 0:NQC], AF.Exp,
                                                      scale=0.5).then_inc(sems["aA"], 1))
                release("psB", "aA", vme)
                wait("vector", "aA", vme)
                vng = p.inc("vV")
                p.on("vector", lambda E, h=h: E.tensor_scalar_mul(
                    negm[:, NQC * h:NQC * (h + 1)], mexp[:], -1.0).then_inc(sems["vV"], 1))
                v_prev_negm = vng
            v_negm = p.val("vV")
            if "negm" in taps:
                wait("sync", "vV", v_negm)
                dma("sync", "dS", taps["negm"], negm[:])

            DEPTH = 4
            v_av_con = []
            for h in range(HPC):
                for qc in range(NQC):
                    J = 4 * (qc + 1)
                    qof = h * S + 512 * qc
                    psOb = psO2[(h * NQC + qc) % 2]
                    sdone = {}
                    expdone = {}
                    maskdone = {}
                    avdone = {}

                    def emit_S(j):
                        bank = j % DEPTH
                        use_psS(bank)
                        if j < DEPTH and qc == 0 and h == 0:
                            wait("tensor", "vV", v_negm)
                        vs = p.inc("tT")
                        p.on("tensor", lambda E, j=j, bank=bank, qof=qof, h=h: E.matmul(
                            psS[bank][:], kt[:, h * S + 128 * j: h * S + 128 * j + 128],
                            qt[:, qof:qof + 512], start=True, stop=True).then_inc(sems["tT"], 1))
                        sdone[j] = vs

                    for j in range(min(DEPTH, J)):
                        emit_S(j)
                    for j in range(J):
                        bank = j % DEPTH
                        wait("scalar", "tT", sdone[j])
                        if j - DEPTH >= 0:
                            wait("scalar", "tT", avdone[j - DEPTH])
                        ve = p.inc("aA")
                        p.on("scalar", lambda E, j=j, bank=bank, h=h, qc=qc: E.activation(
                            pbuf[:, 512 * (j % DEPTH):512 * (j % DEPTH) + 512],
                            psS[bank][:], AF.Exp,
                            bias=negm[:, NQC * h + qc: NQC * h + qc + 1],
                            scale=ISQ).then_inc(sems["aA"], 1))
                        expdone[j] = ve
                        bank_state[bank] = ("aA", ve)
                        r = j - 4 * qc
                        if r >= 0:
                            wait("vector", "aA", ve)
                            vmk = p.inc("vV")
                            p.on("vector", lambda E, j=j, r=r: E.tensor_tensor(
                                pbuf[:, 512 * (j % DEPTH):512 * (j % DEPTH) + 512],
                                pbuf[:, 512 * (j % DEPTH):512 * (j % DEPTH) + 512],
                                cmask[:, 512 * r:512 * r + 512], ALU.mult).then_inc(sems["vV"], 1))
                            maskdone[j] = vmk
                            wait("tensor", "vV", vmk)
                        else:
                            wait("tensor", "aA", ve)
                        if j == 0:
                            wait_free("tensor", "psL")
                            wait_free("tensor", ("psO", (h * NQC + qc) % 2))
                        p.on("tensor", lambda E, j=j, first=(j == 0), last=(j == J - 1): E.matmul(
                            psL[:], onescol[:], pbuf[:, 512 * (j % DEPTH):512 * (j % DEPTH) + 512],
                            start=first, stop=last))
                        vav = p.inc("tT")
                        p.on("tensor", lambda E, j=j, h=h, psOb=psOb, first=(j == 0), last=(j == J - 1): E.matmul(
                            psOb[:], vt[:, 256 * j + 128 * h: 256 * j + 128 * h + 128],
                            pbuf[:, 512 * (j % DEPTH):512 * (j % DEPTH) + 512],
                            start=first, stop=last).then_inc(sems["tT"], 1))
                        avdone[j] = vav
                        if j + DEPTH < J:
                            emit_S(j + DEPTH)
                    wait("scalar", "tT", avdone[J - 1])
                    p.on("scalar", lambda E: E.activation(rowB[0:1, :], psL[:], AF.Ln))
                    vrl = p.inc("aA")
                    p.on("scalar", lambda E: E.activation(rowA[0:1, :], rowB[0:1, :], AF.Exp,
                                                          scale=-1.0).then_inc(sems["aA"], 1))
                    release("psL", "aA", vrl)
                    wait_free("tensor", "psB")
                    wait("tensor", "aA", vrl)
                    vbc = p.inc("tT")
                    p.on("tensor", lambda E: E.matmul(psB[:], onesrow[:], rowA[0:1, :],
                                                      start=True, stop=True).then_inc(sems["tT"], 1))
                    wait("vector", "tT", vbc)
                    p.on("vector", lambda E: E.tensor_copy(sqn.ap().bitcast(F32)[:, 0:512], psB[:]))
                    vfin = p.inc("vV")
                    p.on("vector", lambda E, psOb=psOb: E.tensor_tensor(
                        avst[:], psOb[:], sqn.ap().bitcast(F32)[:, 0:512],
                        ALU.mult).then_inc(sems["vV"], 1))
                    release("psB", "vV", vfin)
                    release(("psO", (h * NQC + qc) % 2), "vV", vfin)
                    wait("sync", "vV", flushV())
                    vdc = dma("sync", "dS", av_con2[h][0:128, 512 * qc:512 * qc + 512],
                              avst[:])
                    v_av_con.append(vdc)
                    wait("vector", "dS", vdc)
                if True:
                    wait("gpsimd", "dS", max(v_av_con))
                    wait("gpsimd", "cC", p.val("cC"))
                    p.on("gpsimd", lambda E, h=h: E.collective_compute(
                        "AllGather", ALU.bypass, replica_groups=[list(range(NCORES))],
                        ins=[av_con2[h]], outs=[av_gat2[h]]).then_inc(sems["cC"], 1))
                    p.inc("cC")

            vcc2 = p.val("cC")

            es2.close()
    if PL >= 4:

            xn2 = es.enter_context(nc.sbuf_tensor("xn2", [128, NDC * TOK], BF16))
            es3 = ExitStack()
            sb3 = lambda n, s_, d: es3.enter_context(nc.sbuf_tensor(n, s_, d))
            avo = sb3("avo", [128, NDC * TOK], BF16)
            wob = [sb3(f"wob{i}", [128, NDC * 128], BF16) for i in range(2)]
            ao = sb3("ao", [128, NDC * 512], F32)
            aobf = sb3("aobf", [128, NDC * 512], BF16)
            cb = sb3("cb", [128, NDC * 512], F32)
            xre = [sb3(f"xre{i}", [128, 512], F32) for i in range(2)]
            sqb3 = sb3("sqb3", [128, 2 * 512], BF16)

            cell = {}
            wait("gpsimd", "cC", vcc1 + 1)
            def _getcid(E):
                cell["cid"] = E.partition_id()
            p.on("gpsimd", _getcid)
            vavo_e = p.inc("dG", 16 * (NDC // 2))
            def _avoload_e(E):
                for j in range(0, NDC, 2):
                    E.dma_start(out=avo[:, j * TOK:(j + 1) * TOK],
                                in_=av_gat2[0][128 * (j // 2):128 * (j // 2) + 128,
                                               bass.ts(cell["cid"], TOK)]
                                ).then_inc(sems["dG"], 16)
            p.on("gpsimd", _avoload_e)
            wait("gpsimd", "cC", vcc1 + 2)
            vavo = p.inc("dG", 16 * (NDC // 2))
            def _avoload_o(E):
                for j in range(1, NDC, 2):
                    E.dma_start(out=avo[:, j * TOK:(j + 1) * TOK],
                                in_=av_gat2[1][128 * (j // 2):128 * (j // 2) + 128,
                                               bass.ts(cell["cid"], TOK)]
                                ).then_inc(sems["dG"], 16)
            p.on("gpsimd", _avoload_o)

            def ln_plain3(xf, xbf, out_writer, sqbuf):
                for a in range(NDC):
                    p.on("tensor", lambda E, a=a: E.matmul(
                        psL[:], onescol[:], xbf[:, a * 512:(a + 1) * 512],
                        start=(a == 0), stop=(a == NDC - 1)))
                v_sum = p.inc("tT")
                p.on("tensor", ("INC_LAST", "tT"))
                wait("scalar", "tT", v_sum)
                va = p.inc("aA")
                p.on("scalar", lambda E: E.activation(
                    rowA[0:1, :], psL[:], AF.Identity,
                    scale=float(-1.0 / D)).then_inc(sems["aA"], 1))
                release("psL", "aA", va)
                wait_free("tensor", "psB")
                wait("tensor", "aA", va)
                vb = p.inc("tT")
                p.on("tensor", lambda E: E.matmul(psB[:], onesrow[:], rowA[0:1, :],
                                                  start=True, stop=True).then_inc(sems["tT"], 1))
                wait("vector", "tT", vb)
                wait_free("tensor", "psO")
                sq_rd = {}
                for a in range(NDC):
                    vv = p.inc("vV")
                    p.on("vector", lambda E, a=a: E.tensor_tensor(
                        cb[:, a * 512:(a + 1) * 512], xf[:, a * 512:(a + 1) * 512],
                        psB[:], ALU.add).then_inc(sems["vV"], 1))
                    wait("scalar", "vV", vv)
                    if a >= 2:
                        wait("scalar", "tT", sq_rd[a % 2])
                    vq = p.inc("aA")
                    p.on("scalar", lambda E, a=a: E.activation(
                        sqbuf[:, 512 * (a % 2):512 * (a % 2) + 512],
                        cb[:, a * 512:(a + 1) * 512], AF.Square).then_inc(sems["aA"], 1))
                    wait("tensor", "aA", vq)
                    vt2 = p.inc("tT")
                    p.on("tensor", lambda E, a=a: E.matmul(
                        psO[0:1, :], onescol[:], sqbuf[:, 512 * (a % 2):512 * (a % 2) + 512],
                        start=(a == 0), stop=(a == NDC - 1)).then_inc(sems["tT"], 1))
                    sq_rd[a % 2] = vt2
                v_cent = p.val("vV")
                release("psB", "vV", v_cent)
                vsq = p.val("tT")
                wait("scalar", "tT", vsq)
                p.on("scalar", lambda E: E.activation(rowB[0:1, :], psO[0:1, :], AF.Ln,
                                                      bias=epsv[0:1, 0:1], scale=float(1.0 / D)))
                vr = p.inc("aA")
                p.on("scalar", lambda E: E.activation(rowA[0:1, :], rowB[0:1, :], AF.Exp,
                                                      scale=-0.5).then_inc(sems["aA"], 1))
                release("psO", "aA", vr)
                wait_free("tensor", "psB")
                wait("tensor", "aA", vr)
                vb2 = p.inc("tT")
                p.on("tensor", lambda E: E.matmul(psB[:], onesrow[:], rowA[0:1, :],
                                                  start=True, stop=True).then_inc(sems["tT"], 1))
                wait("vector", "tT", vb2)
                for a in range(NDC):
                    out_writer(a)
                release("psB", "vV", p.val("vV"))

            for tt in range(NTT):
                for ch in range(NDC):
                    bw = ch % 2
                    st = freed.get(("wob", bw))
                    if st:
                        wait("gpsimd", st[0], st[1])
                    if tt == 0 and ch < 2:
                        wait("gpsimd", "dS", max(v_av_con))
                    vw = dma("gpsimd", "dG", wob[bw][:],
                             wo_d[128 * ch:128 * ch + 128, :])
                    wait("tensor", "dG", max(vw, vavo))
                    bank = ch % 4
                    use_psS(bank)
                    for a in range(NDC):
                        p.on("tensor", lambda E, a=a, bw=bw, tt=tt, bank=bank: E.matmul(
                            psS[bank][:],
                            wob[bw][:, 128 * a:128 * a + 128],
                            avo[:, a * TOK + 512 * tt: a * TOK + 512 * tt + 512],
                            start=(a == 0), stop=(a == NDC - 1)))
                    vmm = p.inc("tT")
                    p.on("tensor", ("INC_LAST", "tT"))
                    freed[("wob", bw)] = ("tT", vmm)
                    wait("scalar", "tT", vmm)
                    vev = p.inc("aA")
                    p.on("scalar", lambda E, ch=ch, bank=bank: E.activation(
                        ao[:, ch * 512:(ch + 1) * 512], psS[bank][:], AF.Identity,
                        bias=bo_s[:, ch:ch + 1]).then_inc(sems["aA"], 1))
                    bank_state[bank] = ("aA", vev)
                    wait("gpsimd", "aA", vev)
                    vg = p.inc("gG")
                    p.on("gpsimd", lambda E, ch=ch: E.tensor_copy(
                        aobf[:, ch * 512:(ch + 1) * 512],
                        ao[:, ch * 512:(ch + 1) * 512]).then_inc(sems["gG"], 1))
                if "ao" in taps and tt == 0:
                    wait("sync", "aA", p.val("aA"))
                    dma("sync", "dS", taps["ao"], ao[:])
                wait("tensor", "gG", flushG())

                def w3(a):
                    vv = p.inc("vV")
                    p.on("vector", lambda E, a=a: E.tensor_tensor(
                        ao[:, a * 512:(a + 1) * 512], cb[:, a * 512:(a + 1) * 512],
                        psB[:], ALU.mult).then_inc(sems["vV"], 1))
                ln_plain3(ao, aobf, w3, sqb3)
                for a in range(NDC):
                    bx = a % 2
                    st = freed.get(("xre", bx))
                    if st:
                        wait("sync", st[0], st[1])
                    vx = dma("sync", "dS", xre[bx][:],
                             xt_d[128 * tt:128 * tt + 128, a * 512:(a + 1) * 512])
                    wait("vector", "dS", vx)
                    vv = p.inc("vV")
                    p.on("vector", lambda E, a=a, bx=bx: E.tensor_tensor(
                        ao[:, a * 512:(a + 1) * 512], ao[:, a * 512:(a + 1) * 512],
                        xre[bx][:], ALU.add).then_inc(sems["vV"], 1))
                    freed[("xre", bx)] = ("vV", vv + 1)
                    wait("gpsimd", "vV", vv)
                    vg = p.inc("gG")
                    p.on("gpsimd", lambda E, a=a: E.tensor_copy(
                        aobf[:, a * 512:(a + 1) * 512],
                        ao[:, a * 512:(a + 1) * 512]).then_inc(sems["gG"], 1))
                v_x1 = flushV()
                wait("sync", "vV", v_x1)
                vsp = dma("sync", "dS", x1_sp[128 * tt:128 * tt + 128, :], ao[:])
                if "x1" in taps:
                    dma("sync", "dS", taps["x1"][128 * tt:128 * tt + 128, :], ao[:])
                wait("tensor", "gG", flushG())
                def w4(a):
                    vv = p.inc("vV")
                    p.on("vector", lambda E, a=a, tt=tt: E.tensor_tensor(
                        xn2[:, a * TOK + 512 * tt: a * TOK + 512 * tt + 512],
                        cb[:, a * 512:(a + 1) * 512], psB[:], ALU.mult).then_inc(sems["vV"], 1))
                ln_plain3(ao, aobf, w4, sqb3)
                wait("tensor", "vV", p.val("vV"))
            v_xn2 = p.val("vV")
            if "xn2" in taps:
                wait("sync", "vV", v_xn2)
                dma("sync", "dS", taps["xn2"], xn2[:])
            if "avo" in taps:
                dma("sync", "dS", taps["avo"], avo[:])
            es3.close()
    if PL >= 5:

            es4 = ExitStack()
            sb4 = lambda n, s_, d: es4.enter_context(nc.sbuf_tensor(n, s_, d))
            hbuf = sb4("hbuf", [128, NIC * 512], BF16)
            wgb = [sb4(f"wgb{i}", [128, NDC * 128], BF16) for i in range(2)]
            wub = [sb4(f"wub{i}", [128, NDC * 128], BF16) for i in range(2)]
            wdb = sb4("wdb", [128, 5 * 2048], BF16)
            sgu = sb4("sgu", [128, 4 * 512], F32)
            fo = sb4("fo", [128, NDC * 512], F32)
            sqb4 = sb4("sqb4", [128, 2 * 512], BF16)
            fobf = sb4("fobf", [128, NDC * 512], BF16)
            xr4 = [sb4(f"xr4{i}", [128, 512], F32) for i in range(2)]

            for tt in range(NTT):
                for i in range(NIC):
                    bw = i % 2
                    for (wbs, wdrm) in ((wgb, wgt_d), (wub, wut_d)):
                        st = freed.get(("wb", id(wbs), bw))
                        if st:
                            wait("gpsimd", st[0], st[1])
                        if tt == 0 and i < 2:
                            wait("gpsimd", "vV", v_xn2)
                        vw = dma("gpsimd", "dG", wbs[bw][:],
                                 wdrm[128 * i:128 * i + 128, :])
                        wait("vector", "dG", vw)
                        vg = p.inc("vV")
                        p.on("vector", lambda E, wbs=wbs, bw=bw: E.tensor_scalar(
                            wbs[bw][:], wbs[bw][:], -128.0, None, ALU.add).then_inc(sems["vV"], 1))
                        freed[("wconv", id(wbs), bw)] = vg
                    if P6N < 2:
                        continue
                    bank_g = (2 * i) % 4
                    bank_u = (2 * i + 1) % 4
                    wait("tensor", "vV", freed[("wconv", id(wgb), bw)])
                    wait("tensor", "vV", freed[("wconv", id(wub), bw)])
                    for (bk, wbs) in ((bank_g, wgb), (bank_u, wub)):
                        use_psS(bk)
                        for a in range(NDC):
                            p.on("tensor", lambda E, a=a, bk=bk, wbs=wbs, bw=bw, tt=tt: E.matmul(
                                psS[bk][:], wbs[bw][:, 128 * a:128 * a + 128],
                                xn2[:, a * TOK + 512 * tt: a * TOK + 512 * tt + 512],
                                start=(a == 0), stop=(a == NDC - 1)))
                    vmm = p.inc("tT")
                    p.on("tensor", ("INC_LAST", "tT"))
                    freed[("wb", id(wgb), bw)] = ("tT", vmm)
                    freed[("wb", id(wub), bw)] = ("tT", vmm)
                    sguoff = 1024 * (i % 2)
                    st3 = freed.get(("sgu", i % 2))
                    if st3:
                        wait("scalar", st3[0], st3[1])
                    wait("scalar", "tT", vmm)
                    p.on("scalar", lambda E, bank_g=bank_g, i=i, sguoff=sguoff: E.activation(
                        sgu[:, sguoff:sguoff + 512], psS[bank_g][:], AF.Silu,
                        scale=gsc[:, i:i + 1]))
                    vev = p.inc("aA")
                    p.on("scalar", lambda E, bank_u=bank_u, i=i, sguoff=sguoff: E.activation(
                        sgu[:, sguoff + 512:sguoff + 1024], psS[bank_u][:], AF.Identity,
                        scale=usc[:, i:i + 1]).then_inc(sems["aA"], 1))
                    bank_state[bank_g] = ("aA", vev)
                    bank_state[bank_u] = ("aA", vev)
                    wait("vector", "aA", vev)
                    vh = p.inc("vV")
                    p.on("vector", lambda E, i=i, sguoff=sguoff: E.tensor_tensor(
                        hbuf[:, 512 * i:512 * i + 512], sgu[:, sguoff:sguoff + 512],
                        sgu[:, sguoff + 512:sguoff + 1024], ALU.mult).then_inc(sems["vV"], 1))
                    freed[("sgu", i % 2)] = ("vV", vh + 1)
                    freed[("h", i)] = ("vV", vh)
                if "hb" in taps and tt == 0:
                    wait("sync", "vV", p.val("vV"))
                    dma("sync", "dS", taps["hb"], hbuf[:])

                for hd in (range(NDC) if P6N >= 3 else ()):
                    qdone = []
                    for q in range(4):
                        s5 = (4 * hd + q + 8 * tt * NDC) % 5
                        stq = freed.get(("wdbs", s5))
                        if stq:
                            wait("gpsimd", stq[0], stq[1])
                        vw = dma("gpsimd", "dG", wdb[:, 2048 * s5:2048 * (s5 + 1)],
                                 wdt_d[128 * (4 * hd + q):128 * (4 * hd + q) + 128, :])
                        wait("vector", "dG", vw)
                        vg = p.inc("vV")
                        p.on("vector", lambda E, s5=s5: E.tensor_scalar(
                            wdb[:, 2048 * s5:2048 * (s5 + 1)], wdb[:, 2048 * s5:2048 * (s5 + 1)],
                            -128.0, None, ALU.add).then_inc(sems["vV"], 1))
                        qdone.append(vg)
                    bank = hd % 4
                    use_psS(bank)
                    for a in range(NIC):
                        if a % 16 == 0:
                            wait("tensor", "vV", qdone[a // 16])
                        if a == 0:
                            wait("tensor", "vV", freed[("h", NIC - 1)][1])
                        s5 = (4 * hd + a // 16 + 8 * tt * NDC) % 5
                        p.on("tensor", lambda E, a=a, bank=bank, s5=s5: E.matmul(
                            psS[bank][:], wdb[:, 2048 * s5 + 128 * (a % 16):2048 * s5 + 128 * (a % 16) + 128],
                            hbuf[:, 512 * a:512 * a + 512],
                            start=(a == 0), stop=(a == NIC - 1)))
                    vmm = p.inc("tT")
                    p.on("tensor", ("INC_LAST", "tT"))
                    for q in range(4):
                        s5 = (4 * hd + q + 8 * tt * NDC) % 5
                        freed[("wdbs", s5)] = ("tT", vmm)
                    wait("scalar", "tT", vmm)
                    vev = p.inc("aA")
                    p.on("scalar", lambda E, hd=hd, bank=bank: E.activation(
                        fo[:, hd * 512:(hd + 1) * 512], psS[bank][:], AF.Identity,
                        scale=dsc[:, hd:hd + 1]).then_inc(sems["aA"], 1))
                    bank_state[bank] = ("aA", vev)
                wait("tensor", "gG", p.val("gG"))
                if P6N < 4:
                    continue
                v_dn_all = p.val("tT")
                for hd in range(NDC):
                    wait("gpsimd", "aA", p.val("aA"))
                    wait("gpsimd", "tT", v_dn_all)
                    vg = p.inc("gG")
                    p.on("gpsimd", lambda E, hd=hd: E.tensor_copy(
                        fobf[:, hd * 512:(hd + 1) * 512],
                        fo[:, hd * 512:(hd + 1) * 512]).then_inc(sems["gG"], 1))
                wait("tensor", "gG", flushG())
                if P6N < 5:
                    continue

                def w5(a):
                    vv = p.inc("vV")
                    p.on("vector", lambda E, a=a: E.tensor_tensor(
                        fo[:, a * 512:(a + 1) * 512], fo[:, a * 512:(a + 1) * 512],
                        psB[:], ALU.mult).then_inc(sems["vV"], 1))

                def ln_plain5(xf, xbf, out_writer, sqbuf, cbl):
                    for a in range(NDC):
                        p.on("tensor", lambda E, a=a: E.matmul(
                            psL[:], onescol[:], xbf[:, a * 512:(a + 1) * 512],
                            start=(a == 0), stop=(a == NDC - 1)))
                    v_sum = p.inc("tT")
                    p.on("tensor", ("INC_LAST", "tT"))
                    wait("scalar", "tT", v_sum)
                    va = p.inc("aA")
                    p.on("scalar", lambda E: E.activation(
                        rowA[0:1, :], psL[:], AF.Identity,
                        scale=float(-1.0 / D)).then_inc(sems["aA"], 1))
                    release("psL", "aA", va)
                    wait_free("tensor", "psB")
                    wait("tensor", "aA", va)
                    vb = p.inc("tT")
                    p.on("tensor", lambda E: E.matmul(psB[:], onesrow[:], rowA[0:1, :],
                                                      start=True, stop=True).then_inc(sems["tT"], 1))
                    wait("vector", "tT", vb)
                    wait_free("tensor", "psO")
                    sq_rd = {}
                    for a in range(NDC):
                        vv = p.inc("vV")
                        p.on("vector", lambda E, a=a: E.tensor_tensor(
                            cbl[:, a * 512:(a + 1) * 512], xf[:, a * 512:(a + 1) * 512],
                            psB[:], ALU.add).then_inc(sems["vV"], 1))
                        wait("scalar", "vV", vv)
                        if a >= 2:
                            wait("scalar", "tT", sq_rd[a % 2])
                        vq = p.inc("aA")
                        p.on("scalar", lambda E, a=a: E.activation(
                            sqbuf[:, 512 * (a % 2):512 * (a % 2) + 512],
                            cbl[:, a * 512:(a + 1) * 512], AF.Square).then_inc(sems["aA"], 1))
                        wait("tensor", "aA", vq)
                        vt2 = p.inc("tT")
                        p.on("tensor", lambda E, a=a: E.matmul(
                            psO[0:1, :], onescol[:], sqbuf[:, 512 * (a % 2):512 * (a % 2) + 512],
                            start=(a == 0), stop=(a == NDC - 1)).then_inc(sems["tT"], 1))
                        sq_rd[a % 2] = vt2
                    v_cent = p.val("vV")
                    release("psB", "vV", v_cent)
                    vsq = p.val("tT")
                    wait("scalar", "tT", vsq)
                    p.on("scalar", lambda E: E.activation(rowB[0:1, :], psO[0:1, :], AF.Ln,
                                                          bias=epsv[0:1, 0:1], scale=float(1.0 / D)))
                    vr = p.inc("aA")
                    p.on("scalar", lambda E: E.activation(rowA[0:1, :], rowB[0:1, :], AF.Exp,
                                                          scale=-0.5).then_inc(sems["aA"], 1))
                    release("psO", "aA", vr)
                    wait_free("tensor", "psB")
                    wait("tensor", "aA", vr)
                    vb2 = p.inc("tT")
                    p.on("tensor", lambda E: E.matmul(psB[:], onesrow[:], rowA[0:1, :],
                                                      start=True, stop=True).then_inc(sems["tT"], 1))
                    wait("vector", "tT", vb2)
                    for a in range(NDC):
                        out_writer(a)
                    release("psB", "vV", p.val("vV"))

                ln_plain5(fo, fobf, w5, sqb4, fo)
                v_y5 = p.val("vV")
                if P6N < 6:
                    continue
                if P6N >= 7:
                    for a in range(NDC):
                        bx = a % 2
                        st = freed.get(("xr4", bx))
                        if st:
                            wait("sync", st[0], st[1])
                        if a < 2:
                            wait("sync", "dS", vsp)
                        vx1 = dma("sync", "dS", xr4[bx][:],
                                  x1_sp[128 * tt:128 * tt + 128, a * 512:(a + 1) * 512])
                        wait("vector", "dS", vx1)
                        vv1 = p.inc("vV")
                        p.on("vector", lambda E, a=a, bx=bx: E.tensor_tensor(
                            fo[:, a * 512:(a + 1) * 512], fo[:, a * 512:(a + 1) * 512],
                            xr4[bx][:], ALU.add).then_inc(sems["vV"], 1))
                        freed[("xr4", bx)] = ("vV", vv1 + 1)
                wait("sync", "vV", flushV())
                vst = dma("sync", "dS", out_d[128 * tt:128 * tt + 128, :], fo[:])
                freed[("fo",)] = ("dS", vst)
                if tt + 1 < NTT:
                    wait("scalar", "dS", vst)

            es4.close()

    wait("sync", "tT", p.val("tT"))
    wait("sync", "aA", p.val("aA"))
    wait("sync", "vV", p.val("vV"))
    wait("sync", "gG", p.val("gG"))
    wait("sync", "dS", p.val("dS"))
    wait("sync", "dG", p.val("dG"))

    global _LAST_PROG, _LAST_SEMS
    _LAST_PROG, _LAST_SEMS = p, sems
    with nc.Block() as block:
        for e in ENGINES:
            def runner(E, e=e):
                last = None
                for f in p.ops[e]:
                    if isinstance(f, tuple) and f[0] == "INC_LAST":
                        assert last is not None
                        last.then_inc(sems[f[1]], 1)
                        continue
                    r = f(E)
                    if r is not None:
                        last = r
            getattr(block, e)(runner)
    if PL >= 5:
        es.close()
    return nc


def _prep_inputs(inp):
    X = np.asarray(inp["L0_attn_X"])
    Wq = np.asarray(inp["L0_attn_Wq"]); bq = np.asarray(inp["L0_attn_bq"])
    Wk = np.asarray(inp["L0_attn_Wk"]); bk = np.asarray(inp["L0_attn_bk"])
    Wv = np.asarray(inp["L0_attn_Wv"]); bv = np.asarray(inp["L0_attn_bv"])
    Wo = np.asarray(inp["L0_attn_Wo"]); bo = np.asarray(inp["L0_attn_bo"])
    lng = np.asarray(inp["L0_attn_ln_g"]); lnb = np.asarray(inp["L0_attn_ln_b"])
    gw = np.asarray(inp["L0_ffn_gate_w"]); gs = np.asarray(inp["L0_ffn_gate_scale"])
    uw = np.asarray(inp["L0_ffn_up_w"]); us = np.asarray(inp["L0_ffn_up_scale"])
    dw = np.asarray(inp["L0_ffn_down_w"]); ds = np.asarray(inp["L0_ffn_down_scale"])

    def u8_of(a):
        return np.ascontiguousarray(a.view(np.uint8).reshape(a.shape + (4,))[..., 0])

    gw8 = u8_of(gw).reshape(NIC, 128, NDC, 128)
    wgt = np.ascontiguousarray(gw8.transpose(0, 3, 2, 1)).reshape(NIC * 128, NDC * 128)
    uw8 = u8_of(uw).reshape(NIC, 128, NDC, 128)
    wut = np.ascontiguousarray(uw8.transpose(0, 3, 2, 1)).reshape(NIC * 128, NDC * 128)
    dw8 = u8_of(dw).reshape(NDC, 128, 4, 16, 128)
    wdt = np.ascontiguousarray(dw8.transpose(0, 2, 4, 3, 1)).reshape(NDC * 4 * 128, 16 * 128)

    bf = ml_dtypes.bfloat16
    cm = np.zeros((128, 4 * 512), np.float32)
    kk = np.arange(128)[:, None]; qq = np.arange(512)[None, :]
    for r in range(4):
        cm[:, 512 * r:512 * r + 512] = (128 * r + kk <= qq)
    cmask = cm.astype(bf)
    onescol = np.ones((128, 1), np.float32).astype(bf)
    onesrow = np.ones((1, 128), np.float32)
    epsv = np.array([[1e-5, 1e-30]], np.float32)
    bo_l = np.ascontiguousarray(bo.reshape(NDC, 128).T)
    lng_l = np.ascontiguousarray(lng.reshape(NDC, 128).T)
    lnb_l = np.ascontiguousarray(lnb.reshape(NDC, 128).T)
    gs_l = np.ascontiguousarray(gs.reshape(NIC, 128).T)
    us_l = np.ascontiguousarray(us.reshape(NIC, 128).T)
    ds_l = np.ascontiguousarray(ds.reshape(NDC, 128).T)

    wo_t = np.ascontiguousarray(
        Wo.reshape(NDC, 128, NDC, 128).transpose(2, 1, 0, 3)).reshape(NDC * 128, NDC * 128)

    def qkv_tile(w, hsl):
        return np.ascontiguousarray(
            w[:, hsl].reshape(NDC, 128, 256).transpose(1, 0, 2)).reshape(128, NDC * 256)

    in_maps = []
    for c in range(NCORES):
        hsl = slice(256 * c, 256 * c + 256)
        bqk = np.zeros((128, 4), np.float32)
        bqk[:, 0:2] = bq[hsl].reshape(2, 128).T
        bqk[:, 2:4] = bk[hsl].reshape(2, 128).T
        Xc = X[0, TOK * c:TOK * (c + 1), :]
        xt_t = np.ascontiguousarray(
            Xc.reshape(NTT, 512, NDC, 128).transpose(0, 3, 2, 1)).reshape(NTT * 128, NDC * 512)
        in_maps.append(dict(
            xt=xt_t,
            wq=qkv_tile(Wq, hsl),
            wk=qkv_tile(Wk, hsl),
            wv=qkv_tile(Wv, hsl),
            wo=wo_t, bqk=bqk, bv=np.ascontiguousarray(bv[hsl])[None, :],
            bo=bo_l, lng=lng_l, lnb=lnb_l,
            wgt=wgt, wut=wut, wdt=wdt, gsc=gs_l, usc=us_l, dsc=ds_l,
            cmaskb=cmask, onescol=onescol, onesrow=onesrow, epsv=epsv,
        ))
    return in_maps


def kernel(**inp):
    in_maps = _prep_inputs(inp)
    nc = build_nc()
    res = run_bass_kernel_spmd(nc, in_maps, core_ids=list(range(NCORES)))
    cores = []
    for c in range(NCORES):
        o = np.asarray(res.results[c]["out"]).reshape(NTT, 128, NDC, 512)
        cores.append(o.transpose(0, 3, 2, 1).reshape(TOK, D))
    ret = np.ascontiguousarray(np.concatenate(cores, axis=0))[None]
    if _DEBUG_TAPS:
        kernel.debug_results = res.results
    return ret


# version 23
# speedup vs baseline: 2.8384x; 1.0474x over previous
import numpy as np
import ml_dtypes
import concourse.bass as bass
import concourse.mybir as mybir
from concourse.bass_utils import run_bass_kernel_spmd

F32 = mybir.dt.float32
BF16 = mybir.dt.bfloat16
U8 = mybir.dt.uint8
AF = mybir.ActivationFunctionType
ALU = mybir.AluOpType
AX = mybir.AxisListType

NCORES = 8
D = 2048
S = 8192
TOK = S // NCORES
NDC = D // 128
NTT = TOK // 512
HPC = 2
NQC = S // 512
NKC = S // 128
I_FFN = 8192
NIC = I_FFN // 128
ISQ = float(1.0 / np.sqrt(128.0))
ENGINES = ("sync", "tensor", "scalar", "vector", "gpsimd")

_LAST_PROG = None
_LAST_SEMS = None
_DETECT_RACES = True
_PHASE_LIMIT = 5
_P2_MODE = "all"
_P1_MIN = False
_P6N = 7
_DEBUG_TAPS = []


class Prog:

    def __init__(self):
        self.ops = {e: [] for e in ENGINES}
        self.cnt = {}

    def on(self, eng, fn):
        self.ops[eng].append(fn)

    def inc(self, sem, n=1):
        self.cnt[sem] = self.cnt.get(sem, 0) + n
        return self.cnt[sem]

    def val(self, sem):
        return self.cnt.get(sem, 0)


def build_nc():
    PL = _PHASE_LIMIT
    P2M = _P2_MODE
    P1MIN = _P1_MIN
    P6N = _P6N
    nc = bass.Bass("TRN2", target_bir_lowering=False,
                   detect_race_conditions=_DETECT_RACES)

    xt_d = nc.dram_tensor("xt", [NTT * 128, NDC * 512], F32, kind="ExternalInput").ap()
    wq_d = nc.dram_tensor("wq", [128, NDC * 256], F32, kind="ExternalInput").ap()
    wk_d = nc.dram_tensor("wk", [128, NDC * 256], F32, kind="ExternalInput").ap()
    wv_d = nc.dram_tensor("wv", [128, NDC * 256], F32, kind="ExternalInput").ap()
    wo_d = nc.dram_tensor("wo", [NDC * 128, NDC * 128], F32, kind="ExternalInput").ap()
    bqk_d = nc.dram_tensor("bqk", [128, 4], F32, kind="ExternalInput").ap()
    bv_d = nc.dram_tensor("bv", [1, 256], F32, kind="ExternalInput").ap()
    bo_d = nc.dram_tensor("bo", [128, NDC], F32, kind="ExternalInput").ap()
    lng_d = nc.dram_tensor("lng", [128, NDC], F32, kind="ExternalInput").ap()
    lnb_d = nc.dram_tensor("lnb", [128, NDC], F32, kind="ExternalInput").ap()
    wgt_d = nc.dram_tensor("wgt", [NIC * 128, NDC * 128], U8, kind="ExternalInput").ap()
    wut_d = nc.dram_tensor("wut", [NIC * 128, NDC * 128], U8, kind="ExternalInput").ap()
    wdt_d = nc.dram_tensor("wdt", [NDC * 4 * 128, 16 * 128], U8, kind="ExternalInput").ap()
    gsc_d = nc.dram_tensor("gsc", [128, NIC], F32, kind="ExternalInput").ap()
    usc_d = nc.dram_tensor("usc", [128, NIC], F32, kind="ExternalInput").ap()
    dsc_d = nc.dram_tensor("dsc", [128, NDC], F32, kind="ExternalInput").ap()
    cmask_d = nc.dram_tensor("cmaskb", [128, 4 * 512], BF16, kind="ExternalInput").ap()
    onescol_d = nc.dram_tensor("onescol", [128, 1], BF16, kind="ExternalInput").ap()
    onesrow_d = nc.dram_tensor("onesrow", [1, 128], F32, kind="ExternalInput").ap()
    eps_d = nc.dram_tensor("epsv", [1, 2], F32, kind="ExternalInput").ap()

    out_d = nc.dram_tensor("out", [NTT * 128, NDC * 512], F32, kind="ExternalOutput").ap()

    xn_con2 = [nc.dram_tensor(f"xn_con{i}", [128, NDC * 512], BF16, kind="Internal").ap()
               for i in range(NTT)]
    xn_gat2 = [nc.dram_tensor(f"xn_gat{i}", [NCORES * 128, NDC * 512], BF16,
                              kind="Internal", addr_space="Shared").ap()
               for i in range(NTT)]
    av_con2 = [nc.dram_tensor(f"av_con{i}", [128, S], BF16, kind="Internal").ap()
               for i in range(HPC)]
    av_gat2 = [nc.dram_tensor(f"av_gat{i}", [NCORES * 128, S], BF16,
                              kind="Internal", addr_space="Shared").ap()
               for i in range(HPC)]
    x1_sp = nc.dram_tensor("x1_sp", [NTT * 128, NDC * 512], F32, kind="Internal").ap()

    taps = {}
    for t in _DEBUG_TAPS:
        shapes = {"xn": ([128, NDC * TOK], BF16), "qt": ([128, HPC * S], BF16),
                  "kt": ([128, HPC * S], BF16), "vt": ([128, HPC * S], BF16),
                  "avo": ([128, NDC * TOK], BF16), "x1": ([NTT * 128, NDC * 512], F32),
                  "xn2": ([128, NDC * TOK], BF16), "ao": ([128, NDC * 512], F32),
                  "negm": ([128, HPC * NQC], F32), "hb": ([128, NIC * 512], BF16)}
        sh, dt = shapes[t]
        taps[t] = nc.dram_tensor("tap_" + t, sh, dt, kind="ExternalOutput").ap()

    from contextlib import ExitStack
    es = ExitStack()
    sb = lambda n, s_, d: es.enter_context(nc.sbuf_tensor(n, s_, d))
    psb = lambda n, s_, d: es.enter_context(nc.psum_tensor(n, s_, d))

    sems = {}
    for name in ("dS", "dG", "tT", "aA", "vV", "gG", "cC"):
        sems[name] = es.enter_context(nc.semaphore(name))

    onescol = sb("onescol_s", [128, 1], BF16)
    onesrow = sb("onesrow_s", [1, 128], F32)
    epsv = sb("eps_s", [1, 2], F32)
    bqk = sb("bqk_s", [128, 4], F32)
    bo_s = sb("bo_s", [128, NDC], F32)
    lng = sb("lng_s", [128, NDC], F32)
    lnb = sb("lnb_s", [128, NDC], F32)
    gsc = sb("gsc_s", [128, NIC], F32)
    usc = sb("usc_s", [128, NIC], F32)
    dsc = sb("dsc_s", [128, NDC], F32)
    negm = sb("negm_s", [128, HPC * NQC], F32)
    vflush = sb("vflush_s", [1, 8], F32)
    gflush = sb("gflush_s", [1, 8], F32)
    rowA = sb("rowA_s", [1, 512], F32)
    rowB = sb("rowB_s", [1, 512], F32)
    mhatln = sb("mhatln_s", [1, NQC], F32)
    mxkln = sb("mxkln_s", [1, NQC], F32)
    mexp = sb("mexp_s", [128, NQC], F32)

    psS = [psb(f"psS{i}", [128, 512], F32) for i in range(4)]
    psO2 = [psb(f"psOx{i}", [128, 512], F32) for i in range(2)]
    psO = psO2[0]
    psL = psb("psL", [1, 512], F32)
    psB = psb("psB", [128, 512], F32)

    p = Prog()
    freed = {}

    def wait_free(eng, key):
        st = freed.get(key)
        if st is not None:
            sem, v = st
            p.on(eng, lambda E, s=sems[sem], vv=v: (E.wait_ge(s, vv), None)[1])

    def release(key, sem, val):
        freed[key] = (sem, val)

    def emit(eng, f):
        p.on(eng, f)

    def dma(eng, sem_name, out_ap, in_ap, **kw):
        v = p.inc(sem_name, 16)
        p.on(eng, lambda E, o=out_ap, i=in_ap, s=sems[sem_name], kw=kw:
             E.dma_start(out=o, in_=i, **kw).then_inc(s, 16))
        return v

    def flushV():
        v = p.inc("vV")
        p.on("vector", lambda E: E.memset(vflush[0:1, 0:1], 0.0).then_inc(sems["vV"], 1))
        return v

    def flushG():
        v = p.inc("gG")
        p.on("gpsimd", lambda E: E.memset(gflush[0:1, 0:1], 0.0).then_inc(sems["gG"], 1))
        return v

    def wait(eng, sem_name, val):
        if val > 0:
            p.on(eng, lambda E, s=sems[sem_name], v=val: (E.wait_ge(s, v), None)[1])

    for dst, src in ((onescol, onescol_d), (onesrow, onesrow_d), (epsv, eps_d),
                     (bqk, bqk_d), (bo_s, bo_d), (lng, lng_d),
                     (lnb, lnb_d), (gsc, gsc_d), (usc, usc_d), (dsc, dsc_d)):
        dma("sync", "dS", dst[:], src)
    v_consts = p.val("dS")

    es1 = ExitStack()
    sb1 = lambda n, s_, d: es1.enter_context(nc.sbuf_tensor(n, s_, d))
    xbuf = sb1("xbuf", [128, NDC * 512], F32)
    xbb = sb1("xbb", [128, NDC * 512], BF16)
    cb1 = sb1("cb1", [128, NDC * 512], F32)
    cb2 = sb1("cb2", [128, NDC * 512], F32)
    xnb = sb1("xnb", [128, NDC * TOK], BF16)
    sqb = sb1("sqb", [128, 2 * 512], BF16)

    def ln_plain(xf, xf_off, xbf, xbf_off, ncols, out_writer, aff=None):
        wait_free("tensor", "psL")
        for a in range(NDC):
            vs = p.inc("tT")
            p.on("tensor", lambda E, a=a: E.matmul(
                psL[:, 0:ncols], onescol[:],
                xbf[:, xbf_off + a * ncols: xbf_off + (a + 1) * ncols],
                start=(a == 0), stop=(a == NDC - 1)).then_inc(sems["tT"], 1))
        v_sum = p.val("tT")
        wait("scalar", "tT", v_sum)
        va = p.inc("aA")
        p.on("scalar", lambda E: E.activation(
            rowA[0:1, 0:ncols], psL[:, 0:ncols], AF.Identity,
            scale=float(-1.0 / D)).then_inc(sems["aA"], 1))
        release("psL", "aA", va)
        wait_free("tensor", "psB")
        wait("tensor", "aA", va)
        vb = p.inc("tT")
        p.on("tensor", lambda E: E.matmul(
            psB[:, 0:ncols], onesrow[:], rowA[0:1, 0:ncols],
            start=True, stop=True).then_inc(sems["tT"], 1))
        wait("vector", "tT", vb)
        wait_free("tensor", "psO")
        sq_rd = {}
        for a in range(NDC):
            vv = p.inc("vV")
            p.on("vector", lambda E, a=a: E.tensor_tensor(
                cb1[:, a * ncols:(a + 1) * ncols],
                xf[:, xf_off + a * ncols: xf_off + (a + 1) * ncols],
                psB[:, 0:ncols], ALU.add).then_inc(sems["vV"], 1))
            wait("scalar", "vV", vv)
            if a >= 2:
                wait("scalar", "tT", sq_rd[a % 2])
            vq = p.inc("aA")
            p.on("scalar", lambda E, a=a: E.activation(
                sqb[:, 512 * (a % 2):512 * (a % 2) + ncols],
                cb1[:, a * ncols:(a + 1) * ncols], AF.Square).then_inc(sems["aA"], 1))
            wait("tensor", "aA", vq)
            vt2 = p.inc("tT")
            p.on("tensor", lambda E, a=a: E.matmul(
                psO[0:1, 0:ncols], onescol[:],
                sqb[:, 512 * (a % 2):512 * (a % 2) + ncols],
                start=(a == 0), stop=(a == NDC - 1)).then_inc(sems["tT"], 1))
            sq_rd[a % 2] = vt2
        v_cent = p.val("vV")
        v_sq = p.val("tT")
        release("psB", "vV", v_cent)
        wait("scalar", "tT", v_sq)
        p.on("scalar", lambda E: E.activation(
            rowB[0:1, 0:ncols], psO[0:1, 0:ncols], AF.Ln,
            bias=epsv[0:1, 0:1], scale=float(1.0 / D)))
        vr = p.inc("aA")
        p.on("scalar", lambda E: E.activation(
            rowA[0:1, 0:ncols], rowB[0:1, 0:ncols], AF.Exp,
            scale=-0.5).then_inc(sems["aA"], 1))
        release("psO", "aA", vr)
        wait_free("tensor", "psB")
        wait("tensor", "aA", vr)
        vb2 = p.inc("tT")
        p.on("tensor", lambda E: E.matmul(
            psB[:, 0:ncols], onesrow[:], rowA[0:1, 0:ncols],
            start=True, stop=True).then_inc(sems["tT"], 1))
        wait("vector", "tT", vb2)
        for a in range(NDC):
            out_writer(a)
        vfin = p.val("vV")
        release("psB", "vV", vfin)
        return vfin

    v_xt_tiles = []
    for tt in range(NTT):
        if tt > 0:
            wait("sync", "vV", p.val("vV"))
            wait("sync", "tT", p.val("tT"))
            wait("gpsimd", "vV", p.val("vV"))
            wait("gpsimd", "tT", p.val("tT"))
        src = xt_d[128 * tt:128 * tt + 128, :]
        vx = dma("sync", "dS", xbuf[:], src)
        vxb = dma("gpsimd", "dG", xbb[:], src)
        wait("tensor", "dS", vx)
        wait("tensor", "dG", vxb)
        wait("vector", "dS", vx)
        wait("vector", "dG", vxb)
        wait("scalar", "dS", vx)

        if P1MIN:
            for a in range(NDC):
                vv = p.inc("vV")
                p.on("vector", lambda E, a=a, tt=tt: E.tensor_copy(
                    xnb[:, a * TOK + 512 * tt: a * TOK + 512 * tt + 512],
                    xbb[:, a * 512:(a + 1) * 512]).then_inc(sems["vV"], 1))
            v_xt_tiles.append(p.val("vV"))
            continue
        def w1(a):
            vv = p.inc("vV")
            p.on("vector", lambda E, a=a: E.tensor_tensor(
                cb2[:, a * 512:(a + 1) * 512], cb1[:, a * 512:(a + 1) * 512],
                psB[:, 0:512], ALU.mult).then_inc(sems["vV"], 1))
            wait("gpsimd", "vV", vv)
            vg = p.inc("gG")
            p.on("gpsimd", lambda E, a=a: E.tensor_copy(
                xbb[:, a * 512:(a + 1) * 512],
                cb2[:, a * 512:(a + 1) * 512]).then_inc(sems["gG"], 1))
        ln_plain(xbuf, 0, xbb, 0, 512, w1)
        v_y1bf = flushG()
        wait("tensor", "gG", v_y1bf)

        def w2(a):
            vv0 = p.inc("vV")
            p.on("vector", lambda E, a=a: E.tensor_tensor(
                cb1[:, a * 512:(a + 1) * 512], cb1[:, a * 512:(a + 1) * 512],
                psB[:, 0:512], ALU.mult).then_inc(sems["vV"], 1))
            vv1 = p.inc("vV")
            p.on("vector", lambda E, a=a, tt=tt: E.tensor_scalar(
                xnb[:, tt * NDC * 512 + a * 512: tt * NDC * 512 + a * 512 + 512],
                cb1[:, a * 512:(a + 1) * 512],
                lng[:, a:a + 1], lnb[:, a:a + 1],
                ALU.mult, ALU.add).then_inc(sems["vV"], 1))
        ln_plain(cb2, 0, xbb, 0, 512, w2)
        v_xn_tt = p.val("vV")
        v_xt_tiles.append(v_xn_tt)
        wait("sync", "vV", flushV())
        v_con = dma("sync", "dS", xn_con2[tt],
                    xnb[:, tt * NDC * 512:(tt + 1) * NDC * 512])
        wait("gpsimd", "dS", v_con)
        if tt > 0:
            wait("gpsimd", "cC", p.val("cC"))
        p.on("gpsimd", lambda E, tt=tt: E.collective_compute(
            "AllGather", ALU.bypass, replica_groups=[list(range(NCORES))],
            ins=[xn_con2[tt]], outs=[xn_gat2[tt]],
        ).then_inc(sems["cC"], 1))
        p.inc("cC")

    vcc1 = p.val("cC")
    if "xn" in taps:
        dma("sync", "dS", taps["xn"], xnb[:])

    es1.close()

    if PL >= 2:
            es2 = ExitStack()
            sb2 = lambda n, s_, d: es2.enter_context(nc.sbuf_tensor(n, s_, d))
            wqb = sb2("wqb", [128, NDC * 256], BF16)
            wkb = sb2("wkb", [128, NDC * 256], BF16)
            wvb = sb2("wvb", [128, NDC * 256], BF16)
            xg = [sb2(f"xg{i}", [128, NDC * 512], BF16) for i in range(2)]
            qt = sb2("qt", [128, HPC * S], BF16)
            kt = sb2("kt", [128, HPC * S], BF16)
            vt = sb2("vt", [128, HPC * S], BF16)
            pbuf = sb2("pbuf", [128, 4 * 512], BF16)
            avst = sb2("avst", [128, 512], BF16)
            sqn = sb2("sqn", [128, 2 * 512], BF16)
            nrow = sb2("nrow", [1, 2 * 512], F32)
            mkparts = sb2("mkparts", [1, NQC], F32)
            cmask = sb2("cmask_s", [128, 4 * 512], BF16)
            bvrow = sb2("bvrow_s", [1, 256], F32)
            bvb = sb2("bvb_s", [128, 256], F32)
            dma("sync", "dS", cmask[:], cmask_d)
            v_bv = dma("sync", "dS", bvrow[:], bv_d)
            wait_free("tensor", "psB")
            wait("tensor", "dS", v_bv)
            vbv = p.inc("tT")
            p.on("tensor", lambda E: E.matmul(psB[:, 0:256], onesrow[:], bvrow[:],
                                              start=True, stop=True).then_inc(sems["tT"], 1))
            wait("scalar", "tT", vbv)
            vbv2 = p.inc("aA")
            p.on("scalar", lambda E: E.activation(bvb[:], psB[:, 0:256],
                                                  AF.Identity).then_inc(sems["aA"], 1))
            release("psB", "aA", vbv2)

            if P2M == "cc":
                wait("gpsimd", "cC", vcc1)
            for dst, src in (((wqb, wq_d), (wkb, wk_d), (wvb, wv_d)) if P2M not in ("xg", "cc") else ()):
                wait("gpsimd", "dS", v_con)
                dma("gpsimd", "dG", dst[:], src)
            v_wqkv = p.val("dG")

            bank_state = {}

            def use_psS(i, eng="tensor"):
                st = bank_state.get(i)
                if st:
                    wait(eng, st[0], st[1])

            xg_free = [None, None]
            for half in range(2):
                for r in (range(NCORES) if P2M not in ("w", "cc") else ()):
                    b = (half * NCORES + r) % 2
                    if xg_free[b]:
                        wait("sync", xg_free[b][0], xg_free[b][1])
                    wait("sync", "cC", half + 1)
                    v = dma("sync", "dS", xg[b][:],
                            xn_gat2[half][128 * r:128 * (r + 1), :])
                    wait("tensor", "dS", v)
                    if half == 0 and r == 0:
                        wait("tensor", "dG", v_wqkv)
                    g = 2 * r + half
                    for (wb, dstbuf) in (((wqb, qt), (wkb, kt)) if P2M in ("qk", "all") else ()):
                        for h in range(2):
                            bank = (2 * g + h) % 4
                            use_psS(bank)
                            vt_mm = p.inc("tT")
                            for a in range(NDC):
                                p.on("tensor", lambda E, a=a, h=h, wb=wb, bank=bank, b=b: (
                                    E.matmul(
                                        psS[bank][:], wb[:, a * 256 + 128 * h: a * 256 + 128 * h + 128],
                                        xg[b][:, a * 512:(a + 1) * 512],
                                        start=(a == 0), stop=(a == NDC - 1)
                                    ).then_inc(sems["tT"], 1) if a == NDC - 1 else
                                    E.matmul(
                                        psS[bank][:], wb[:, a * 256 + 128 * h: a * 256 + 128 * h + 128],
                                        xg[b][:, a * 512:(a + 1) * 512],
                                        start=(a == 0), stop=(a == NDC - 1))))
                            wait("scalar", "tT", vt_mm)
                            bcol = (0 if dstbuf is qt else 2) + h
                            va2 = p.inc("aA")
                            p.on("scalar", lambda E, bank=bank, h=h, g=g, dstbuf=dstbuf, bcol=bcol:
                                 E.activation(dstbuf[:, h * S + 512 * g: h * S + 512 * g + 512],
                                              psS[bank][:], AF.Identity,
                                              bias=bqk[:, bcol:bcol + 1]).then_inc(sems["aA"], 1))
                            bank_state[bank] = ("aA", va2)
                    for sub in (range(4) if P2M in ("v", "all") else ()):
                        bank = sub % 4
                        use_psS(bank)
                        vt_mm = p.inc("tT")
                        for a in range(NDC):
                            p.on("tensor", lambda E, a=a, sub=sub, bank=bank, b=b: (
                                E.matmul(
                                    psS[bank][:, 0:256],
                                    xg[b][:, a * 512 + 128 * sub: a * 512 + 128 * sub + 128],
                                    wvb[:, a * 256:(a + 1) * 256],
                                    start=(a == 0), stop=(a == NDC - 1)
                                ).then_inc(sems["tT"], 1) if a == NDC - 1 else
                                E.matmul(
                                    psS[bank][:, 0:256],
                                    xg[b][:, a * 512 + 128 * sub: a * 512 + 128 * sub + 128],
                                    wvb[:, a * 256:(a + 1) * 256],
                                    start=(a == 0), stop=(a == NDC - 1))))
                        wait("vector", "tT", vt_mm)
                        if half == 0 and r == 0 and sub == 0:
                            wait("vector", "aA", vbv2)
                        kc = 4 * g + sub
                        vv2 = p.inc("vV")
                        p.on("vector", lambda E, bank=bank, kc=kc: E.tensor_tensor(
                            vt[:, 256 * kc:256 * kc + 256], psS[bank][:, 0:256], bvb[:],
                            ALU.add).then_inc(sems["vV"], 1))
                        bank_state[bank] = ("vV", vv2)
                    xg_free[b] = ("tT", p.val("tT"))

            if "qt" in taps:
                wait("sync", "aA", p.val("aA"))
                dma("sync", "dS", taps["qt"], qt[:])
                dma("sync", "dS", taps["kt"], kt[:])
            if "vt" in taps:
                wait("sync", "vV", p.val("vV"))
                dma("sync", "dS", taps["vt"], vt[:])

    if PL >= 3:
            v_qk_done = p.val("aA")
            v_prev_negm = 0
            sqn_rd = {}
            for h in range(HPC):
                nrow_free = [None, None]
                for t in range(2 * (S // 512)):
                    srcbuf, col = (qt, 512 * t) if t < S // 512 else (kt, 512 * (t - S // 512))
                    col += h * S
                    wait("scalar", "aA", v_qk_done)
                    if h > 0 and t == 0:
                        wait("scalar", "vV", v_prev_negm)
                    if sqn_rd.get(t % 2):
                        wait("scalar", "tT", sqn_rd[t % 2])
                    vq = p.inc("aA")
                    p.on("scalar", lambda E, srcbuf=srcbuf, col=col, t=t: E.activation(
                        sqn[:, 512 * (t % 2):512 * (t % 2) + 512],
                        srcbuf[:, col:col + 512], AF.Square).then_inc(sems["aA"], 1))
                    wait("tensor", "aA", vq)
                    use_psS(t % 4)
                    vm = p.inc("tT")
                    p.on("tensor", lambda E, t=t: E.matmul(
                        psS[t % 4][0:1, :], onescol[:],
                        sqn[:, 512 * (t % 2):512 * (t % 2) + 512],
                        start=True, stop=True).then_inc(sems["tT"], 1))
                    sqn_rd[t % 2] = vm
                    wait("scalar", "tT", vm)
                    if nrow_free[t % 2]:
                        wait("scalar", *nrow_free[t % 2])
                    sc = 1.0 if t < S // 512 else (1.0 / 128.0)
                    va3 = p.inc("aA")
                    p.on("scalar", lambda E, sc=sc, t=t: E.activation(
                        nrow[0:1, 512 * (t % 2):512 * (t % 2) + 512], psS[t % 4][0:1, :],
                        AF.Ln, bias=epsv[0:1, 1:2], scale=sc).then_inc(sems["aA"], 1))
                    bank_state[t % 4] = ("aA", va3)
                    wait("vector", "aA", va3)
                    vmx_t = p.inc("vV")
                    dst, colq = (mhatln, t) if t < S // 512 else (mkparts, t - S // 512)
                    p.on("vector", lambda E, dst=dst, colq=colq, t=t: E.reduce_max(
                        dst[0:1, colq:colq + 1],
                        nrow[0:1, 512 * (t % 2):512 * (t % 2) + 512],
                        axis=AX.X).then_inc(sems["vV"], 1))
                    nrow_free[t % 2] = ("vV", vmx_t)
                for qc in range(NQC):
                    p.on("vector", lambda E, qc=qc: E.reduce_max(
                        mxkln[0:1, qc:qc + 1], mkparts[0:1, :], axis=AX.X))
                vmx = p.inc("vV")
                p.on("vector", ("INC_LAST", "vV"))
                wait_free("tensor", "psB")
                wait("tensor", "vV", vmx)
                p.on("tensor", lambda E: E.matmul(psB[:, 0:NQC], onesrow[:], mhatln[:],
                                                  start=True, stop=False))
                vb3 = p.inc("tT")
                p.on("tensor", lambda E: E.matmul(psB[:, 0:NQC], onesrow[:], mxkln[:],
                                                  start=False, stop=True).then_inc(sems["tT"], 1))
                wait("scalar", "tT", vb3)
                vme = p.inc("aA")
                p.on("scalar", lambda E: E.activation(mexp[:], psB[:, 0:NQC], AF.Exp,
                                                      scale=0.5).then_inc(sems["aA"], 1))
                release("psB", "aA", vme)
                wait("vector", "aA", vme)
                vng = p.inc("vV")
                p.on("vector", lambda E, h=h: E.tensor_scalar_mul(
                    negm[:, NQC * h:NQC * (h + 1)], mexp[:], -1.0).then_inc(sems["vV"], 1))
                v_prev_negm = vng
            v_negm = p.val("vV")
            if "negm" in taps:
                wait("sync", "vV", v_negm)
                dma("sync", "dS", taps["negm"], negm[:])

            DEPTH = 4
            v_av_con = []
            for h in range(HPC):
                for qc in range(NQC):
                    J = 4 * (qc + 1)
                    qof = h * S + 512 * qc
                    psOb = psO2[(h * NQC + qc) % 2]
                    sdone = {}
                    expdone = {}
                    maskdone = {}
                    avdone = {}

                    def emit_S(j):
                        bank = j % DEPTH
                        use_psS(bank)
                        if j < DEPTH and qc == 0 and h == 0:
                            wait("tensor", "vV", v_negm)
                        vs = p.inc("tT")
                        p.on("tensor", lambda E, j=j, bank=bank, qof=qof, h=h: E.matmul(
                            psS[bank][:], kt[:, h * S + 128 * j: h * S + 128 * j + 128],
                            qt[:, qof:qof + 512], start=True, stop=True).then_inc(sems["tT"], 1))
                        sdone[j] = vs

                    for j in range(min(DEPTH, J)):
                        emit_S(j)
                    for j in range(J):
                        bank = j % DEPTH
                        wait("scalar", "tT", sdone[j])
                        if j - DEPTH >= 0:
                            wait("scalar", "tT", avdone[j - DEPTH])
                        ve = p.inc("aA")
                        p.on("scalar", lambda E, j=j, bank=bank, h=h, qc=qc: E.activation(
                            pbuf[:, 512 * (j % DEPTH):512 * (j % DEPTH) + 512],
                            psS[bank][:], AF.Exp,
                            bias=negm[:, NQC * h + qc: NQC * h + qc + 1],
                            scale=ISQ).then_inc(sems["aA"], 1))
                        expdone[j] = ve
                        bank_state[bank] = ("aA", ve)
                        r = j - 4 * qc
                        if r >= 0:
                            wait("vector", "aA", ve)
                            vmk = p.inc("vV")
                            p.on("vector", lambda E, j=j, r=r: E.tensor_tensor(
                                pbuf[:, 512 * (j % DEPTH):512 * (j % DEPTH) + 512],
                                pbuf[:, 512 * (j % DEPTH):512 * (j % DEPTH) + 512],
                                cmask[:, 512 * r:512 * r + 512], ALU.mult).then_inc(sems["vV"], 1))
                            maskdone[j] = vmk
                            wait("tensor", "vV", vmk)
                        else:
                            wait("tensor", "aA", ve)
                        if j == 0:
                            wait_free("tensor", "psL")
                            wait_free("tensor", ("psO", (h * NQC + qc) % 2))
                        p.on("tensor", lambda E, j=j, first=(j == 0), last=(j == J - 1): E.matmul(
                            psL[:], onescol[:], pbuf[:, 512 * (j % DEPTH):512 * (j % DEPTH) + 512],
                            start=first, stop=last))
                        vav = p.inc("tT")
                        p.on("tensor", lambda E, j=j, h=h, psOb=psOb, first=(j == 0), last=(j == J - 1): E.matmul(
                            psOb[:], vt[:, 256 * j + 128 * h: 256 * j + 128 * h + 128],
                            pbuf[:, 512 * (j % DEPTH):512 * (j % DEPTH) + 512],
                            start=first, stop=last).then_inc(sems["tT"], 1))
                        avdone[j] = vav
                        if j + DEPTH < J:
                            emit_S(j + DEPTH)
                    wait("scalar", "tT", avdone[J - 1])
                    p.on("scalar", lambda E: E.activation(rowB[0:1, :], psL[:], AF.Ln))
                    vrl = p.inc("aA")
                    p.on("scalar", lambda E: E.activation(rowA[0:1, :], rowB[0:1, :], AF.Exp,
                                                          scale=-1.0).then_inc(sems["aA"], 1))
                    release("psL", "aA", vrl)
                    wait_free("tensor", "psB")
                    wait("tensor", "aA", vrl)
                    vbc = p.inc("tT")
                    p.on("tensor", lambda E: E.matmul(psB[:], onesrow[:], rowA[0:1, :],
                                                      start=True, stop=True).then_inc(sems["tT"], 1))
                    wait("vector", "tT", vbc)
                    p.on("vector", lambda E: E.tensor_copy(sqn.ap().bitcast(F32)[:, 0:512], psB[:]))
                    vfin = p.inc("vV")
                    p.on("vector", lambda E, psOb=psOb: E.tensor_tensor(
                        avst[:], psOb[:], sqn.ap().bitcast(F32)[:, 0:512],
                        ALU.mult).then_inc(sems["vV"], 1))
                    release("psB", "vV", vfin)
                    release(("psO", (h * NQC + qc) % 2), "vV", vfin)
                    wait("sync", "vV", flushV())
                    vdc = dma("sync", "dS", av_con2[h][0:128, 512 * qc:512 * qc + 512],
                              avst[:])
                    v_av_con.append(vdc)
                    wait("vector", "dS", vdc)
                if True:
                    wait("gpsimd", "dS", max(v_av_con))
                    wait("gpsimd", "cC", p.val("cC"))
                    p.on("gpsimd", lambda E, h=h: E.collective_compute(
                        "AllGather", ALU.bypass, replica_groups=[list(range(NCORES))],
                        ins=[av_con2[h]], outs=[av_gat2[h]]).then_inc(sems["cC"], 1))
                    p.inc("cC")

            vcc2 = p.val("cC")

            es2.close()
    if PL >= 4:

            xn2 = es.enter_context(nc.sbuf_tensor("xn2", [128, NDC * TOK], BF16))
            es3 = ExitStack()
            sb3 = lambda n, s_, d: es3.enter_context(nc.sbuf_tensor(n, s_, d))
            avo = sb3("avo", [128, NDC * TOK], BF16)
            wob = [sb3(f"wob{i}", [128, NDC * 128], BF16) for i in range(2)]
            ao2 = [sb3(f"ao{i}", [128, NDC * 512], F32) for i in range(2)]
            aobf = sb3("aobf", [128, NDC * 512], BF16)
            cb = sb3("cb", [128, NDC * 512], F32)
            xre = [sb3(f"xre{i}", [128, 512], F32) for i in range(2)]
            sqb3 = sb3("sqb3", [128, 2 * 512], BF16)

            cell = {}
            wait("gpsimd", "cC", vcc1 + 1)
            def _getcid(E):
                cell["cid"] = E.partition_id()
            p.on("gpsimd", _getcid)
            vavo_e = p.inc("dG", 16 * (NDC // 2))
            def _avoload_e(E):
                for j in range(0, NDC, 2):
                    E.dma_start(out=avo[:, j * TOK:(j + 1) * TOK],
                                in_=av_gat2[0][128 * (j // 2):128 * (j // 2) + 128,
                                               bass.ts(cell["cid"], TOK)]
                                ).then_inc(sems["dG"], 16)
            p.on("gpsimd", _avoload_e)
            wait("gpsimd", "cC", vcc1 + 2)
            vavo = p.inc("dG", 16 * (NDC // 2))
            def _avoload_o(E):
                for j in range(1, NDC, 2):
                    E.dma_start(out=avo[:, j * TOK:(j + 1) * TOK],
                                in_=av_gat2[1][128 * (j // 2):128 * (j // 2) + 128,
                                               bass.ts(cell["cid"], TOK)]
                                ).then_inc(sems["dG"], 16)
            p.on("gpsimd", _avoload_o)

            def ln_plain3(xf, xbf, out_writer, sqbuf):
                for a in range(NDC):
                    p.on("tensor", lambda E, a=a: E.matmul(
                        psL[:], onescol[:], xbf[:, a * 512:(a + 1) * 512],
                        start=(a == 0), stop=(a == NDC - 1)))
                v_sum = p.inc("tT")
                p.on("tensor", ("INC_LAST", "tT"))
                wait("scalar", "tT", v_sum)
                va = p.inc("aA")
                p.on("scalar", lambda E: E.activation(
                    rowA[0:1, :], psL[:], AF.Identity,
                    scale=float(-1.0 / D)).then_inc(sems["aA"], 1))
                release("psL", "aA", va)
                wait_free("tensor", "psB")
                wait("tensor", "aA", va)
                vb = p.inc("tT")
                p.on("tensor", lambda E: E.matmul(psB[:], onesrow[:], rowA[0:1, :],
                                                  start=True, stop=True).then_inc(sems["tT"], 1))
                wait("vector", "tT", vb)
                wait_free("tensor", "psO")
                sq_rd = {}
                for a in range(NDC):
                    vv = p.inc("vV")
                    p.on("vector", lambda E, a=a: E.tensor_tensor(
                        cb[:, a * 512:(a + 1) * 512], xf[:, a * 512:(a + 1) * 512],
                        psB[:], ALU.add).then_inc(sems["vV"], 1))
                    wait("scalar", "vV", vv)
                    if a >= 2:
                        wait("scalar", "tT", sq_rd[a % 2])
                    vq = p.inc("aA")
                    p.on("scalar", lambda E, a=a: E.activation(
                        sqbuf[:, 512 * (a % 2):512 * (a % 2) + 512],
                        cb[:, a * 512:(a + 1) * 512], AF.Square).then_inc(sems["aA"], 1))
                    wait("tensor", "aA", vq)
                    vt2 = p.inc("tT")
                    p.on("tensor", lambda E, a=a: E.matmul(
                        psO[0:1, :], onescol[:], sqbuf[:, 512 * (a % 2):512 * (a % 2) + 512],
                        start=(a == 0), stop=(a == NDC - 1)).then_inc(sems["tT"], 1))
                    sq_rd[a % 2] = vt2
                v_cent = p.val("vV")
                release("psB", "vV", v_cent)
                vsq = p.val("tT")
                wait("scalar", "tT", vsq)
                p.on("scalar", lambda E: E.activation(rowB[0:1, :], psO[0:1, :], AF.Ln,
                                                      bias=epsv[0:1, 0:1], scale=float(1.0 / D)))
                vr = p.inc("aA")
                p.on("scalar", lambda E: E.activation(rowA[0:1, :], rowB[0:1, :], AF.Exp,
                                                      scale=-0.5).then_inc(sems["aA"], 1))
                release("psO", "aA", vr)
                wait_free("tensor", "psB")
                wait("tensor", "aA", vr)
                vb2 = p.inc("tT")
                p.on("tensor", lambda E: E.matmul(psB[:], onesrow[:], rowA[0:1, :],
                                                  start=True, stop=True).then_inc(sems["tT"], 1))
                wait("vector", "tT", vb2)
                for a in range(NDC):
                    out_writer(a)
                release("psB", "vV", p.val("vV"))

            for tt in range(NTT):
                for ch in range(NDC):
                    bw = (tt * NDC + ch) % 2
                    st = freed.get(("wob", bw))
                    if st:
                        wait("gpsimd", st[0], st[1])
                    if tt == 0 and ch < 2:
                        wait("gpsimd", "dS", max(v_av_con))
                    vw = dma("gpsimd", "dG", wob[bw][:],
                             wo_d[128 * ch:128 * ch + 128, :])
                    wait("tensor", "dG", max(vw, vavo))
                    bank = ch % 4
                    use_psS(bank)
                    for a in range(NDC):
                        p.on("tensor", lambda E, a=a, bw=bw, tt=tt, bank=bank:
                             E.matmul(
                            psS[bank][:],
                            wob[bw][:, 128 * a:128 * a + 128],
                            avo[:, a * TOK + 512 * tt: a * TOK + 512 * tt + 512],
                            start=(a == 0), stop=(a == NDC - 1)))
                    vmm = p.inc("tT")
                    p.on("tensor", ("INC_LAST", "tT"))
                    freed[("wob", bw)] = ("tT", vmm)
                    wait("scalar", "tT", vmm)
                    vev = p.inc("aA")
                    p.on("scalar", lambda E, ch=ch, bank=bank, tt=tt: E.activation(
                        ao2[tt][:, ch * 512:(ch + 1) * 512], psS[bank][:], AF.Identity,
                        bias=bo_s[:, ch:ch + 1]).then_inc(sems["aA"], 1))
                    bank_state[bank] = ("aA", vev)
                    if tt == 0:
                        v_ev_t0 = vev
                    else:
                        v_ev_t1 = vev
            wait("gpsimd", "aA", v_ev_t0 + 1)
            for ch in range(NDC):
                vg = p.inc("gG")
                p.on("gpsimd", lambda E, ch=ch: E.tensor_copy(
                    aobf[:, ch * 512:(ch + 1) * 512],
                    ao2[0][:, ch * 512:(ch + 1) * 512]).then_inc(sems["gG"], 1))
            if "ao" in taps:
                wait("sync", "aA", p.val("aA"))
                dma("sync", "dS", taps["ao"], ao2[0][:])

            vsp_l = [0]
            def ln_chain(tt, ao):
                wait("tensor", "gG", flushG())

                def w3(a):
                    vv = p.inc("vV")
                    p.on("vector", lambda E, a=a: E.tensor_tensor(
                        ao[:, a * 512:(a + 1) * 512], cb[:, a * 512:(a + 1) * 512],
                        psB[:], ALU.mult).then_inc(sems["vV"], 1))
                ln_plain3(ao, aobf, w3, sqb3)
                for a in range(NDC):
                    bx = a % 2
                    st = freed.get(("xre", bx))
                    if st:
                        wait("sync", st[0], st[1])
                    vx = dma("sync", "dS", xre[bx][:],
                             xt_d[128 * tt:128 * tt + 128, a * 512:(a + 1) * 512])
                    wait("vector", "dS", vx)
                    vv = p.inc("vV")
                    p.on("vector", lambda E, a=a, bx=bx: E.tensor_tensor(
                        ao[:, a * 512:(a + 1) * 512], ao[:, a * 512:(a + 1) * 512],
                        xre[bx][:], ALU.add).then_inc(sems["vV"], 1))
                    freed[("xre", bx)] = ("vV", vv + 1)
                    wait("gpsimd", "vV", vv)
                    vg = p.inc("gG")
                    p.on("gpsimd", lambda E, a=a: E.tensor_copy(
                        aobf[:, a * 512:(a + 1) * 512],
                        ao[:, a * 512:(a + 1) * 512]).then_inc(sems["gG"], 1))
                v_x1 = flushV()
                wait("sync", "vV", v_x1)
                vsp_l[0] = dma("sync", "dS", x1_sp[128 * tt:128 * tt + 128, :], ao[:])
                if "x1" in taps:
                    dma("sync", "dS", taps["x1"][128 * tt:128 * tt + 128, :], ao[:])
                wait("tensor", "gG", flushG())
                def w4(a):
                    vv = p.inc("vV")
                    p.on("vector", lambda E, a=a, tt=tt: E.tensor_tensor(
                        xn2[:, a * TOK + 512 * tt: a * TOK + 512 * tt + 512],
                        cb[:, a * 512:(a + 1) * 512], psB[:], ALU.mult).then_inc(sems["vV"], 1))
                ln_plain3(ao, aobf, w4, sqb3)
                wait("tensor", "vV", p.val("vV"))

            for tt in range(NTT):
                if tt == 1:
                    wait("gpsimd", "tT", p.val("tT"))
                    wait("gpsimd", "aA", v_ev_t1)
                    for ch in range(NDC):
                        vg = p.inc("gG")
                        p.on("gpsimd", lambda E, ch=ch: E.tensor_copy(
                            aobf[:, ch * 512:(ch + 1) * 512],
                            ao2[1][:, ch * 512:(ch + 1) * 512]).then_inc(sems["gG"], 1))
                ln_chain(tt, ao2[tt])
            v_xn2 = p.val("vV")
            if "xn2" in taps:
                wait("sync", "vV", v_xn2)
                dma("sync", "dS", taps["xn2"], xn2[:])
            if "avo" in taps:
                dma("sync", "dS", taps["avo"], avo[:])
            es3.close()
    if PL >= 5:

            es4 = ExitStack()
            sb4 = lambda n, s_, d: es4.enter_context(nc.sbuf_tensor(n, s_, d))
            hbuf = sb4("hbuf", [128, NIC * 512], BF16)
            wgb = [sb4(f"wgb{i}", [128, NDC * 128], BF16) for i in range(2)]
            wub = [sb4(f"wub{i}", [128, NDC * 128], BF16) for i in range(2)]
            wdb = sb4("wdb", [128, 5 * 2048], BF16)
            sgu = sb4("sgu", [128, 4 * 512], F32)
            fo = sb4("fo", [128, NDC * 512], F32)
            sqb4 = sb4("sqb4", [128, 2 * 512], BF16)
            fobf = sb4("fobf", [128, NDC * 512], BF16)
            xr4 = [sb4(f"xr4{i}", [128, 512], F32) for i in range(2)]

            for tt in range(NTT):
                for i in range(NIC):
                    bw = i % 2
                    for (wbs, wdrm) in ((wgb, wgt_d), (wub, wut_d)):
                        st = freed.get(("wb", id(wbs), bw))
                        if st:
                            wait("gpsimd", st[0], st[1])
                        if tt == 0 and i < 2:
                            wait("gpsimd", "vV", v_xn2)
                        vw = dma("gpsimd", "dG", wbs[bw][:],
                                 wdrm[128 * i:128 * i + 128, :])
                        wait("vector", "dG", vw)
                        vg = p.inc("vV")
                        p.on("vector", lambda E, wbs=wbs, bw=bw: E.tensor_scalar(
                            wbs[bw][:], wbs[bw][:], -128.0, None, ALU.add).then_inc(sems["vV"], 1))
                        freed[("wconv", id(wbs), bw)] = vg
                    if P6N < 2:
                        continue
                    bank_g = (2 * i) % 4
                    bank_u = (2 * i + 1) % 4
                    wait("tensor", "vV", freed[("wconv", id(wgb), bw)])
                    wait("tensor", "vV", freed[("wconv", id(wub), bw)])
                    for (bk, wbs) in ((bank_g, wgb), (bank_u, wub)):
                        use_psS(bk)
                        for a in range(NDC):
                            p.on("tensor", lambda E, a=a, bk=bk, wbs=wbs, bw=bw, tt=tt: E.matmul(
                                psS[bk][:], wbs[bw][:, 128 * a:128 * a + 128],
                                xn2[:, a * TOK + 512 * tt: a * TOK + 512 * tt + 512],
                                start=(a == 0), stop=(a == NDC - 1)))
                    vmm = p.inc("tT")
                    p.on("tensor", ("INC_LAST", "tT"))
                    freed[("wb", id(wgb), bw)] = ("tT", vmm)
                    freed[("wb", id(wub), bw)] = ("tT", vmm)
                    sguoff = 1024 * (i % 2)
                    st3 = freed.get(("sgu", i % 2))
                    if st3:
                        wait("scalar", st3[0], st3[1])
                    wait("scalar", "tT", vmm)
                    p.on("scalar", lambda E, bank_g=bank_g, i=i, sguoff=sguoff: E.activation(
                        sgu[:, sguoff:sguoff + 512], psS[bank_g][:], AF.Silu,
                        scale=gsc[:, i:i + 1]))
                    vev = p.inc("aA")
                    p.on("scalar", lambda E, bank_u=bank_u, i=i, sguoff=sguoff: E.activation(
                        sgu[:, sguoff + 512:sguoff + 1024], psS[bank_u][:], AF.Identity,
                        scale=usc[:, i:i + 1]).then_inc(sems["aA"], 1))
                    bank_state[bank_g] = ("aA", vev)
                    bank_state[bank_u] = ("aA", vev)
                    wait("vector", "aA", vev)
                    vh = p.inc("vV")
                    p.on("vector", lambda E, i=i, sguoff=sguoff: E.tensor_tensor(
                        hbuf[:, 512 * i:512 * i + 512], sgu[:, sguoff:sguoff + 512],
                        sgu[:, sguoff + 512:sguoff + 1024], ALU.mult).then_inc(sems["vV"], 1))
                    freed[("sgu", i % 2)] = ("vV", vh + 1)
                    freed[("h", i)] = ("vV", vh)
                if "hb" in taps and tt == 0:
                    wait("sync", "vV", p.val("vV"))
                    dma("sync", "dS", taps["hb"], hbuf[:])

                for hd in (range(NDC) if P6N >= 3 else ()):
                    qdone = []
                    for q in range(4):
                        s5 = (4 * hd + q + 8 * tt * NDC) % 5
                        stq = freed.get(("wdbs", s5))
                        if stq:
                            wait("gpsimd", stq[0], stq[1])
                        vw = dma("gpsimd", "dG", wdb[:, 2048 * s5:2048 * (s5 + 1)],
                                 wdt_d[128 * (4 * hd + q):128 * (4 * hd + q) + 128, :])
                        wait("vector", "dG", vw)
                        vg = p.inc("vV")
                        p.on("vector", lambda E, s5=s5: E.tensor_scalar(
                            wdb[:, 2048 * s5:2048 * (s5 + 1)], wdb[:, 2048 * s5:2048 * (s5 + 1)],
                            -128.0, None, ALU.add).then_inc(sems["vV"], 1))
                        qdone.append(vg)
                    bank = hd % 4
                    use_psS(bank)
                    for a in range(NIC):
                        if a % 16 == 0:
                            wait("tensor", "vV", qdone[a // 16])
                        if a == 0:
                            wait("tensor", "vV", freed[("h", NIC - 1)][1])
                        s5 = (4 * hd + a // 16 + 8 * tt * NDC) % 5
                        p.on("tensor", lambda E, a=a, bank=bank, s5=s5: E.matmul(
                            psS[bank][:], wdb[:, 2048 * s5 + 128 * (a % 16):2048 * s5 + 128 * (a % 16) + 128],
                            hbuf[:, 512 * a:512 * a + 512],
                            start=(a == 0), stop=(a == NIC - 1)))
                    vmm = p.inc("tT")
                    p.on("tensor", ("INC_LAST", "tT"))
                    for q in range(4):
                        s5 = (4 * hd + q + 8 * tt * NDC) % 5
                        freed[("wdbs", s5)] = ("tT", vmm)
                    wait("scalar", "tT", vmm)
                    vev = p.inc("aA")
                    p.on("scalar", lambda E, hd=hd, bank=bank: E.activation(
                        fo[:, hd * 512:(hd + 1) * 512], psS[bank][:], AF.Identity,
                        scale=dsc[:, hd:hd + 1]).then_inc(sems["aA"], 1))
                    bank_state[bank] = ("aA", vev)
                wait("tensor", "gG", p.val("gG"))
                if P6N < 4:
                    continue
                v_dn_all = p.val("tT")
                for hd in range(NDC):
                    wait("gpsimd", "aA", p.val("aA"))
                    wait("gpsimd", "tT", v_dn_all)
                    vg = p.inc("gG")
                    p.on("gpsimd", lambda E, hd=hd: E.tensor_copy(
                        fobf[:, hd * 512:(hd + 1) * 512],
                        fo[:, hd * 512:(hd + 1) * 512]).then_inc(sems["gG"], 1))
                wait("tensor", "gG", flushG())
                if P6N < 5:
                    continue

                def w5(a):
                    vv = p.inc("vV")
                    p.on("vector", lambda E, a=a: E.tensor_tensor(
                        fo[:, a * 512:(a + 1) * 512], fo[:, a * 512:(a + 1) * 512],
                        psB[:], ALU.mult).then_inc(sems["vV"], 1))

                def ln_plain5(xf, xbf, out_writer, sqbuf, cbl):
                    for a in range(NDC):
                        p.on("tensor", lambda E, a=a: E.matmul(
                            psL[:], onescol[:], xbf[:, a * 512:(a + 1) * 512],
                            start=(a == 0), stop=(a == NDC - 1)))
                    v_sum = p.inc("tT")
                    p.on("tensor", ("INC_LAST", "tT"))
                    wait("scalar", "tT", v_sum)
                    va = p.inc("aA")
                    p.on("scalar", lambda E: E.activation(
                        rowA[0:1, :], psL[:], AF.Identity,
                        scale=float(-1.0 / D)).then_inc(sems["aA"], 1))
                    release("psL", "aA", va)
                    wait_free("tensor", "psB")
                    wait("tensor", "aA", va)
                    vb = p.inc("tT")
                    p.on("tensor", lambda E: E.matmul(psB[:], onesrow[:], rowA[0:1, :],
                                                      start=True, stop=True).then_inc(sems["tT"], 1))
                    wait("vector", "tT", vb)
                    wait_free("tensor", "psO")
                    sq_rd = {}
                    for a in range(NDC):
                        vv = p.inc("vV")
                        p.on("vector", lambda E, a=a: E.tensor_tensor(
                            cbl[:, a * 512:(a + 1) * 512], xf[:, a * 512:(a + 1) * 512],
                            psB[:], ALU.add).then_inc(sems["vV"], 1))
                        wait("scalar", "vV", vv)
                        if a >= 2:
                            wait("scalar", "tT", sq_rd[a % 2])
                        vq = p.inc("aA")
                        p.on("scalar", lambda E, a=a: E.activation(
                            sqbuf[:, 512 * (a % 2):512 * (a % 2) + 512],
                            cbl[:, a * 512:(a + 1) * 512], AF.Square).then_inc(sems["aA"], 1))
                        wait("tensor", "aA", vq)
                        vt2 = p.inc("tT")
                        p.on("tensor", lambda E, a=a: E.matmul(
                            psO[0:1, :], onescol[:], sqbuf[:, 512 * (a % 2):512 * (a % 2) + 512],
                            start=(a == 0), stop=(a == NDC - 1)).then_inc(sems["tT"], 1))
                        sq_rd[a % 2] = vt2
                    v_cent = p.val("vV")
                    release("psB", "vV", v_cent)
                    vsq = p.val("tT")
                    wait("scalar", "tT", vsq)
                    p.on("scalar", lambda E: E.activation(rowB[0:1, :], psO[0:1, :], AF.Ln,
                                                          bias=epsv[0:1, 0:1], scale=float(1.0 / D)))
                    vr = p.inc("aA")
                    p.on("scalar", lambda E: E.activation(rowA[0:1, :], rowB[0:1, :], AF.Exp,
                                                          scale=-0.5).then_inc(sems["aA"], 1))
                    release("psO", "aA", vr)
                    wait_free("tensor", "psB")
                    wait("tensor", "aA", vr)
                    vb2 = p.inc("tT")
                    p.on("tensor", lambda E: E.matmul(psB[:], onesrow[:], rowA[0:1, :],
                                                      start=True, stop=True).then_inc(sems["tT"], 1))
                    wait("vector", "tT", vb2)
                    for a in range(NDC):
                        out_writer(a)
                    release("psB", "vV", p.val("vV"))

                ln_plain5(fo, fobf, w5, sqb4, fo)
                v_y5 = p.val("vV")
                if P6N < 6:
                    continue
                if P6N >= 7:
                    for a in range(NDC):
                        bx = a % 2
                        st = freed.get(("xr4", bx))
                        if st:
                            wait("sync", st[0], st[1])
                        if a < 2:
                            wait("sync", "dS", vsp_l[0])
                        vx1 = dma("sync", "dS", xr4[bx][:],
                                  x1_sp[128 * tt:128 * tt + 128, a * 512:(a + 1) * 512])
                        wait("vector", "dS", vx1)
                        vv1 = p.inc("vV")
                        p.on("vector", lambda E, a=a, bx=bx: E.tensor_tensor(
                            fo[:, a * 512:(a + 1) * 512], fo[:, a * 512:(a + 1) * 512],
                            xr4[bx][:], ALU.add).then_inc(sems["vV"], 1))
                        freed[("xr4", bx)] = ("vV", vv1 + 1)
                wait("sync", "vV", flushV())
                vst = dma("sync", "dS", out_d[128 * tt:128 * tt + 128, :], fo[:])
                freed[("fo",)] = ("dS", vst)
                if tt + 1 < NTT:
                    wait("scalar", "dS", vst)

            es4.close()

    wait("sync", "tT", p.val("tT"))
    wait("sync", "aA", p.val("aA"))
    wait("sync", "vV", p.val("vV"))
    wait("sync", "gG", p.val("gG"))
    wait("sync", "dS", p.val("dS"))
    wait("sync", "dG", p.val("dG"))

    global _LAST_PROG, _LAST_SEMS
    _LAST_PROG, _LAST_SEMS = p, sems
    with nc.Block() as block:
        for e in ENGINES:
            def runner(E, e=e):
                last = None
                for f in p.ops[e]:
                    if isinstance(f, tuple) and f[0] == "INC_LAST":
                        assert last is not None
                        last.then_inc(sems[f[1]], 1)
                        continue
                    r = f(E)
                    if r is not None:
                        last = r
            getattr(block, e)(runner)
    if PL >= 5:
        es.close()
    return nc


def _prep_inputs(inp):
    X = np.asarray(inp["L0_attn_X"])
    Wq = np.asarray(inp["L0_attn_Wq"]); bq = np.asarray(inp["L0_attn_bq"])
    Wk = np.asarray(inp["L0_attn_Wk"]); bk = np.asarray(inp["L0_attn_bk"])
    Wv = np.asarray(inp["L0_attn_Wv"]); bv = np.asarray(inp["L0_attn_bv"])
    Wo = np.asarray(inp["L0_attn_Wo"]); bo = np.asarray(inp["L0_attn_bo"])
    lng = np.asarray(inp["L0_attn_ln_g"]); lnb = np.asarray(inp["L0_attn_ln_b"])
    gw = np.asarray(inp["L0_ffn_gate_w"]); gs = np.asarray(inp["L0_ffn_gate_scale"])
    uw = np.asarray(inp["L0_ffn_up_w"]); us = np.asarray(inp["L0_ffn_up_scale"])
    dw = np.asarray(inp["L0_ffn_down_w"]); ds = np.asarray(inp["L0_ffn_down_scale"])

    def u8_of(a):
        return np.ascontiguousarray(a.view(np.uint8).reshape(a.shape + (4,))[..., 0])

    gw8 = u8_of(gw).reshape(NIC, 128, NDC, 128)
    wgt = np.ascontiguousarray(gw8.transpose(0, 3, 2, 1)).reshape(NIC * 128, NDC * 128)
    uw8 = u8_of(uw).reshape(NIC, 128, NDC, 128)
    wut = np.ascontiguousarray(uw8.transpose(0, 3, 2, 1)).reshape(NIC * 128, NDC * 128)
    dw8 = u8_of(dw).reshape(NDC, 128, 4, 16, 128)
    wdt = np.ascontiguousarray(dw8.transpose(0, 2, 4, 3, 1)).reshape(NDC * 4 * 128, 16 * 128)

    bf = ml_dtypes.bfloat16
    cm = np.zeros((128, 4 * 512), np.float32)
    kk = np.arange(128)[:, None]; qq = np.arange(512)[None, :]
    for r in range(4):
        cm[:, 512 * r:512 * r + 512] = (128 * r + kk <= qq)
    cmask = cm.astype(bf)
    onescol = np.ones((128, 1), np.float32).astype(bf)
    onesrow = np.ones((1, 128), np.float32)
    epsv = np.array([[1e-5, 1e-30]], np.float32)
    bo_l = np.ascontiguousarray(bo.reshape(NDC, 128).T)
    lng_l = np.ascontiguousarray(lng.reshape(NDC, 128).T)
    lnb_l = np.ascontiguousarray(lnb.reshape(NDC, 128).T)
    gs_l = np.ascontiguousarray(gs.reshape(NIC, 128).T)
    us_l = np.ascontiguousarray(us.reshape(NIC, 128).T)
    ds_l = np.ascontiguousarray(ds.reshape(NDC, 128).T)

    wo_t = np.ascontiguousarray(
        Wo.reshape(NDC, 128, NDC, 128).transpose(2, 1, 0, 3)).reshape(NDC * 128, NDC * 128)

    def qkv_tile(w, hsl):
        return np.ascontiguousarray(
            w[:, hsl].reshape(NDC, 128, 256).transpose(1, 0, 2)).reshape(128, NDC * 256)

    in_maps = []
    for c in range(NCORES):
        hsl = slice(256 * c, 256 * c + 256)
        bqk = np.zeros((128, 4), np.float32)
        bqk[:, 0:2] = bq[hsl].reshape(2, 128).T
        bqk[:, 2:4] = bk[hsl].reshape(2, 128).T
        Xc = X[0, TOK * c:TOK * (c + 1), :]
        xt_t = np.ascontiguousarray(
            Xc.reshape(NTT, 512, NDC, 128).transpose(0, 3, 2, 1)).reshape(NTT * 128, NDC * 512)
        in_maps.append(dict(
            xt=xt_t,
            wq=qkv_tile(Wq, hsl),
            wk=qkv_tile(Wk, hsl),
            wv=qkv_tile(Wv, hsl),
            wo=wo_t, bqk=bqk, bv=np.ascontiguousarray(bv[hsl])[None, :],
            bo=bo_l, lng=lng_l, lnb=lnb_l,
            wgt=wgt, wut=wut, wdt=wdt, gsc=gs_l, usc=us_l, dsc=ds_l,
            cmaskb=cmask, onescol=onescol, onesrow=onesrow, epsv=epsv,
        ))
    return in_maps


def kernel(**inp):
    in_maps = _prep_inputs(inp)
    nc = build_nc()
    res = run_bass_kernel_spmd(nc, in_maps, core_ids=list(range(NCORES)))
    cores = []
    for c in range(NCORES):
        o = np.asarray(res.results[c]["out"]).reshape(NTT, 128, NDC, 512)
        cores.append(o.transpose(0, 3, 2, 1).reshape(TOK, D))
    ret = np.ascontiguousarray(np.concatenate(cores, axis=0))[None]
    if _DEBUG_TAPS:
        kernel.debug_results = res.results
    return ret
